# version 16
# baseline (speedup 1.0000x reference)
import contextlib
import numpy as np

import concourse.bass as bass
import concourse.mybir as mybir
from concourse.bass_utils import run_bass_kernel_spmd

F32 = mybir.dt.float32
BF16 = mybir.dt.bfloat16
AF = mybir.ActivationFunctionType
ALU = mybir.AluOpType
AX = mybir.AxisListType

LSEQ = 2048
DM = 2048
PROJ = 5136
CONVD = 3072
NH = 16
QCH = 256
NCH = LSEQ // QCH
KC = 16
TGRP = 512
NGRP = LSEQ // TGRP
EPS = 1e-5
NEGBIG = -1.0e30

ENG_SEM = {"tensor": "tensor", "vector": "vector", "scalar": "scalar_c",
           "gpsimd": "gpsimd_c"}
DMA_SEM = {"sync": "d_sync", "gpsimd": "d_gpsimd", "scalar": "d_scalar"}

CHAIN_OPS = {"tensor_scalar", "tensor_single_scalar", "tensor_tensor", "tensor_copy",
             "tensor_add", "tensor_sub", "tensor_mul", "tensor_max", "tensor_relu",
             "scalar_tensor_tensor", "tensor_tensor_scan", "tensor_reduce",
             "reduce_sum", "reduce_max", "reciprocal", "memset", "memzero", "iota",
             "activation", "copy", "select", "copy_predicated", "add", "mul", "sign",
             "sqrt", "square", "bn_stats", "bn_aggr", "stream_shuffle"}


class RInst:
    __slots__ = ("entry",)
    def __init__(self, entry):
        self.entry = entry

    def then_inc(self, sem, n):
        self.entry[3].append((sem, n))
        return self


class RQ:
    def __init__(self, S, eng):
        self._S = S
        self._eng = eng

    def __getattr__(self, name):
        S, eng = self._S, self._eng
        def rec(*a, **k):
            entry = [name, a, k, []]
            S.pending[eng].append(entry)
            return RInst(entry)
        return rec


class QW:
    def __init__(self, S, eng, q):
        self._S = S
        self._eng = eng
        self._q = q

    def __getattr__(self, name):
        f = getattr(self._q, name)
        if name not in CHAIN_OPS:
            return f
        S, eng, q = self._S, self._eng, self._q
        sem = ENG_SEM[eng]
        def wrap(*a, **k):
            S.wait(q, eng, (sem, S.cnt[sem]))
            inst = f(*a, **k)
            S.cnt[sem] += 1
            inst.then_inc(S.sems[sem], 1)
            return inst
        return wrap

DEBUG_OUTS = []
BUILD_STAGES = ["s0", "s1", "s2a", "s2b", "s2g", "s3", "s4", "s56"]


class Sched:
    def __init__(self, nc, es):
        self.nc = nc
        self.block = es.enter_context(nc.Block())
        self.sems = {}
        self.cnt = {}
        self.waited = {}
        self.bar_expected = 0
        self.bar_snapshot = {}
        self.stage_last = {}
        self.stage_dma = {}
        self.stage_first_wait_done = set()
        self.pending = {e: [] for e in ("tensor", "vector", "scalar", "gpsimd", "sync")}
        names = ["tensor", "vector", "scalar_c", "gpsimd_c"]
        for qn in ("sync", "gpsimd", "scalar"):
            for k in range(8):
                names.append(f"d_{qn}_{k}")
        for name in names:
            self.sems[name] = es.enter_context(nc.semaphore(name))
            self.cnt[name] = 0
        self.dma_rot = {"sync": 0, "gpsimd": 0, "scalar": 0}

    def sec(self, eng, body):
        q = RQ(self, eng)
        if eng not in self.stage_first_wait_done:
            self.stage_first_wait_done.add(eng)
            for sname, v in self.bar_snapshot.items():
                if v > 0 and self.waited.get((eng, sname), 0) < v:
                    q.wait_ge(self.sems[sname], v)
                    self.waited[(eng, sname)] = v
        qq = QW(self, eng, q) if eng in ("vector", "scalar", "gpsimd") else q
        return body(qq)

    def flush(self):
        for eng, entries in self.pending.items():
            if not entries:
                continue
            def fn(q, entries=tuple(entries)):
                for name, a, k, incs in entries:
                    inst = getattr(q, name)(*a, **k)
                    for sem, n in incs:
                        inst.then_inc(sem, n)
            getattr(self.block, eng)(fn)
            self.pending[eng] = []

    def wait(self, q, eng, token):
        if token is None:
            return
        if isinstance(token, (list, tuple)) and token and isinstance(token[0], (list, tuple)):
            for tk in token:
                self.wait(q, eng, tk)
            return
        sem, val = token
        if val <= 0:
            return
        k = (eng, sem)
        if self.waited.get(k, 0) >= val:
            return
        if isinstance(q, QW):
            q = q._q
        q.wait_ge(self.sems[sem], val)
        self.waited[k] = val

    dsc = None
    _dsc_init = False

    def _dsc_ensure(self, q):
        if not self._dsc_init:
            self._dsc_init = True
            q.memzero(self.dsc[0:1, 0:1])
            q.activation(out=self.dsc[0:1, 1:2], in_=self.dsc[0:1, 0:1], func=AF.Exp)

    def dum_pair(self, q):
        self._dsc_ensure(q)
        q.activation(out=self.dsc[0:1, 1:2], in_=self.dsc[0:1, 0:1], func=AF.Exp)
        q.activation(out=self.dsc[0:1, 0:1], in_=self.dsc[0:1, 1:2], func=AF.Ln)

    def dum_ln(self, q):
        self._dsc_ensure(q)
        q.activation(out=self.dsc[0:1, 0:1], in_=self.dsc[0:1, 1:2], func=AF.Ln)

    def dum_silu(self, q):
        self._dsc_ensure(q)
        q.activation(out=self.dsc[0:1, 2:3], in_=self.dsc[0:1, 0:1], func=AF.Silu)

    def tok(self, sem):
        return (sem, self.cnt[sem])

    def done(self, eng, inst):
        sem = ENG_SEM[eng]
        if eng == "tensor":
            self.cnt[sem] += 1
            inst.then_inc(self.sems[sem], 1)
        return (sem, self.cnt[sem])

    def note(self, eng, inst):
        self.stage_last[eng] = inst
        return inst

    def dma(self, q, eng, out, in_, transpose=False):
        if isinstance(q, QW):
            q = q._q
        k = self.dma_rot[eng]
        self.dma_rot[eng] = (k + 1) % 8
        sem = f"d_{eng}_{k}"
        self.wait(q, eng, (sem, self.cnt[sem]))
        if transpose:
            inst = q.dma_start_transpose(out=out, in_=in_)
        else:
            inst = q.dma_start(out=out, in_=in_)
        self.cnt[sem] += 16
        inst.then_inc(self.sems[sem], 16)
        return (sem, self.cnt[sem])

    def stage_end(self):
        self.flush()
        self.stage_first_wait_done = set()
        self.bar_snapshot = dict(self.cnt)


class Ctx:
    pass


def build_program(debug_outs=(), stages=None):
    stages = list(stages if stages is not None else BUILD_STAGES)
    nc = bass.Bass("TRN2", target_bir_lowering=False)
    C = Ctx()
    C.nc = nc
    d = {}
    d["x"] = nc.dram_tensor("x", [LSEQ, DM], F32, kind="ExternalInput").ap()
    d["w_in"] = nc.dram_tensor("w_in", [DM, PROJ], F32, kind="ExternalInput").ap()
    d["conv_w"] = nc.dram_tensor("conv_w", [CONVD, 4], F32, kind="ExternalInput").ap()
    d["conv_b"] = nc.dram_tensor("conv_b", [CONVD], F32, kind="ExternalInput").ap()
    d["a_log"] = nc.dram_tensor("a_log", [NH], F32, kind="ExternalInput").ap()
    d["d_vec"] = nc.dram_tensor("d_vec", [NH], F32, kind="ExternalInput").ap()
    d["dt_bias"] = nc.dram_tensor("dt_bias", [NH], F32, kind="ExternalInput").ap()
    d["normw"] = nc.dram_tensor("normw", [DM], F32, kind="ExternalInput").ap()
    d["w_out"] = nc.dram_tensor("w_out", [DM, DM], F32, kind="ExternalInput").ap()
    d["w_ffn"] = nc.dram_tensor("w_ffn", [DM, DM], F32, kind="ExternalInput").ap()
    d["out"] = nc.dram_tensor("out", [LSEQ, DM], F32, kind="ExternalOutput").ap()
    def scratch(name, shape, dt_):
        kind = "ExternalOutput" if name in debug_outs else "Internal"
        nm = ("dbg_" + name) if name in debug_outs else name
        return nc.dram_tensor(nm, shape, dt_, kind=kind).ap()
    d["LNF"] = scratch("LNF", [KC, 128, LSEQ], BF16)
    d["XBC"] = scratch("XBC", [24, 128, LSEQ], BF16)
    d["SG"] = scratch("SG", [KC, 128, LSEQ], BF16)
    d["GS"] = scratch("GS", [KC, 128, LSEQ], BF16)
    d["X1"] = scratch("X1", [LSEQ, DM], F32)
    d["WQT"] = scratch("WQT", [KC, 128, KC, 128], BF16)
    d["RSTDD"] = scratch("RSTDD", [2, 512], F32)
    C.d = d

    with contextlib.ExitStack() as es:
        S = Sched(nc, es)
        C.S = S
        sb = {}
        C.sb = sb
        ps = {}
        C.ps = ps
        def sbuf(name, shape, dt_):
            sb[name] = es.enter_context(nc.sbuf_tensor(name, shape, dt_))
            return sb[name]
        sbuf("I_f32", [128, 128], F32)
        sbuf("I_bf", [128, 128], BF16)
        sbuf("onesrow", [1, 128], F32)
        sbuf("onescol", [128, 1], F32)
        sbuf("ones16", [16, 256], F32)
        sbuf("D_bc", [128, 16], F32)
        sbuf("normw_fm", [128, 16], F32)
        sbuf("convw_col", [128, 96], F32)
        sbuf("convb_col", [128, 24], F32)
        sbuf("acol", [16, 1], F32)
        sbuf("dtb_col", [16, 1], F32)
        sbuf("drow", [1, 16], F32)
        sbuf("smalls", [128, 48], F32)
        sbuf("cbias", [128, 2], F32)
        sbuf("row24", [24, 128], F32)
        sbuf("zb", [128, 72], BF16)
        sbuf("pcol", [128, 1], F32)
        sbuf("irow", [128, 512], F32)
        sbuf("dsc", [1, 3], F32)
        S.dsc = sb["dsc"]
        for i in range(7):
            ps[f"p{i}"] = es.enter_context(nc.psum_tensor(f"p{i}", [128, 512], F32))
        ps["pt"] = es.enter_context(nc.psum_tensor("pt", [128, 1024], BF16))

        if "s0" in stages:
            stage0(C)
            S.stage_end()
        if "s1" in stages:
            stage1(C)
            S.stage_end()
        with contextlib.ExitStack() as es23:
            sb23 = {}
            C.sb23 = sb23
            def sbuf23(name, shape, dt_):
                sb23[name] = es23.enter_context(nc.sbuf_tensor(name, shape, dt_))
                return sb23[name]
            if any(s in stages for s in ("s2a", "s2b", "s2g", "s3")):
                sbuf23("IND", [16, KC * 128], F32)
                sbuf23("maskneg", [128, 512], F32)
                sbuf23("cum_all", [16, LSEQ], F32)
                sbuf23("expcum_all", [16, LSEQ], F32)
                sbuf23("colT", [128, NCH * 2 * 3 * 16], F32)
                sbuf23("S_a", [128, NH * 128], F32)
                sbuf23("S_b", [128, NH * 128], F32)
                sbuf23("S_bf", [128, NH * 128], BF16)
                stage23_consts(C)
            if "s2a" in stages:
                stage2(C, 0)
                S.stage_end()
            if "s2b" in stages:
                stage2(C, 1)
                S.stage_end()
            if "s2g" in stages:
                stage2g(C)
                S.stage_end()
            if "s3" in stages:
                stage3(C)
                S.stage_end()
        if "s4" in stages:
            stage4(C)
            S.stage_end()
        if "s56" in stages:
            stage56(C)
            S.stage_end()
        def drain(q):
            for sname, v in S.cnt.items():
                if sname.startswith("d_") and v > 0 and S.waited.get(("sync", sname), 0) < v:
                    q.wait_ge(S.sems[sname], v)
                    S.waited[("sync", sname)] = v
        S.sec("sync", drain)
        S.flush()
    return nc


def stage0(C):
    nc, S, sb, ps, d = C.nc, C.S, C.sb, C.ps, C.d

    def ld(q):
        t = {}
        t["convw"] = S.dma(q, "sync", sb["convw_col"].ap().rearrange("p (cc k) -> p cc k", k=4),
                           d["conv_w"].rearrange("(cc p) k -> p cc k", p=128))
        t["cb"] = S.dma(q, "sync", sb["row24"][:, :],
                        d["conv_b"].rearrange("(cc p) -> cc p", p=128))
        t["al"] = S.dma(q, "sync", sb["acol"][:, :], d["a_log"].rearrange("(a o) -> a o", o=1))
        t["db"] = S.dma(q, "sync", sb["dtb_col"][:, :], d["dt_bias"].rearrange("(a o) -> a o", o=1))
        t["dr"] = S.dma(q, "sync", sb["drow"][:, :], d["d_vec"].rearrange("(o a) -> o a", o=1))
        return t
    t = S.sec("sync", ld)

    def io(q):
        i1 = q.iota(sb["irow"][:, :], pattern=[[1, 512]], base=0, channel_multiplier=0,
                    allow_small_or_imprecise_dtypes=True)
        t1 = S.done("gpsimd", i1)
        i2 = q.iota(sb["pcol"][:, :], pattern=[[0, 1]], base=0, channel_multiplier=1,
                    allow_small_or_imprecise_dtypes=True)
        return t1, S.done("gpsimd", i2)
    t_irow, t_pcol = S.sec("gpsimd", io)

    def consts(q):
        S.wait(q, "vector", t_irow)
        S.wait(q, "vector", t_pcol)
        inst = q.tensor_scalar(out=sb["I_f32"][:, :], in0=sb["irow"][:, 0:128],
                               scalar1=sb["pcol"][:, 0:1], scalar2=None, op0=ALU.is_equal)
        tI = S.done("vector", inst)
        inst = q.tensor_copy(out=sb["I_bf"][:, :], in_=sb["I_f32"][:, :])
        tIb = S.done("vector", inst)
        q.memset(sb["onesrow"][:, :], 1.0)
        q.memset(sb["onescol"][:, :], 1.0)
        q.memset(sb["ones16"][:, :], 1.0)
        q.memset(sb["cbias"][:, 0:1], EPS)
        q.memset(sb["cbias"][:, 1:2], 1.0)
        inst = q.memset(sb["zb"][:, :], 0.0)
        return tI, tIb, S.done("vector", inst)
    t_I, t_Ibf, t_ms = S.sec("vector", consts)
    C.t_Ibf = t_Ibf

    def tr2(q):
        S.wait(q, "tensor", t["cb"])
        S.wait(q, "tensor", t_I)
        inst = q.transpose(out=ps["p0"][0:128, 0:24], in_=sb["row24"][0:24, 0:128],
                           identity=sb["I_f32"][0:24, 0:24])
        return S.done("tensor", inst)
    t_cbT = S.sec("tensor", tr2)
    def ev2(q):
        S.wait(q, "vector", t_cbT)
        inst = q.tensor_copy(out=sb["convb_col"][:, :], in_=ps["p0"][0:128, 0:24])
        return S.done("vector", inst)
    t_cbfm = S.sec("vector", ev2)

    def ldnw(q):
        S.wait(q, "sync", t_cbT)
        return S.dma(q, "sync", sb["row24"][0:16, :],
                     d["normw"].rearrange("(a b) -> a b", b=128))
    t_nw = S.sec("sync", ldnw)
    def tr1(q):
        S.wait(q, "tensor", t_nw)
        S.wait(q, "tensor", t_cbfm)
        inst = q.transpose(out=ps["p0"][0:128, 0:16], in_=sb["row24"][0:16, 0:128],
                           identity=sb["I_f32"][0:16, 0:16])
        return S.done("tensor", inst)
    t_nwT = S.sec("tensor", tr1)
    def ev1(q):
        S.wait(q, "vector", t_nwT)
        inst = q.tensor_copy(out=sb["normw_fm"][:, :], in_=ps["p0"][0:128, 0:16])
        return S.done("vector", inst)
    t_nwfm = S.sec("vector", ev1)

    def mmD(q):
        S.wait(q, "tensor", t["dr"])
        S.wait(q, "tensor", t_ms)
        S.wait(q, "tensor", t_nwfm)
        inst = q.matmul(out=ps["p0"][0:128, 0:16], lhsT=sb["onesrow"][:, :],
                        rhs=sb["drow"][:, :], start=True, stop=True)
        return S.done("tensor", inst)
    t_dbc = S.sec("tensor", mmD)
    def ev3(q):
        S.wait(q, "vector", t_dbc)
        inst = q.tensor_copy(out=sb["D_bc"][:, :], in_=ps["p0"][0:128, 0:16])
        return S.done("vector", inst)
    t_dbcv = S.sec("vector", ev3)

    def sA(q):
        S.wait(q, "scalar", t["al"])
        inst = q.activation(out=sb["smalls"][0:16, 40:41], in_=sb["acol"][:, :], func=AF.Exp)
        return S.done("scalar", inst)
    t_ea = S.sec("scalar", sA)
    def vA(q):
        S.wait(q, "vector", t_ea)
        inst = q.tensor_scalar(out=sb["acol"][:, :], in0=sb["smalls"][0:16, 40:41],
                               scalar1=-1.0, scalar2=None, op0=ALU.mult)
        return S.done("vector", inst)
    S.sec("vector", vA)

    with contextlib.ExitStack() as es0:
        wld = [es0.enter_context(nc.sbuf_tensor(f"s0_wld{i}", [128, 2048], F32)) for i in range(2)]
        wjunk = es0.enter_context(nc.sbuf_tensor("s0_junk", [128, 2048], F32))
        wn = es0.enter_context(nc.sbuf_tensor("s0_wn", [128, 2048], F32))
        q2t = es0.enter_context(nc.sbuf_tensor("s0_q2", [128, 2048], F32))
        wqb = es0.enter_context(nc.sbuf_tensor("s0_wqb", [128, 2048], BF16))
        tq = es0.enter_context(nc.sbuf_tensor("s0_tq", [128, 16, 128], BF16))
        rs = es0.enter_context(nc.sbuf_tensor("s0_rs", [128, 16], F32))

        tload = [None, None]
        tuse = [None, None]
        for sl in range(16):
            b = sl % 2
            def ldw(q, b=b, sl=sl):
                S.wait(q, "sync", tuse[b])
                return S.dma(q, "sync", wld[b][:, :], d["w_ffn"][sl * 128:(sl + 1) * 128, :])
            tload[b] = S.sec("sync", ldw)
            def absum(q, b=b, sl=sl):
                S.wait(q, "scalar", tload[b])
                inst = q.activation(out=wjunk[:, :], in_=wld[b][:, :], func=AF.Abs,
                                    accum_out=rs[:, sl:sl + 1])
                return S.done("scalar", inst)
            tuse[b] = S.sec("scalar", absum)
        def vtot(q):
            S.wait(q, "vector", tuse[0])
            S.wait(q, "vector", tuse[1])
            inst = q.tensor_reduce(out=sb["smalls"][:, 32:33], in_=rs[:, :], axis=AX.X, op=ALU.add)
            return S.done("vector", inst)
        t_rtot = S.sec("vector", vtot)
        def mtot(q):
            S.wait(q, "tensor", t_rtot)
            S.wait(q, "tensor", t_dbcv)
            inst = q.matmul(out=ps["p1"][0:1, 0:1], lhsT=sb["smalls"][:, 32:33],
                            rhs=sb["onescol"][:, :], start=True, stop=True)
            return S.done("tensor", inst)
        t_tot = S.sec("tensor", mtot)
        def vsc(q):
            S.wait(q, "vector", t_tot)
            q.tensor_scalar(out=sb["smalls"][0:1, 33:34], in0=ps["p1"][0:1, 0:1],
                            scalar1=1.0 / (DM * DM), scalar2=1e-6, op0=ALU.mult, op1=ALU.max)
            inst = q.reciprocal(out=sb["smalls"][0:1, 34:35], in_=sb["smalls"][0:1, 33:34])
            return S.done("vector", inst)
        t_inv = S.sec("vector", vsc)
        def mbc(q):
            S.wait(q, "tensor", t_inv)
            q.matmul(out=ps["p1"][0:128, 2:3], lhsT=sb["onesrow"][:, :],
                     rhs=sb["smalls"][0:1, 33:34], start=True, stop=True)
            inst = q.matmul(out=ps["p1"][0:128, 3:4], lhsT=sb["onesrow"][:, :],
                            rhs=sb["smalls"][0:1, 34:35], start=True, stop=True)
            return S.done("tensor", inst)
        t_bc2 = S.sec("tensor", mbc)
        def vbc(q):
            S.wait(q, "vector", t_bc2)
            q.tensor_copy(out=sb["smalls"][:, 35:36], in_=ps["p1"][0:128, 2:3])
            inst = q.tensor_copy(out=sb["smalls"][:, 36:37], in_=ps["p1"][0:128, 3:4])
            return S.done("vector", inst)
        t_inv2 = S.sec("vector", vbc)

        tload = [None, None]
        t_tq = None
        t_st = None
        t_q2_prev = [None]
        for sl in range(16):
            b = sl % 2
            def ldw(q, b=b, sl=sl):
                S.wait(q, "sync", tuse[b])
                return S.dma(q, "sync", wld[b][:, :], d["w_ffn"][sl * 128:(sl + 1) * 128, :])
            tload[b] = S.sec("sync", ldw)
            def vwn(q, b=b, t_q2p=None):
                S.wait(q, "vector", tload[b])
                S.wait(q, "vector", t_inv2)
                S.wait(q, "vector", t_tq)
                S.wait(q, "vector", t_q2_prev[0])
                inst = q.tensor_scalar(out=wn[:, :], in0=wld[b][:, :],
                                       scalar1=sb["smalls"][:, 36:37], scalar2=None, op0=ALU.mult)
                return S.done("vector", inst)
            tuse[b] = S.sec("vector", vwn)
            def gq2(q, b=b):
                inst = q.tensor_scalar(out=q2t[:, :], in0=wn[:, :], scalar1=-0.5,
                                       scalar2=None, op0=ALU.is_le)
                return S.done("vector", inst)
            t_q2 = S.sec("vector", gq2)
            t_q2_prev[0] = t_q2
            def vwq(q, t_q2=t_q2):
                S.wait(q, "vector", t_q2)
                inst = q.scalar_tensor_tensor(out=wqb[:, :], in0=wn[:, :], scalar=0.5,
                                              in1=q2t[:, :], op0=ALU.is_ge, op1=ALU.subtract)
                return S.done("vector", inst)
            t_wq = S.sec("vector", vwq)
            def str_(q, t_wq=t_wq):
                S.wait(q, "scalar", t_wq)
                S.wait(q, "scalar", t_st)
                return S.dma(q, "scalar", tq[:, :, :], wqb[:, :], transpose=True)
            t_tq = S.sec("scalar", str_)
            def sto(q, t_tq=t_tq, sl=sl):
                S.wait(q, "sync", t_tq)
                return S.dma(q, "sync",
                             d["WQT"][:, :, sl, :].rearrange("ic i o -> i ic o"),
                             tq[:, :, :])
            t_st = S.sec("sync", sto)
        S.flush()


def stage1(C):
    nc, S, sb, ps, d = C.nc, C.S, C.sb, C.ps, C.d
    with contextlib.ExitStack() as es1:
        xt = [es1.enter_context(nc.sbuf_tensor(f"s1_x{i}", [128, 2048], F32)) for i in range(2)]
        bns = es1.enter_context(nc.sbuf_tensor("s1_bns", [128, 4, 6], F32))
        lns = [es1.enter_context(nc.sbuf_tensor(f"s1_l{i}", [128, 2048], BF16)) for i in range(2)]
        lnt = [es1.enter_context(nc.sbuf_tensor(f"s1_t{i}", [128, 16, 128], BF16)) for i in range(2)]
        t_xuse = [None, None]
        t_lnuse = [None, None]
        t_tuse = [None, None]
        for tt in range(16):
            b = tt % 2
            c0 = 8 * b
            def ldx(q, b=b, tt=tt):
                S.wait(q, "sync", t_xuse[b])
                return S.dma(q, "sync", xt[b][:, :], d["x"][tt * 128:(tt + 1) * 128, :])
            t_x = S.sec("sync", ldx)
            def vstats(q, b=b, c0=c0, t_x=t_x):
                S.wait(q, "vector", t_x)
                for j in range(4):
                    q.bn_stats(out=bns[:, j, :], in_=xt[b][:, j * 512:(j + 1) * 512])
                inst = q.bn_aggr(out=sb["smalls"][:, c0 + 2:c0 + 4],
                                 in_=bns.ap().rearrange("p a b -> p (a b)"))
                return S.done("vector", inst)
            t_var = S.sec("vector", vstats)
            def srstd(q, c0=c0, t_var=t_var):
                S.wait(q, "scalar", t_var)
                q.activation(out=sb["smalls"][:, c0 + 5:c0 + 6],
                             in_=sb["smalls"][:, c0 + 3:c0 + 4], func=AF.Ln,
                             bias=sb["cbias"][:, 0:1])
                inst = q.activation(out=sb["smalls"][:, c0 + 6:c0 + 7],
                                    in_=sb["smalls"][:, c0 + 5:c0 + 6], func=AF.Exp, scale=-0.5)
                return S.done("scalar", inst)
            t_rstd = S.sec("scalar", srstd)
            def vapply(q, b=b, c0=c0, t_rstd=t_rstd):
                S.wait(q, "vector", t_rstd)
                S.wait(q, "vector", t_lnuse[b])
                inst = q.tensor_scalar(out=lns[b][:, :], in0=xt[b][:, :],
                                       scalar1=sb["smalls"][:, c0 + 2:c0 + 3],
                                       scalar2=sb["smalls"][:, c0 + 6:c0 + 7],
                                       op0=ALU.subtract, op1=ALU.mult)
                return S.done("vector", inst)
            t_ln = S.sec("vector", vapply)
            t_xuse[b] = t_ln
            def strp(q, b=b, t_ln=t_ln):
                S.wait(q, "scalar", t_ln)
                S.wait(q, "scalar", t_tuse[b])
                return S.dma(q, "scalar", lnt[b][:, :, :], lns[b][:, :], transpose=True)
            t_lnuse[b] = S.sec("scalar", strp)
            def sto(q, b=b, tt=tt, tk=t_lnuse[b]):
                S.wait(q, "sync", tk)
                return S.dma(q, "sync",
                             d["LNF"][:, :, tt * 128:(tt + 1) * 128].rearrange("k f t -> f k t"),
                             lnt[b][:, :, :])
            t_tuse[b] = S.sec("sync", sto)
        S.flush()


def stage23_consts(C):
    S, sb, sb23 = C.S, C.sb, C.sb23
    def io(q):
        q.iota(sb23["maskneg"][:, 0:256], pattern=[[1, 256]], base=0, channel_multiplier=0,
               allow_small_or_imprecise_dtypes=True)
        i2 = q.iota(sb23["maskneg"][:, 256:512], pattern=[[1, 256]], base=-128,
                    channel_multiplier=0, allow_small_or_imprecise_dtypes=True)
        t1 = S.done("gpsimd", i2)
        i3 = q.iota(sb23["IND"][:, :], pattern=[[0, KC * 128]], base=0, channel_multiplier=1,
                    allow_small_or_imprecise_dtypes=True)
        return t1, S.done("gpsimd", i3)
    t_io, t_indk = S.sec("gpsimd", io)

    def vm(q):
        S.wait(q, "vector", t_io)
        q.tensor_scalar(out=sb23["maskneg"][:, 0:256], in0=sb23["maskneg"][:, 0:256],
                        scalar1=sb["pcol"][:, 0:1], scalar2=NEGBIG,
                        op0=ALU.is_lt, op1=ALU.mult)
        inst = q.tensor_scalar(out=sb23["maskneg"][:, 256:512], in0=sb23["maskneg"][:, 256:512],
                               scalar1=sb["pcol"][:, 0:1], scalar2=NEGBIG,
                               op0=ALU.is_lt, op1=ALU.mult)
        t1 = S.done("vector", inst)
        S.wait(q, "vector", t_indk)
        inst = None
        for h in range(NH):
            inst = q.tensor_scalar(out=sb23["IND"][:, h * 128:(h + 1) * 128],
                                   in0=sb23["IND"][:, h * 128:(h + 1) * 128],
                                   scalar1=float(h), scalar2=None, op0=ALU.is_equal)
        t2 = S.done("vector", inst)
        q.memset(sb23["S_a"][:, :], 0.0)
        inst = q.memset(sb23["S_bf"][:, :], 0.0)
        t3 = S.done("vector", inst)
        return t1, t2, t3
    C.t_mask, C.t_ind, C.t_szero = S.sec("vector", vm)


def stage2(C, half):
    nc, S, sb, sb23, ps, d = C.nc, C.S, C.sb, C.sb23, C.ps, C.d
    ncc = 12
    ccbase = half * 12
    col0 = DM + half * 1536
    ncols = 1536 + (16 if half == 1 else 0)
    with contextlib.ExitStack() as es2:
        pf = f"s2h{half}_"
        WB = es2.enter_context(nc.sbuf_tensor(pf + "wb", [128, 16, ncols], BF16))
        LNG = es2.enter_context(nc.sbuf_tensor(pf + "lng", [128, 16, TGRP], BF16))
        HBCR = es2.enter_context(nc.sbuf_tensor(pf + "hbcr", [128, ncc, TGRP + 3], BF16))
        XBCg = es2.enter_context(nc.sbuf_tensor(pf + "xbc", [128, ncc, TGRP], BF16))
        CVA = es2.enter_context(nc.sbuf_tensor(pf + "cva", [128, 2, TGRP], BF16))
        CVB = es2.enter_context(nc.sbuf_tensor(pf + "cvb", [128, ncc, TGRP], BF16))
        DTRAW = es2.enter_context(nc.sbuf_tensor(pf + "dtraw", [16, TGRP], F32))
        PAN = es2.enter_context(nc.sbuf_tensor(pf + "pan", [16, 6, TGRP], F32))

        def ldw(q):
            return [S.dma(q, "gpsimd", WB[:, kc, :],
                          d["w_in"][kc * 128:(kc + 1) * 128, col0:col0 + ncols])
                    for kc in range(16)]
        t_w = S.sec("gpsimd", ldw)

        t_lng = None
        t_lng_use = None
        t_evac = {}
        t_conv = {}
        t_store = None
        t_silu = {}
        t_silu_slot = {}
        t_halo = None
        t_dtevac = None
        t_panT_evac = None
        t_mm_last = None
        t_pe2_prev = None
        t_panT_prev = None
        for g in range(NGRP):
            def ldl(q, g=g):
                S.wait(q, "sync", t_lng_use)
                return S.dma(q, "sync", LNG[:, :, :],
                             d["LNF"][:, :, g * TGRP:(g + 1) * TGRP].rearrange("k f t -> f k t"))
            t_lng = S.sec("sync", ldl)

            def halo(q, g=g):
                for cc in range(ncc):
                    S.wait(q, "vector", t_conv.get(cc))
                if g == 0:
                    S.wait(q, "vector", C.t_mask)
                    inst = q.tensor_copy(out=HBCR[:, :, 0:3],
                                         in_=sb["zb"].ap().rearrange("p (a b) -> p a b", b=3)[:, 0:ncc, :])
                else:
                    inst = q.tensor_copy(out=HBCR[:, :, 0:3], in_=HBCR[:, :, TGRP:TGRP + 3])
                return S.done("vector", inst)
            t_halo = S.sec("vector", halo)

            for cc in range(ncc):
                gcc = g * ncc + cc
                bank = ps[f"p{cc % 2}"]
                def mm(q, cc=cc, gcc=gcc, bank=bank, t_lng=t_lng):
                    S.wait(q, "tensor", t_w)
                    S.wait(q, "tensor", t_lng)
                    S.wait(q, "tensor", t_evac.get(gcc - 2))
                    inst = None
                    for kc in range(16):
                        inst = q.matmul(out=bank[:, :], lhsT=WB[:, kc, cc * 128:(cc + 1) * 128],
                                        rhs=LNG[:, kc, :], start=(kc == 0), stop=(kc == 15))
                    return S.done("tensor", inst)
                t_mm = S.sec("tensor", mm)
                t_mm_last = t_mm
                eng = "scalar" if cc % 2 == 0 else "vector"
                def ev(q, cc=cc, bank=bank, t_mm=t_mm, eng=eng):
                    S.wait(q, eng, t_mm)
                    S.wait(q, eng, t_halo)
                    S.wait(q, eng, t_conv.get(cc))
                    if eng == "scalar":
                        inst = q.copy(out=HBCR[:, cc, 3:TGRP + 3], in_=bank[:, :])
                    else:
                        inst = q.tensor_copy(out=HBCR[:, cc, 3:TGRP + 3], in_=bank[:, :])
                    return S.done(eng, inst)
                t_evac[gcc] = S.sec(eng, ev)
            t_lng_use = t_mm_last

            if half == 1:
                def mmdt(q, t_lng=t_lng):
                    S.wait(q, "tensor", t_dtevac)
                    inst = None
                    for kc in range(16):
                        inst = q.matmul(out=ps["p2"][0:16, :], lhsT=WB[:, kc, 1536:1552],
                                        rhs=LNG[:, kc, :], start=(kc == 0), stop=(kc == 15))
                    return S.done("tensor", inst)
                t_mmdt = S.sec("tensor", mmdt)
                t_lng_use = t_mmdt
                def evdt(q, t_mmdt=t_mmdt):
                    S.wait(q, "vector", t_mmdt)
                    inst = q.tensor_copy(out=DTRAW[:, :], in_=ps["p2"][0:16, :])
                    return S.done("vector", inst)
                t_dtevac = S.sec("vector", evdt)

            for cc in range(ncc):
                ccg = ccbase + cc
                eng = "vector"
                ei = cc % 2
                def taps(q, cc=cc, ccg=ccg, ei=ei):
                    S.wait(q, "vector", t_evac[cc])
                    S.wait(q, "vector", t_silu_slot.get(ei))
                    wcol = lambda k: sb["convw_col"][:, 4 * ccg + k:4 * ccg + k + 1]
                    q.tensor_scalar(out=CVA[:, ei, :], in0=HBCR[:, cc, 0:TGRP],
                                    scalar1=wcol(0), scalar2=None, op0=ALU.mult)
                    q.scalar_tensor_tensor(out=CVB[:, cc, :], in0=HBCR[:, cc, 1:TGRP + 1],
                                           scalar=wcol(1), in1=CVA[:, ei, :],
                                           op0=ALU.mult, op1=ALU.add)
                    q.scalar_tensor_tensor(out=CVA[:, ei, :], in0=HBCR[:, cc, 2:TGRP + 2],
                                           scalar=wcol(2), in1=CVB[:, cc, :],
                                           op0=ALU.mult, op1=ALU.add)
                    inst = q.scalar_tensor_tensor(out=CVB[:, cc, :], in0=HBCR[:, cc, 3:TGRP + 3],
                                                  scalar=wcol(3), in1=CVA[:, ei, :],
                                                  op0=ALU.mult, op1=ALU.add)
                    return S.done("vector", inst)
                t_conv[cc] = S.sec("vector", taps)
                def silu(q, cc=cc, ccg=ccg, g=g):
                    S.wait(q, "scalar", t_store)
                    S.wait(q, "scalar", t_conv[cc])
                    inst = q.activation(out=XBCg[:, cc, :], in_=CVB[:, cc, :], func=AF.Silu,
                                        bias=sb["convb_col"][:, ccg:ccg + 1])
                    return S.done("scalar", inst)
                t_silu[cc] = S.sec("scalar", silu)
            t_silu_slot[0] = t_silu[ncc - 1]
            t_silu_slot[1] = t_silu[ncc - 1]

            if half == 1:
                def pan(q, g=g):
                    S.wait(q, "vector", t_dtevac)
                    S.wait(q, "vector", t_pe2_prev)
                    inst = q.tensor_scalar(out=PAN[:, 0, :], in0=DTRAW[:, :],
                                           scalar1=sb["dtb_col"][:, 0:1], scalar2=None, op0=ALU.add)
                    return S.done("vector", inst)
                t_z = S.sec("vector", pan)
                def pe1(q, t_z=t_z):
                    S.wait(q, "scalar", t_z)
                    q.activation(out=PAN[:, 1, :], in_=PAN[:, 0, :], func=AF.Abs)
                    q.activation(out=PAN[:, 2, :], in_=PAN[:, 1, :], func=AF.Exp, scale=-1.0)
                    q.activation(out=PAN[:, 3, :], in_=PAN[:, 2, :], func=AF.Ln,
                                 bias=sb["cbias"][0:16, 1:2])
                    inst = q.activation(out=PAN[:, 1, :], in_=PAN[:, 0, :], func=AF.Relu)
                    return S.done("scalar", inst)
                t_sp = S.sec("scalar", pe1)
                def pv1(q, t_sp=t_sp, g=g):
                    S.wait(q, "vector", t_sp)
                    q.tensor_tensor(out=PAN[:, 0, :], in0=PAN[:, 1, :], in1=PAN[:, 3, :], op=ALU.add)
                    inst = q.tensor_scalar(out=PAN[:, 2, :], in0=PAN[:, 0, :],
                                           scalar1=sb["acol"][:, 0:1], scalar2=None, op0=ALU.mult)
                    t_dt = S.done("vector", inst)
                    inst = None
                    for c01 in range(2):
                        cg = g * 2 + c01
                        inst = q.tensor_tensor_scan(out=sb23["cum_all"][:, cg * 256:(cg + 1) * 256],
                                                    data0=sb["ones16"][:, :],
                                                    data1=PAN[:, 2, c01 * 256:(c01 + 1) * 256],
                                                    initial=0.0, op0=ALU.mult, op1=ALU.add)
                    return t_dt, S.done("vector", inst)
                t_dt, t_cum = S.sec("vector", pv1)
                def pe2(q, g=g, t_cum=t_cum):
                    S.wait(q, "scalar", t_cum)
                    S.wait(q, "scalar", t_panT_prev)
                    q.activation(out=PAN[:, 3, :], in_=PAN[:, 0, :], func=AF.Ln)
                    inst = None
                    for c01 in range(2):
                        cg = g * 2 + c01
                        csl = sb23["cum_all"][:, cg * 256:(cg + 1) * 256]
                        q.activation(out=PAN[:, 4, c01 * 256:(c01 + 1) * 256], in_=csl,
                                     func=AF.Exp, scale=-1.0,
                                     bias=sb23["cum_all"][:, cg * 256 + 255:cg * 256 + 256])
                        inst = q.activation(out=sb23["expcum_all"][:, cg * 256:(cg + 1) * 256],
                                            in_=csl, func=AF.Exp)
                    return S.done("scalar", inst)
                t_pe2 = S.sec("scalar", pe2)
                t_pe2_prev = t_pe2
                def pv2(q, g=g, t_pe2=t_pe2):
                    S.wait(q, "vector", t_pe2)
                    S.wait(q, "vector", t_panT_prev)
                    q.tensor_tensor(out=PAN[:, 5, :], in0=PAN[:, 3, :],
                                    in1=sb23["cum_all"][:, g * TGRP:(g + 1) * TGRP], op=ALU.subtract)
                    inst = q.tensor_tensor(out=PAN[:, 4, :], in0=PAN[:, 0, :], in1=PAN[:, 4, :],
                                           op=ALU.mult)
                    t_w_ = S.done("vector", inst)
                    inst = None
                    for c01 in range(2):
                        cg = g * 2 + c01
                        inst = q.tensor_scalar(out=PAN[:, 3, c01 * 256:(c01 + 1) * 256],
                                               in0=sb["ones16"][:, :],
                                               scalar1=sb23["expcum_all"][:, cg * 256 + 255:cg * 256 + 256],
                                               scalar2=None, op0=ALU.mult)
                    return t_w_, S.done("vector", inst)
                t_wp, t_cd = S.sec("vector", pv2)
                def ptr(q, g=g, t_wp=t_wp, t_cd=t_cd):
                    S.wait(q, "tensor", t_wp)
                    S.wait(q, "tensor", t_cd)
                    S.wait(q, "tensor", t_panT_evac)
                    inst = None
                    for c01 in range(2):
                        for jt in range(2):
                            for qi, pan_i in ((0, 5), (1, 4), (2, 3)):
                                j = (c01 * 2 + jt) * 3 + qi
                                inst = q.transpose(out=ps["p3"][0:128, j * 16:(j + 1) * 16],
                                                   in_=PAN[:, pan_i, (c01 * 2 + jt) * 128:(c01 * 2 + jt + 1) * 128],
                                                   identity=sb["I_f32"][0:16, 0:16])
                    return S.done("tensor", inst)
                t_panT = S.sec("tensor", ptr)
                t_panT_prev = t_panT
                def pev(q, g=g, t_panT=t_panT):
                    S.wait(q, "vector", t_panT)
                    inst = q.tensor_copy(out=sb23["colT"][:, g * 192:(g + 1) * 192],
                                         in_=ps["p3"][0:128, 0:192])
                    return S.done("vector", inst)
                t_panT_evac = S.sec("vector", pev)

            def sto(q, g=g):
                for cc in range(ncc):
                    S.wait(q, "sync", t_silu[cc])
                return S.dma(q, "sync",
                             d["XBC"][ccbase:ccbase + ncc, :, g * TGRP:(g + 1) * TGRP].rearrange("c f t -> f c t"),
                             XBCg[:, :, :])
            t_store = S.sec("sync", sto)
        S.flush()


def stage2g(C):
    nc, S, sb, ps, d = C.nc, C.S, C.sb, C.ps, C.d
    with contextlib.ExitStack() as es2:
        WG = es2.enter_context(nc.sbuf_tensor("s2g_wg", [128, 16, 2048], BF16))
        LNG = es2.enter_context(nc.sbuf_tensor("s2g_lng", [128, 16, TGRP], BF16))
        SGG = es2.enter_context(nc.sbuf_tensor("s2g_sgg", [128, 16, TGRP], BF16))
        def ldw(q):
            return [S.dma(q, "gpsimd", WG[:, kc, :],
                          d["w_in"][kc * 128:(kc + 1) * 128, 0:DM])
                    for kc in range(16)]
        t_w = S.sec("gpsimd", ldw)
        t_lng_use = None
        t_store = None
        t_ev = {}
        for g in range(NGRP):
            def ldl(q, g=g):
                S.wait(q, "sync", t_lng_use)
                return S.dma(q, "sync", LNG[:, :, :],
                             d["LNF"][:, :, g * TGRP:(g + 1) * TGRP].rearrange("k f t -> f k t"))
            t_lng = S.sec("sync", ldl)
            for oc in range(16):
                bank = ps[f"p{oc % 4}"]
                def mm(q, oc=oc, bank=bank, t_lng=t_lng):
                    S.wait(q, "tensor", t_w)
                    S.wait(q, "tensor", t_lng)
                    S.wait(q, "tensor", t_ev.get(oc - 4))
                    inst = None
                    for kc in range(16):
                        inst = q.matmul(out=bank[:, :], lhsT=WG[:, kc, oc * 128:(oc + 1) * 128],
                                        rhs=LNG[:, kc, :], start=(kc == 0), stop=(kc == 15))
                    return S.done("tensor", inst)
                t_mm = S.sec("tensor", mm)
                def ev(q, oc=oc, bank=bank, t_mm=t_mm):
                    S.wait(q, "scalar", t_mm)
                    S.wait(q, "scalar", t_store)
                    inst = q.activation(out=SGG[:, oc, :], in_=bank[:, :], func=AF.Silu)
                    return S.done("scalar", inst)
                t_ev[oc] = S.sec("scalar", ev)
            t_lng_use = t_ev[15]
            def sto(q, g=g):
                for oc in range(16):
                    S.wait(q, "sync", t_ev[oc])
                return S.dma(q, "sync",
                             d["SG"][:, :, g * TGRP:(g + 1) * TGRP].rearrange("k f t -> f k t"),
                             SGG[:, :, :])
            t_store = S.sec("sync", sto)
        S.flush()


def stage3(C):
    nc, S, sb, sb23, ps, d = C.nc, C.S, C.sb, C.sb23, C.ps, C.d
    with contextlib.ExitStack() as es3:
        XBCc = [es3.enter_context(nc.sbuf_tensor(f"s3_xbc{i}", [128, 24, QCH], BF16)) for i in range(2)]
        SGc = [es3.enter_context(nc.sbuf_tensor(f"s3_sg{i}", [128, 16, QCH], BF16)) for i in range(2)]
        XT = es3.enter_context(nc.sbuf_tensor("s3_xt", [128, 8, 128], BF16))
        BT = es3.enter_context(nc.sbuf_tensor("s3_bt", [128, 8, 128], BF16))
        BW = es3.enter_context(nc.sbuf_tensor("s3_bw", [128, 4, 2, 128], BF16))
        MTD = es3.enter_context(nc.sbuf_tensor("s3_mtd", [128, 4, 512], BF16))
        EAR = es3.enter_context(nc.sbuf_tensor("s3_ear", [128, 4, 512], F32))
        ESB = es3.enter_context(nc.sbuf_tensor("s3_esb", [128, 4, 512], F32))
        CS = es3.enter_context(nc.sbuf_tensor("s3_cs", [128, 4, QCH], BF16))
        Y = es3.enter_context(nc.sbuf_tensor("s3_y", [128, 4, QCH], F32))
        G16 = es3.enter_context(nc.sbuf_tensor("s3_g16", [128, 16, QCH], BF16))
        GSQ = es3.enter_context(nc.sbuf_tensor("s3_gsq", [128, 4, QCH], F32))
        GSO = [es3.enter_context(nc.sbuf_tensor(f"s3_gso{i}", [128, 16, QCH], BF16)) for i in range(2)]
        RSUMS = es3.enter_context(nc.sbuf_tensor("s3_rsums", [1, 2, 512], F32))
        RSTDS = es3.enter_context(nc.sbuf_tensor("s3_rstds", [1, 2, 512], F32))
        RTMP = es3.enter_context(nc.sbuf_tensor("s3_rtmp", [1, 512], F32))
        RB = es3.enter_context(nc.sbuf_tensor("s3_rb", [128, 2, 512], F32))

        t_ld = [None, None]
        t_ld_sg = [None, None]
        t_xbc_use = [None, None]
        t_sg_use = [None, None]
        t_gso_use = [None, None]
        t_mtd_done = {}
        t_cs_done = {}
        t_yev = {}
        t_sup = {}
        t_su = {}
        t_gso = {}
        t_e_prev = {}
        t_y_prev = {}
        t_sc_prev = {}
        t_g_prev = {}
        t_gsq_prev = {}
        t_gsq_tok = {}
        t_xt_ev = None
        t_bt_ev = None
        t_sc_last = {}
        pend_sbf = [None]
        t_mm3 = {}
        t_evac = {}
        t_srs = {}
        t_dma1 = {}
        t_dma2 = {}

        def emit_scalar_merged(q, h, fl, sl, t_ear):
            S.wait(q, "scalar", t_ear)
            inst = None
            for jt in range(2):
                ci = ((fl % NH * 0 + (fl // NH) * 2 + jt) * 3 + 0) * 16 + h
                inst = q.activation(out=ESB[:, sl, jt * 256:(jt + 1) * 256],
                                    in_=EAR[:, sl, jt * 256:(jt + 1) * 256], func=AF.Exp,
                                    bias=sb23["colT"][:, ci:ci + 1])
            return S.tok("scalar_c")

        for c in range(NCH):
            b = c % 2
            def ldc(q, c=c, b=b):
                S.wait(q, "sync", t_xbc_use[b])
                tk1 = S.dma(q, "sync", XBCc[b][:, :, :],
                            d["XBC"][:, :, c * QCH:(c + 1) * QCH].rearrange("a f t -> f a t"))
                S.wait(q, "sync", t_sg_use[b])
                tk2 = S.dma(q, "sync", SGc[b][:, :, :],
                            d["SG"][:, :, c * QCH:(c + 1) * QCH].rearrange("a f t -> f a t"))
                return tk1, tk2
            t_ld[b], t_ld_sg[b] = S.sec("sync", ldc)

            def trx(q, c=c, b=b):
                S.wait(q, "tensor", t_ld[b])
                S.wait(q, "tensor", t_xt_ev)
                S.wait(q, "tensor", C.t_Ibf)
                inst = None
                for gx in range(4):
                    for jt in range(2):
                        k = gx * 2 + jt
                        inst = q.transpose(out=ps["pt"][:, k * 128:(k + 1) * 128],
                                           in_=XBCc[b][:, gx, jt * 128:(jt + 1) * 128],
                                           identity=sb["I_bf"][:, :])
                return S.done("tensor", inst)
            t_trx = S.sec("tensor", trx)
            def evx(q, t_trx=t_trx, c=c):
                S.wait(q, "vector", t_trx)
                S.wait(q, "vector", t_sc_last.get(c - 1))
                inst = q.tensor_copy(out=XT.ap().rearrange("p a b -> p (a b)"),
                                     in_=ps["pt"][:, 0:1024])
                return S.done("vector", inst)
            t_xt_evac = S.sec("vector", evx)
            def trb(q, c=c, b=b, t_xt_evac=t_xt_evac):
                S.wait(q, "tensor", t_xt_evac)
                inst = None
                for gx in range(4):
                    for jt in range(2):
                        k = gx * 2 + jt
                        inst = q.transpose(out=ps["pt"][:, k * 128:(k + 1) * 128],
                                           in_=XBCc[b][:, 4 + gx, jt * 128:(jt + 1) * 128],
                                           identity=sb["I_bf"][:, :])
                return S.done("tensor", inst)
            t_trb = S.sec("tensor", trb)
            def evb(q, t_trb=t_trb, c=c):
                S.wait(q, "vector", t_trb)
                inst = q.tensor_copy(out=BT.ap().rearrange("p a b -> p (a b)"),
                                     in_=ps["pt"][:, 0:1024])
                return S.done("vector", inst)
            t_bt_ev = S.sec("vector", evb)
            t_xt_ev = t_bt_ev

            Sold = sb23["S_a"] if c % 2 == 0 else sb23["S_b"]
            Snew = sb23["S_b"] if c % 2 == 0 else sb23["S_a"]

            def emit_vgs(q, kv, b):
                S.wait(q, "vector", t_dma2[kv])
                S.wait(q, "vector", t_gso_use[b])
                toks = []
                for hh in range(2):
                    flv = kv * 2 + hh
                    hv = flv % NH
                    inst = q.scalar_tensor_tensor(out=GSO[b][:, hv, :], in0=G16[:, hv, :],
                                                  scalar=sb["normw_fm"][:, hv:hv + 1],
                                                  in1=RB[:, kv % 2, hh * 256:(hh + 1) * 256],
                                                  op0=ALU.mult, op1=ALU.mult)
                    toks.append(S.done("vector", inst))
                return toks

            def emit_rb_dma(kk):
                def dd1(q, kk=kk):
                    S.wait(q, "sync", t_srs[kk])
                    S.wait(q, "sync", t_dma2.get(kk - 2))
                    return S.dma(q, "sync", d["RSTDD"][kk % 2:kk % 2 + 1, :],
                                 RSTDS[0:1, kk % 2, :])
                t_dma1[kk] = S.sec("sync", dd1)
                def dd2(q, kk=kk):
                    S.wait(q, "sync", t_dma1[kk])
                    S.wait(q, "sync", t_gso.get(2 * (kk - 2) + 1))
                    return S.dma(q, "sync", RB[:, kk % 2, :],
                                 d["RSTDD"][kk % 2:kk % 2 + 1, :].partition_broadcast(128).squeeze(1))
                t_dma2[kk] = S.sec("sync", dd2)

            tok_mt = {}
            tok_cb = {}
            tok_ear = {}

            def e_mm1(h, c=c, b=b):
                fl = c * NH + h
                g4 = h // 4
                mt_bank = ps[f"p{h % 2}"]
                cb_bank = ps[f"p{2 + h % 2}"]
                def mm1(q, h=h, g4=g4, mt_bank=mt_bank, cb_bank=cb_bank, c=c, b=b, fl=fl):
                    S.wait(q, "tensor", t_ld[b])
                    S.wait(q, "tensor", t_mtd_done.get(fl - 2))
                    inst = None
                    for jt in range(2):
                        inst = q.matmul(out=mt_bank[:, jt * 256:(jt + 1) * 256],
                                        lhsT=XBCc[b][:, 4 + g4, jt * 128:(jt + 1) * 128],
                                        rhs=XBCc[b][:, 8 + h, :], start=True, stop=True)
                    tmt = S.done("tensor", inst)
                    S.wait(q, "tensor", t_cs_done.get(fl - 2))
                    q.matmul(out=cb_bank[:, 0:256], lhsT=sb23["IND"][:, h * 128:(h + 1) * 128],
                             rhs=sb23["cum_all"][:, c * 256:(c + 1) * 256], start=True, stop=True)
                    inst = q.matmul(out=cb_bank[:, 256:512], lhsT=sb23["IND"][:, h * 128:(h + 1) * 128],
                                    rhs=sb23["expcum_all"][:, c * 256:(c + 1) * 256], start=True, stop=True)
                    return tmt, S.done("tensor", inst)
                tok_mt[fl], tok_cb[fl] = S.sec("tensor", mm1)

            def e_vear(h, c=c):
                fl = c * NH + h
                sl = fl % 4
                cb_bank = ps[f"p{2 + h % 2}"]
                def vear(q, cb_bank=cb_bank, fl=fl, sl=sl):
                    S.wait(q, "vector", tok_cb[fl])
                    S.wait(q, "vector", t_e_prev.get(fl - 4))
                    q.tensor_tensor(out=EAR[:, sl, 0:256], in0=cb_bank[:, 0:256],
                                    in1=sb23["maskneg"][:, 0:256], op=ALU.add)
                    inst = q.tensor_tensor(out=EAR[:, sl, 256:512], in0=cb_bank[:, 0:256],
                                           in1=sb23["maskneg"][:, 256:512], op=ALU.add)
                    return S.done("vector", inst)
                tok_ear[fl] = S.sec("vector", vear)

            def e_sc(h, c=c):
                fl = c * NH + h
                sl = fl % 4
                t_e = S.sec("scalar", lambda q, h=h, fl=fl, sl=sl:
                            emit_scalar_merged(q, h, fl, sl, tok_ear[fl]))
                t_e_prev[fl] = t_e

            def e_vmid(h, c=c, b=b):
                fl = c * NH + h
                sl = fl % 4
                g4 = h // 4
                mt_bank = ps[f"p{h % 2}"]
                cb_bank = ps[f"p{2 + h % 2}"]
                def cidx(qi, jt, c=c, h=h):
                    return ((c * 2 + jt) * 3 + qi) * 16 + h
                def vmid(q, mt_bank=mt_bank, cb_bank=cb_bank,
                         h=h, g4=g4, b=b, fl=fl, sl=sl, cidx=cidx):
                    S.wait(q, "vector", t_e_prev[fl])
                    S.wait(q, "vector", tok_mt[fl])
                    S.wait(q, "vector", t_sc_prev.get(fl - 4))
                    inst = None
                    for jt in range(2):
                        inst = q.tensor_tensor(out=MTD[:, sl, jt * 256:(jt + 1) * 256],
                                               in0=mt_bank[:, jt * 256:(jt + 1) * 256],
                                               in1=ESB[:, sl, jt * 256:(jt + 1) * 256], op=ALU.mult)
                    t1 = S.done("vector", inst)
                    inst = q.tensor_tensor(out=CS[:, sl, :], in0=XBCc[b][:, 8 + h, :],
                                           in1=cb_bank[:, 256:512], op=ALU.mult)
                    t2 = S.done("vector", inst)
                    for jt in range(2):
                        inst = q.tensor_scalar(out=BW[:, sl, jt, :], in0=BT[:, g4 * 2 + jt, :],
                                               scalar1=sb23["colT"][:, cidx(1, jt):cidx(1, jt) + 1],
                                               scalar2=None, op0=ALU.mult)
                    return t1, t2, S.done("vector", inst)
                t_mtd, t_cs, t_bw = S.sec("vector", vmid)
                t_mtd_done[fl] = t_mtd
                t_cs_done[fl] = t_cs
                return t_bw

            def e_mm2(h, t_bw, c=c):
                fl = c * NH + h
                sl = fl % 4
                g4 = h // 4
                y_bank = ps[f"p{4 + h % 2}"]
                def mm2(q, h=h, g4=g4, y_bank=y_bank, t_bw=t_bw, c=c, fl=fl, sl=sl):
                    S.wait(q, "tensor", t_cs_done[fl])
                    S.wait(q, "tensor", t_yev.get(fl - 2))
                    S.wait(q, "tensor", t_sup.get(fl - NH) if c > 0 else C.t_szero)
                    q.matmul(out=y_bank[:, 0:256], lhsT=sb23["S_bf"][:, h * 128:(h + 1) * 128],
                             rhs=CS[:, sl, :], start=True, stop=False)
                    q.matmul(out=y_bank[:, 0:256], lhsT=XT[:, g4 * 2 + 0, :],
                             rhs=MTD[:, sl, 0:256], start=False, stop=False)
                    inst = q.matmul(out=y_bank[:, 0:256], lhsT=XT[:, g4 * 2 + 1, :],
                                    rhs=MTD[:, sl, 256:512], start=False, stop=True)
                    ty = S.done("tensor", inst)
                    S.wait(q, "tensor", t_bw)
                    q.matmul(out=y_bank[:, 256:384], lhsT=BW[:, sl, 0, :],
                             rhs=XT[:, g4 * 2 + 0, :], start=True, stop=False)
                    inst = q.matmul(out=y_bank[:, 256:384], lhsT=BW[:, sl, 1, :],
                                    rhs=XT[:, g4 * 2 + 1, :], start=False, stop=True)
                    return ty, S.done("tensor", inst)
                t_y, t_sc = S.sec("tensor", mm2)
                t_y_prev[fl] = t_y
                t_sc_prev[fl] = t_sc
                if h == NH - 1:
                    t_sc_last[c] = t_sc

            def e_vtail(h, c=c, b=b, Sold=Sold, Snew=Snew):
                fl = c * NH + h
                sl = fl % 4
                g4 = h // 4
                y_bank = ps[f"p{4 + h % 2}"]
                def cidx(qi, jt, c=c, h=h):
                    return ((c * 2 + jt) * 3 + qi) * 16 + h
                def vtail(q, h=h, g4=g4, y_bank=y_bank, b=b, fl=fl, sl=sl,
                          Sold=Sold, Snew=Snew, cidx=cidx):
                    S.wait(q, "vector", t_sc_prev[fl])
                    q.scalar_tensor_tensor(out=Y[:, sl, :], in0=XBCc[b][:, g4, :],
                                           scalar=sb["D_bc"][:, h:h + 1],
                                           in1=y_bank[:, 0:256], op0=ALU.mult, op1=ALU.add)
                    q.scalar_tensor_tensor(out=Snew[:, h * 128:(h + 1) * 128],
                                           in0=Sold[:, h * 128:(h + 1) * 128],
                                           scalar=sb23["colT"][:, cidx(2, 0):cidx(2, 0) + 1],
                                           in1=y_bank[:, 256:384], op0=ALU.mult, op1=ALU.add)
                    tsu = S.done("vector", None)
                    S.wait(q, "vector", t_ld_sg[b])
                    S.wait(q, "vector", t_gsq_tok.get(fl - NH))
                    inst = q.tensor_tensor(out=G16[:, h, :], in0=Y[:, sl, :],
                                           in1=SGc[b][:, h, :], op=ALU.mult)
                    tg = S.done("vector", inst)
                    kv = fl // 2 - 3
                    toks3 = emit_vgs(q, kv, b) if (h % 2 == 0 and h >= 6) else None
                    return tsu, tg, toks3
                t_suv, t_g, toks_vgs = S.sec("vector", vtail)
                t_su[fl] = t_suv
                t_yev[fl] = t_suv
                t_g_prev[fl] = t_g
                if toks_vgs is not None:
                    kv = fl // 2 - 3
                    t_gso[kv * 2], t_gso[kv * 2 + 1] = toks_vgs
                pend_sbf[0] = (fl, Snew)

            def e_stail(h, c=c):
                fl = c * NH + h
                sl = fl % 4
                k2 = fl // 2
                def stail(q, fl=fl, sl=sl, h=h, k2=k2):
                    S.wait(q, "scalar", t_g_prev[fl])
                    if pend_sbf[0] is not None:
                        pfl, Snew_ = pend_sbf[0]
                        ph = pfl % NH
                        q.copy(out=sb23["S_bf"][:, ph * 128:(ph + 1) * 128],
                               in_=Snew_[:, ph * 128:(ph + 1) * 128])
                        t_sup[pfl] = S.tok("scalar_c")
                        pend_sbf[0] = None
                    S.wait(q, "scalar", t_mm3.get(k2 - 2))
                    q.activation(out=GSQ[:, sl, :], in_=G16[:, h, :], func=AF.Square)
                    tq = S.tok("scalar_c")
                    te = ts = None
                    if h % 2 == 0 and h >= 4:
                        kk = k2 - 2
                        S.wait(q, "scalar", t_mm3[kk])
                        q.copy(out=RSUMS[0:1, kk % 2, :], in_=ps["p6"][0:1, 0:512])
                        te = S.tok("scalar_c")
                        S.wait(q, "scalar", t_dma1.get(kk - 2))
                        q.activation(out=RTMP[0:1, :], in_=RSUMS[0:1, kk % 2, :],
                                     func=AF.Ln, scale=1.0 / 128.0, bias=sb["cbias"][0:1, 0:1])
                        q.activation(out=RSTDS[0:1, kk % 2, :], in_=RTMP[0:1, :],
                                     func=AF.Exp, scale=-0.5)
                        ts = S.tok("scalar_c")
                    return tq, te, ts
                t_gsq, te_, ts_ = S.sec("scalar", stail)
                t_gsq_tok[fl] = t_gsq
                t_gsq_prev[fl] = t_gsq
                if te_ is not None:
                    t_evac[k2 - 2] = te_
                    t_srs[k2 - 2] = ts_
                    emit_rb_dma(k2 - 2)

            def e_mm3(j, c=c):
                def mm3(q, j=j):
                    S.wait(q, "tensor", t_gsq_tok[2 * j + 1])
                    S.wait(q, "tensor", t_evac.get(j - 1))
                    sg = (2 * j) % 4
                    inst = q.matmul(out=ps["p6"][0:1, 0:512], lhsT=sb["onescol"][:, :],
                                    rhs=GSQ[:, sg:sg + 2, :].rearrange("p a b -> p (a b)"),
                                    start=True, stop=True)
                    return S.done("tensor", inst)
                t_mm3[j] = S.sec("tensor", mm3)

            def e_evac_srs(kk):
                def sc_t(q, kk=kk):
                    S.wait(q, "scalar", t_mm3[kk])
                    q.copy(out=RSUMS[0:1, kk % 2, :], in_=ps["p6"][0:1, 0:512])
                    te = S.tok("scalar_c")
                    S.wait(q, "scalar", t_dma1.get(kk - 2))
                    q.activation(out=RTMP[0:1, :], in_=RSUMS[0:1, kk % 2, :],
                                 func=AF.Ln, scale=1.0 / 128.0, bias=sb["cbias"][0:1, 0:1])
                    q.activation(out=RSTDS[0:1, kk % 2, :], in_=RTMP[0:1, :],
                                 func=AF.Exp, scale=-0.5)
                    return te, S.tok("scalar_c")
                t_evac[kk], t_srs[kk] = S.sec("scalar", sc_t)

            e_mm1(0)
            e_vear(0)
            e_sc(0)
            e_mm1(1)
            e_vear(1)
            e_sc(1)
            vmid_bw = {0: e_vmid(0)}
            for h in range(NH):
                fl = c * NH + h
                k2 = fl // 2
                if h + 2 < NH:
                    e_mm1(h + 2)
                if h % 2 == 1 and h >= 3:
                    e_mm3(k2 - 1)
                e_mm2(h, vmid_bw.pop(h))
                if h + 2 < NH:
                    e_vear(h + 2)
                if h + 1 < NH:
                    vmid_bw[h + 1] = e_vmid(h + 1)
                e_vtail(h)
                if h + 2 < NH:
                    e_sc(h + 2)
                e_stail(h)

            k5, k6, k7 = c * 8 + 5, c * 8 + 6, c * 8 + 7
            e_evac_srs(k6)
            emit_rb_dma(k6)
            e_mm3(k7)
            e_evac_srs(k7)
            def v_t1(q, b=b, k5=k5):
                t_gso[k5 * 2], t_gso[k5 * 2 + 1] = emit_vgs(q, k5, b)
            S.sec("vector", v_t1)
            emit_rb_dma(k7)
            def v_t2(q, b=b, k6=k6, k7=k7):
                for kv in (k6, k7):
                    t_gso[kv * 2], t_gso[kv * 2 + 1] = emit_vgs(q, kv, b)
            S.sec("vector", v_t2)

            t_xbc_use[b] = t_yev[c * NH + NH - 1]
            t_sg_use[b] = t_g_prev[c * NH + NH - 1]
            def sto(q, c=c, b=b):
                for kv in range(c * 8, c * 8 + 8):
                    S.wait(q, "sync", t_gso[kv * 2 + 1])
                return S.dma(q, "sync",
                             d["GS"][:, :, c * QCH:(c + 1) * QCH].rearrange("k f t -> f k t"),
                             GSO[b][:, :, :])
            t_store = S.sec("sync", sto)
            t_gso_use[b] = t_store
        pend_sbf[0] = None
        S.flush()


def stage4(C):
    nc, S, sb, ps, d = C.nc, C.S, C.sb, C.ps, C.d
    with contextlib.ExitStack() as es4:
        WO = es4.enter_context(nc.sbuf_tensor("s4_wo", [128, 16, 2048], BF16))
        GSG = [es4.enter_context(nc.sbuf_tensor(f"s4_gs{i}", [128, 16, TGRP], BF16)) for i in range(2)]
        ATT = es4.enter_context(nc.sbuf_tensor("s4_att", [128, 2, 4, 16, 128], BF16))
        XTI = [es4.enter_context(nc.sbuf_tensor(f"s4_x{i}", [128, 2048], F32)) for i in range(2)]
        ATTT = [es4.enter_context(nc.sbuf_tensor(f"s4_at{i}", [128, 16, 128], BF16)) for i in range(2)]
        X1O = [es4.enter_context(nc.sbuf_tensor(f"s4_o{i}", [128, 2048], F32)) for i in range(2)]
        def ldw(q):
            return [S.dma(q, "gpsimd", WO[:, kc, :], d["w_out"][kc * 128:(kc + 1) * 128, :])
                    for kc in range(16)]
        t_w = S.sec("gpsimd", ldw)
        t_gs_use = [None, None]
        t_ev = {}
        t_tr_last_p = [None, None]
        t_xti_use = [None, None]
        t_attt_use = [None, None]
        t_x1o_use = [None, None]
        for g in range(NGRP):
            b = g % 2
            def ldg(q, g=g, b=b):
                S.wait(q, "gpsimd", t_gs_use[b])
                return S.dma(q, "gpsimd", GSG[b][:, :, :],
                             d["GS"][:, :, g * TGRP:(g + 1) * TGRP].rearrange("k f t -> f k t"))
            t_gs = S.sec("gpsimd", ldg)
            for oc in range(16):
                goc = g * 16 + oc
                bank = ps[f"p{oc % 4}"]
                def mm(q, oc=oc, goc=goc, bank=bank, t_gs=t_gs, b=b):
                    S.wait(q, "tensor", t_w)
                    S.wait(q, "tensor", t_gs)
                    S.wait(q, "tensor", t_ev.get(goc - 4))
                    inst = None
                    for kc in range(16):
                        inst = q.matmul(out=bank[:, :], lhsT=WO[:, kc, oc * 128:(oc + 1) * 128],
                                        rhs=GSG[b][:, kc, :], start=(kc == 0), stop=(kc == 15))
                    return S.done("tensor", inst)
                t_mm = S.sec("tensor", mm)
                def ev(q, oc=oc, bank=bank, t_mm=t_mm, b=b):
                    S.wait(q, "scalar", t_mm)
                    S.wait(q, "scalar", t_tr_last_p[b])
                    inst = q.copy(out=ATT[:, b, :, oc, :], in_=bank[:, :].rearrange("p (a b) -> p a b", a=4))
                    return S.done("scalar", inst)
                t_ev[goc] = S.sec("scalar", ev)
            t_gs_use[b] = t_ev[g * 16 + 15]
            for tt in range(4):
                gtt = g * 4 + tt
                b2 = tt % 2
                def ldx(q, gtt=gtt, b2=b2):
                    S.wait(q, "gpsimd", t_xti_use[b2])
                    return S.dma(q, "gpsimd", XTI[b2][:, :], d["x"][gtt * 128:(gtt + 1) * 128, :])
                t_x = S.sec("gpsimd", ldx)
                def trp(q, tt=tt, b2=b2, b=b, g=g):
                    for oc in range(16):
                        S.wait(q, "scalar", t_ev[g * 16 + oc])
                    S.wait(q, "scalar", t_attt_use[b2])
                    return S.dma(q, "scalar", ATTT[b2][:, :, :], ATT[:, b, tt, :, :],
                                 transpose=True)
                t_tr = S.sec("scalar", trp)
                if tt == 3:
                    t_tr_last_p[b] = t_tr
                def vadd(q, b2=b2, t_x=t_x, t_tr=t_tr):
                    S.wait(q, "vector", t_x)
                    S.wait(q, "vector", t_tr)
                    S.wait(q, "vector", t_x1o_use[b2])
                    inst = q.tensor_tensor(out=X1O[b2][:, :], in0=XTI[b2][:, :],
                                           in1=ATTT[b2].ap().rearrange("p a b -> p (a b)"),
                                           op=ALU.add)
                    return S.done("vector", inst)
                t_add = S.sec("vector", vadd)
                t_xti_use[b2] = t_add
                t_attt_use[b2] = t_add
                def sto(q, gtt=gtt, b2=b2, t_add=t_add):
                    S.wait(q, "sync", t_add)
                    return S.dma(q, "sync", d["X1"][gtt * 128:(gtt + 1) * 128, :], X1O[b2][:, :])
                t_x1o_use[b2] = S.sec("sync", sto)
        S.flush() if (g == NGRP - 1 and tt == 3) else None


def stage56(C):
    nc, S, sb, ps, d = C.nc, C.S, C.sb, C.ps, C.d
    LT = [64.0, 32.0, 16.0, 8.0, 4.0, 2.0, 1.0]
    with contextlib.ExitStack() as es5:
        WQ = es5.enter_context(nc.sbuf_tensor("s5_wq", [128, 16, 2048], BF16))
        XTI = [es5.enter_context(nc.sbuf_tensor(f"s5_x{i}", [128, 2048], F32)) for i in range(2)]
        XTI2 = es5.enter_context(nc.sbuf_tensor("s5_x2", [128, 2048], F32))
        UU = es5.enter_context(nc.sbuf_tensor("s5_uu", [128, 2048], F32))
        UA = es5.enter_context(nc.sbuf_tensor("s5_ua", [128, 2048], F32))
        UB = es5.enter_context(nc.sbuf_tensor("s5_ub", [128, 2048], F32))
        MB = es5.enter_context(nc.sbuf_tensor("s5_mb", [128, 2048], F32))
        SGN = es5.enter_context(nc.sbuf_tensor("s5_sgn", [128, 2048], BF16))
        XQT = es5.enter_context(nc.sbuf_tensor("s5_xqt", [128, 2048], BF16))
        XQF = es5.enter_context(nc.sbuf_tensor("s5_xqf", [128, 2, 4, 16, 128], BF16))
        FF = es5.enter_context(nc.sbuf_tensor("s5_ff", [128, 4, 16, 128], BF16))
        FFT = [es5.enter_context(nc.sbuf_tensor(f"s5_ft{i}", [128, 16, 128], BF16)) for i in range(4)]
        OUTB = es5.enter_context(nc.sbuf_tensor("s5_ob", [128, 2048], F32))
        SCC = es5.enter_context(nc.sbuf_tensor("s5_scc", [128, 16], F32))
        SM = es5.enter_context(nc.sbuf_tensor("s5_sm", [128, 28], F32))
        BNS = es5.enter_context(nc.sbuf_tensor("s5_bns", [128, 4, 6], F32))

        t_wq = S.sec("scalar", lambda q: S.dma(q, "scalar", WQ[:, :, :],
                     d["WQT"].rearrange("ic i oc o -> i ic (oc o)")))

        t_x_use = [None, None]
        t_ffev = {}
        t_lad = None
        t_lad2 = None
        t_qtr = None
        t_xqf_use = [None, None]
        t_ff_use = None
        t_ftr_use = [None, None, None, None]
        t_out_use = None
        t_sgn_use = None
        t_ft_last = None
        t_lad_v = None
        def emit_out_block(gp, ffev_snap):
            def ftr(q, gp=gp, ffev_snap=ffev_snap):
                for oc in range(16):
                    S.wait(q, "scalar", ffev_snap[oc])
                toks = []
                for tt in range(4):
                    S.wait(q, "scalar", t_ftr_use[tt])
                    toks.append(S.dma(q, "scalar", FFT[tt][:, :, :], FF[:, tt, :, :],
                                      transpose=True))
                return toks
            t_fts = S.sec("scalar", ftr)
            st["t_ft_last"] = t_fts[3]
            for tt in range(4):
                gtt = gp * 4 + tt
                b2 = tt % 2
                def ldx2(q, gtt=gtt):
                    S.wait(q, "sync", st["t_x2_use"])
                    return S.dma(q, "sync", XTI2[:, :], d["X1"][gtt * 128:(gtt + 1) * 128, :])
                t_x2 = S.sec("sync", ldx2)
                def vout(q, gtt=gtt, tt=tt, t_x2=t_x2, t_ft=t_fts[tt]):
                    S.wait(q, "vector", t_x2)
                    S.wait(q, "vector", t_ft)
                    S.wait(q, "vector", st["t_out_use"])
                    inst = q.scalar_tensor_tensor(out=OUTB[:, :],
                                                  in0=FFT[tt].ap().rearrange("p a b -> p (a b)"),
                                                  scalar=SCC[:, gtt % 16:gtt % 16 + 1],
                                                  in1=XTI2[:, :], op0=ALU.mult, op1=ALU.add)
                    return S.done("vector", inst)
                t_o = S.sec("vector", vout)
                st["t_x2_use"] = t_o
                t_ftr_use[tt] = t_o
                def sto(q, gtt=gtt, t_o=t_o):
                    S.wait(q, "sync", t_o)
                    return S.dma(q, "sync", d["out"][gtt * 128:(gtt + 1) * 128, :], OUTB[:, :])
                st["t_out_use"] = S.sec("sync", sto)

        st = {"t_ft_last": None, "t_out_use": None, "t_x2_use": None}
        ffev_snaps = {}
        qtr_snaps = {}
        for g in range(NGRP):
            t_qtr_tiles = []
            for tt in range(4):
                gtt = g * 4 + tt
                b = tt % 2
                c0 = 12 * b
                def ldx(q, gtt=gtt, b=b):
                    S.wait(q, "gpsimd", t_x_use[b])
                    return S.dma(q, "gpsimd", XTI[b][:, :], d["X1"][gtt * 128:(gtt + 1) * 128, :])
                t_x = S.sec("gpsimd", ldx)
                def vstats(q, b=b, c0=c0, t_x=t_x):
                    S.wait(q, "vector", t_x)
                    for j in range(4):
                        q.bn_stats(out=BNS[:, j, :], in_=XTI[b][:, j * 512:(j + 1) * 512])
                    q.bn_aggr(out=SM[:, c0 + 2:c0 + 4],
                              in_=BNS.ap().rearrange("p a b -> p (a b)"))
                    q.tensor_reduce(out=SM[:, c0 + 7:c0 + 8], in_=XTI[b][:, :], axis=AX.X, op=ALU.max)
                    q.tensor_reduce(out=SM[:, c0 + 8:c0 + 9], in_=XTI[b][:, :], axis=AX.X, op=ALU.min)
                    inst = q.tensor_scalar(out=SM[:, c0 + 12:c0 + 13], in0=SM[:, c0 + 2:c0 + 3],
                                           scalar1=-1.0, scalar2=None, op0=ALU.mult)
                    return S.done("vector", inst)
                t_var = S.sec("vector", vstats)
                def srstd(q, c0=c0, b=b, t_var=t_var):
                    S.wait(q, "scalar", t_var)
                    q.activation(out=SM[:, c0 + 5:c0 + 6], in_=SM[:, c0 + 3:c0 + 4], func=AF.Ln,
                                 bias=sb["cbias"][:, 0:1])
                    q.activation(out=SM[:, c0 + 6:c0 + 7], in_=SM[:, c0 + 5:c0 + 6],
                                 func=AF.Exp, scale=-0.5)
                    t_re = S.tok("scalar_c")
                    S.wait(q, "scalar", t_sgn_use)
                    q.activation(out=SGN[:, :], in_=XTI[b][:, :], func=AF.Sign,
                                 bias=SM[:, c0 + 12:c0 + 13])
                    q.activation(out=UA[:, :], in_=XTI[b][:, :], func=AF.Abs,
                                 bias=SM[:, c0 + 12:c0 + 13])
                    return t_re, S.tok("scalar_c")
                t_re, t_rstd = S.sec("scalar", srstd)
                t_x_use[b] = t_rstd
                def vsc(q, c0=c0, gtt=gtt, t_re=t_re):
                    S.wait(q, "vector", t_re)
                    q.tensor_tensor(out=SM[:, c0 + 9:c0 + 10], in0=SM[:, c0 + 7:c0 + 8],
                                    in1=SM[:, c0 + 2:c0 + 3], op=ALU.subtract)
                    q.tensor_tensor(out=SM[:, c0 + 10:c0 + 11], in0=SM[:, c0 + 2:c0 + 3],
                                    in1=SM[:, c0 + 8:c0 + 9], op=ALU.subtract)
                    q.tensor_tensor(out=SM[:, c0 + 9:c0 + 10], in0=SM[:, c0 + 9:c0 + 10],
                                    in1=SM[:, c0 + 10:c0 + 11], op=ALU.max)
                    q.tensor_tensor(out=SM[:, c0 + 9:c0 + 10], in0=SM[:, c0 + 9:c0 + 10],
                                    in1=SM[:, c0 + 6:c0 + 7], op=ALU.mult)
                    q.tensor_single_scalar(out=SM[:, c0 + 9:c0 + 10], in_=SM[:, c0 + 9:c0 + 10],
                                           scalar=1e-8, op=ALU.max)
                    q.tensor_scalar(out=SCC[:, gtt % 16:gtt % 16 + 1], in0=SM[:, c0 + 9:c0 + 10],
                                    scalar1=sb["smalls"][:, 35:36], scalar2=1.0 / 127.0,
                                    op0=ALU.mult, op1=ALU.mult)
                    q.reciprocal(out=SM[:, c0 + 10:c0 + 11], in_=SM[:, c0 + 9:c0 + 10])
                    q.tensor_scalar(out=SM[:, c0 + 10:c0 + 11], in0=SM[:, c0 + 10:c0 + 11],
                                    scalar1=127.0, scalar2=None, op0=ALU.mult)
                    inst = q.tensor_tensor(out=SM[:, c0 + 11:c0 + 12], in0=SM[:, c0 + 6:c0 + 7],
                                           in1=SM[:, c0 + 10:c0 + 11], op=ALU.mult)
                    return S.done("vector", inst)
                t_s3 = S.sec("vector", vsc)
                def vladprep(q, b=b, c0=c0, t_s3=t_s3, t_rstd=t_rstd):
                    S.wait(q, "vector", t_s3)
                    S.wait(q, "vector", t_rstd)
                    S.wait(q, "vector", t_lad)
                    S.wait(q, "vector", t_lad2)
                    S.wait(q, "vector", t_qtr)
                    inst = q.tensor_scalar(out=UU[:, :], in0=UA[:, :],
                                           scalar1=SM[:, c0 + 11:c0 + 12], scalar2=None,
                                           op0=ALU.mult)
                    return S.done("vector", inst)
                t_prep = S.sec("vector", vladprep)
                SPL = 2048
                def ladv(q, t_prep=t_prep):
                    lo, hi, w = 0, SPL, SPL
                    S.wait(q, "vector", t_prep)
                    cur = UU
                    bufs = [UA, UB]
                    for li, T in enumerate(LT):
                        q.tensor_scalar(out=MB[:, lo:hi], in0=cur[:, lo:hi],
                                        scalar1=T - 0.5, scalar2=-T, op0=ALU.is_ge, op1=ALU.mult)
                        nxt = bufs[li % 2]
                        q.tensor_tensor(out=nxt[:, lo:hi], in0=cur[:, lo:hi],
                                        in1=MB[:, lo:hi], op=ALU.add)
                        cur = nxt
                    q.tensor_tensor(out=MB[:, lo:hi], in0=UU[:, lo:hi], in1=cur[:, lo:hi],
                                    op=ALU.subtract)
                    inst = q.tensor_tensor(out=XQT[:, lo:hi], in0=MB[:, lo:hi],
                                           in1=SGN[:, lo:hi], op=ALU.mult)
                    return S.done("vector", inst)
                t_lad = S.sec("vector", ladv)
                t_lad2 = t_lad
                t_lad_v = t_lad
                t_sgn_use = t_lad
                def qtr(q, tt=tt, g=g, t_lad=t_lad, t_lad2=t_lad2):
                    S.wait(q, "scalar", t_lad)
                    S.wait(q, "scalar", t_lad2)
                    S.wait(q, "scalar", t_xqf_use[g % 2])
                    return S.dma(q, "scalar", XQF[:, g % 2, tt, :, :], XQT[:, :],
                                 transpose=True)
                t_qtr = S.sec("scalar", qtr)
                t_qtr_tiles.append(t_qtr)
            qtr_snaps[g] = tuple(t_qtr_tiles)
            if g >= 2:
                emit_out_block(g - 2, ffev_snaps[g - 2])
            if g < 1:
                continue
            gm = g - 1
            for oc in range(16):
                goc = gm * 16 + oc
                bank = ps[f"p{oc % 2}"]
                def mm(q, oc=oc, goc=goc, bank=bank, g=gm, toks=qtr_snaps[gm]):
                    S.wait(q, "tensor", t_wq)
                    for tk in toks:
                        S.wait(q, "tensor", tk)
                    S.wait(q, "tensor", t_ffev.get(goc - 2))
                    inst = None
                    for ic in range(16):
                        inst = q.matmul(out=bank[:, :], lhsT=WQ[:, ic, oc * 128:(oc + 1) * 128],
                                        rhs=XQF[:, g % 2, :, ic, :], start=(ic == 0), stop=(ic == 15))
                    return S.done("tensor", inst)
                t_mm = S.sec("tensor", mm)
                def ev(q, oc=oc, bank=bank, t_mm=t_mm):
                    S.wait(q, "scalar", t_mm)
                    S.wait(q, "scalar", st["t_ft_last"])
                    inst = q.copy(out=FF[:, :, oc, :], in_=bank[:, :].rearrange("p (a b) -> p a b", a=4))
                    return S.done("scalar", inst)
                t_ffev[goc] = S.sec("scalar", ev)
            t_xqf_use[gm % 2] = t_ffev[goc]
            ffev_snaps[gm] = {oc: t_ffev[gm * 16 + oc] for oc in range(16)}
        emit_out_block(NGRP - 2, ffev_snaps[NGRP - 2])
        gm = NGRP - 1
        for oc in range(16):
            goc = gm * 16 + oc
            bank = ps[f"p{oc % 2}"]
            def mm(q, oc=oc, goc=goc, bank=bank, g=gm, toks=qtr_snaps[gm]):
                S.wait(q, "tensor", t_wq)
                for tk in toks:
                    S.wait(q, "tensor", tk)
                S.wait(q, "tensor", t_ffev.get(goc - 2))
                inst = None
                for ic in range(16):
                    inst = q.matmul(out=bank[:, :], lhsT=WQ[:, ic, oc * 128:(oc + 1) * 128],
                                    rhs=XQF[:, g % 2, :, ic, :], start=(ic == 0), stop=(ic == 15))
                return S.done("tensor", inst)
            t_mm = S.sec("tensor", mm)
            def ev(q, oc=oc, bank=bank, t_mm=t_mm):
                S.wait(q, "scalar", t_mm)
                S.wait(q, "scalar", st["t_ft_last"])
                inst = q.copy(out=FF[:, :, oc, :], in_=bank[:, :].rearrange("p (a b) -> p a b", a=4))
                return S.done("scalar", inst)
            t_ffev[goc] = S.sec("scalar", ev)
        ffev_snaps[gm] = {oc: t_ffev[gm * 16 + oc] for oc in range(16)}
        emit_out_block(NGRP - 1, ffev_snaps[NGRP - 1])
        S.flush()


_CACHE = {}

def kernel(L0_attn_hidden_states, L0_attn_W_in, L0_attn_conv_weight,
           L0_attn_conv_bias, L0_attn_A_log, L0_attn_D, L0_attn_dt_bias,
           L0_attn_norm_weight, L0_attn_W_out, L0_ffn_weight):
    x = np.ascontiguousarray(L0_attn_hidden_states, dtype=np.float32)
    key = (tuple(DEBUG_OUTS), tuple(BUILD_STAGES))
    nc = _CACHE.get(key)
    if nc is None:
        nc = build_program(tuple(DEBUG_OUTS), tuple(BUILD_STAGES))
        _CACHE[key] = nc
    base = {
        "w_in": np.ascontiguousarray(L0_attn_W_in, dtype=np.float32),
        "conv_w": np.ascontiguousarray(L0_attn_conv_weight, dtype=np.float32),
        "conv_b": np.ascontiguousarray(L0_attn_conv_bias, dtype=np.float32),
        "a_log": np.ascontiguousarray(L0_attn_A_log, dtype=np.float32),
        "d_vec": np.ascontiguousarray(L0_attn_D, dtype=np.float32),
        "dt_bias": np.ascontiguousarray(L0_attn_dt_bias, dtype=np.float32),
        "normw": np.ascontiguousarray(L0_attn_norm_weight, dtype=np.float32),
        "w_out": np.ascontiguousarray(L0_attn_W_out, dtype=np.float32),
        "w_ffn": np.ascontiguousarray(L0_ffn_weight, dtype=np.float32),
    }
    in_maps = [dict(base, x=np.ascontiguousarray(x[i])) for i in range(8)]
    res = run_bass_kernel_spmd(nc, in_maps, core_ids=list(range(8)))
    _CACHE["last_results"] = res
    return np.stack([np.asarray(r["out"]) for r in res.results], axis=0)


# version 17
# speedup vs baseline: 1.0165x; 1.0165x over previous
import contextlib
import numpy as np

import concourse.bass as bass
import concourse.mybir as mybir
from concourse.bass_utils import run_bass_kernel_spmd

F32 = mybir.dt.float32
BF16 = mybir.dt.bfloat16
AF = mybir.ActivationFunctionType
ALU = mybir.AluOpType
AX = mybir.AxisListType

LSEQ = 2048
DM = 2048
PROJ = 5136
CONVD = 3072
NH = 16
QCH = 256
NCH = LSEQ // QCH
KC = 16
TGRP = 512
NGRP = LSEQ // TGRP
EPS = 1e-5
NEGBIG = -1.0e30

ENG_SEM = {"tensor": "tensor", "vector": "vector", "scalar": "scalar_c",
           "gpsimd": "gpsimd_c"}
DMA_SEM = {"sync": "d_sync", "gpsimd": "d_gpsimd", "scalar": "d_scalar"}

CHAIN_OPS = {"tensor_scalar", "tensor_single_scalar", "tensor_tensor", "tensor_copy",
             "tensor_add", "tensor_sub", "tensor_mul", "tensor_max", "tensor_relu",
             "scalar_tensor_tensor", "tensor_tensor_scan", "tensor_reduce",
             "reduce_sum", "reduce_max", "reciprocal", "memset", "memzero", "iota",
             "activation", "copy", "select", "copy_predicated", "add", "mul", "sign",
             "sqrt", "square", "bn_stats", "bn_aggr", "stream_shuffle"}


class RInst:
    __slots__ = ("entry",)
    def __init__(self, entry):
        self.entry = entry

    def then_inc(self, sem, n):
        self.entry[3].append((sem, n))
        return self


class RQ:
    def __init__(self, S, eng):
        self._S = S
        self._eng = eng

    def __getattr__(self, name):
        S, eng = self._S, self._eng
        def rec(*a, **k):
            entry = [name, a, k, []]
            S.pending[eng].append(entry)
            return RInst(entry)
        return rec


class QW:
    def __init__(self, S, eng, q):
        self._S = S
        self._eng = eng
        self._q = q

    def __getattr__(self, name):
        f = getattr(self._q, name)
        if name not in CHAIN_OPS:
            return f
        S, eng, q = self._S, self._eng, self._q
        sem = ENG_SEM[eng]
        def wrap(*a, **k):
            S.wait(q, eng, (sem, S.cnt[sem]))
            inst = f(*a, **k)
            S.cnt[sem] += 1
            inst.then_inc(S.sems[sem], 1)
            return inst
        return wrap

DEBUG_OUTS = []
BUILD_STAGES = ["s0", "s1", "s2a", "s2b", "s2g", "s3", "s4", "s56"]


class Sched:
    def __init__(self, nc, es):
        self.nc = nc
        self.block = es.enter_context(nc.Block())
        self.sems = {}
        self.cnt = {}
        self.waited = {}
        self.bar_expected = 0
        self.bar_snapshot = {}
        self.stage_last = {}
        self.stage_dma = {}
        self.stage_first_wait_done = set()
        self.pending = {e: [] for e in ("tensor", "vector", "scalar", "gpsimd", "sync")}
        names = ["tensor", "vector", "scalar_c", "gpsimd_c"]
        for qn in ("sync", "gpsimd", "scalar"):
            for k in range(8):
                names.append(f"d_{qn}_{k}")
        for name in names:
            self.sems[name] = es.enter_context(nc.semaphore(name))
            self.cnt[name] = 0
        self.dma_rot = {"sync": 0, "gpsimd": 0, "scalar": 0}

    def sec(self, eng, body):
        q = RQ(self, eng)
        if eng not in self.stage_first_wait_done:
            self.stage_first_wait_done.add(eng)
            for sname, v in self.bar_snapshot.items():
                if v > 0 and self.waited.get((eng, sname), 0) < v:
                    q.wait_ge(self.sems[sname], v)
                    self.waited[(eng, sname)] = v
        qq = QW(self, eng, q) if eng in ("vector", "scalar", "gpsimd") else q
        return body(qq)

    def flush(self):
        for eng, entries in self.pending.items():
            if not entries:
                continue
            def fn(q, entries=tuple(entries)):
                for name, a, k, incs in entries:
                    inst = getattr(q, name)(*a, **k)
                    for sem, n in incs:
                        inst.then_inc(sem, n)
            getattr(self.block, eng)(fn)
            self.pending[eng] = []

    def wait(self, q, eng, token):
        if token is None:
            return
        if isinstance(token, (list, tuple)) and token and isinstance(token[0], (list, tuple)):
            for tk in token:
                self.wait(q, eng, tk)
            return
        sem, val = token
        if val <= 0:
            return
        k = (eng, sem)
        if self.waited.get(k, 0) >= val:
            return
        if isinstance(q, QW):
            q = q._q
        q.wait_ge(self.sems[sem], val)
        self.waited[k] = val

    dsc = None
    _dsc_init = False

    def _dsc_ensure(self, q):
        if not self._dsc_init:
            self._dsc_init = True
            q.memzero(self.dsc[0:1, 0:1])
            q.activation(out=self.dsc[0:1, 1:2], in_=self.dsc[0:1, 0:1], func=AF.Exp)

    def dum_pair(self, q):
        self._dsc_ensure(q)
        q.activation(out=self.dsc[0:1, 1:2], in_=self.dsc[0:1, 0:1], func=AF.Exp)
        q.activation(out=self.dsc[0:1, 0:1], in_=self.dsc[0:1, 1:2], func=AF.Ln)

    def dum_ln(self, q):
        self._dsc_ensure(q)
        q.activation(out=self.dsc[0:1, 0:1], in_=self.dsc[0:1, 1:2], func=AF.Ln)

    def dum_silu(self, q):
        self._dsc_ensure(q)
        q.activation(out=self.dsc[0:1, 2:3], in_=self.dsc[0:1, 0:1], func=AF.Silu)

    def tok(self, sem):
        return (sem, self.cnt[sem])

    def done(self, eng, inst):
        sem = ENG_SEM[eng]
        if eng == "tensor":
            self.cnt[sem] += 1
            inst.then_inc(self.sems[sem], 1)
        return (sem, self.cnt[sem])

    def note(self, eng, inst):
        self.stage_last[eng] = inst
        return inst

    def dma(self, q, eng, out, in_, transpose=False):
        if isinstance(q, QW):
            q = q._q
        k = self.dma_rot[eng]
        self.dma_rot[eng] = (k + 1) % 8
        sem = f"d_{eng}_{k}"
        self.wait(q, eng, (sem, self.cnt[sem]))
        if transpose:
            inst = q.dma_start_transpose(out=out, in_=in_)
        else:
            inst = q.dma_start(out=out, in_=in_)
        self.cnt[sem] += 16
        inst.then_inc(self.sems[sem], 16)
        return (sem, self.cnt[sem])

    def stage_end(self):
        self.flush()
        self.stage_first_wait_done = set()
        self.bar_snapshot = dict(self.cnt)


class Ctx:
    pass


def build_program(debug_outs=(), stages=None):
    stages = list(stages if stages is not None else BUILD_STAGES)
    nc = bass.Bass("TRN2", target_bir_lowering=False)
    C = Ctx()
    C.nc = nc
    d = {}
    d["x"] = nc.dram_tensor("x", [LSEQ, DM], F32, kind="ExternalInput").ap()
    d["w_in"] = nc.dram_tensor("w_in", [DM, PROJ], F32, kind="ExternalInput").ap()
    d["conv_w"] = nc.dram_tensor("conv_w", [CONVD, 4], F32, kind="ExternalInput").ap()
    d["conv_b"] = nc.dram_tensor("conv_b", [CONVD], F32, kind="ExternalInput").ap()
    d["a_log"] = nc.dram_tensor("a_log", [NH], F32, kind="ExternalInput").ap()
    d["d_vec"] = nc.dram_tensor("d_vec", [NH], F32, kind="ExternalInput").ap()
    d["dt_bias"] = nc.dram_tensor("dt_bias", [NH], F32, kind="ExternalInput").ap()
    d["normw"] = nc.dram_tensor("normw", [DM], F32, kind="ExternalInput").ap()
    d["w_out"] = nc.dram_tensor("w_out", [DM, DM], F32, kind="ExternalInput").ap()
    d["w_ffn"] = nc.dram_tensor("w_ffn", [DM, DM], F32, kind="ExternalInput").ap()
    d["out"] = nc.dram_tensor("out", [LSEQ, DM], F32, kind="ExternalOutput").ap()
    def scratch(name, shape, dt_):
        kind = "ExternalOutput" if name in debug_outs else "Internal"
        nm = ("dbg_" + name) if name in debug_outs else name
        return nc.dram_tensor(nm, shape, dt_, kind=kind).ap()
    d["LNF"] = scratch("LNF", [KC, 128, LSEQ], BF16)
    d["XBC"] = scratch("XBC", [24, 128, LSEQ], BF16)
    d["SG"] = scratch("SG", [KC, 128, LSEQ], BF16)
    d["GS"] = scratch("GS", [KC, 128, LSEQ], BF16)
    d["X1"] = scratch("X1", [LSEQ, DM], F32)
    d["WQT"] = scratch("WQT", [KC, 128, KC, 128], BF16)
    d["RSTDD"] = scratch("RSTDD", [2, 512], F32)
    C.d = d

    with contextlib.ExitStack() as es:
        S = Sched(nc, es)
        C.S = S
        sb = {}
        C.sb = sb
        ps = {}
        C.ps = ps
        def sbuf(name, shape, dt_):
            sb[name] = es.enter_context(nc.sbuf_tensor(name, shape, dt_))
            return sb[name]
        sbuf("I_f32", [128, 128], F32)
        sbuf("I_bf", [128, 128], BF16)
        sbuf("onesrow", [1, 128], F32)
        sbuf("onescol", [128, 1], F32)
        sbuf("ones16", [16, 256], F32)
        sbuf("D_bc", [128, 16], F32)
        sbuf("normw_fm", [128, 16], F32)
        sbuf("convw_col", [128, 96], F32)
        sbuf("convb_col", [128, 24], F32)
        sbuf("acol", [16, 1], F32)
        sbuf("dtb_col", [16, 1], F32)
        sbuf("drow", [1, 16], F32)
        sbuf("smalls", [128, 48], F32)
        sbuf("cbias", [128, 2], F32)
        sbuf("row24", [24, 128], F32)
        sbuf("zb", [128, 72], BF16)
        sbuf("pcol", [128, 1], F32)
        sbuf("irow", [128, 512], F32)
        sbuf("dsc", [1, 3], F32)
        S.dsc = sb["dsc"]
        for i in range(7):
            ps[f"p{i}"] = es.enter_context(nc.psum_tensor(f"p{i}", [128, 512], F32))
        ps["pt"] = es.enter_context(nc.psum_tensor("pt", [128, 1024], BF16))

        if "s0" in stages:
            stage0(C)
            S.stage_end()
        if "s1" in stages:
            stage1(C)
            S.stage_end()
        with contextlib.ExitStack() as es23:
            sb23 = {}
            C.sb23 = sb23
            def sbuf23(name, shape, dt_):
                sb23[name] = es23.enter_context(nc.sbuf_tensor(name, shape, dt_))
                return sb23[name]
            if any(s in stages for s in ("s2a", "s2b", "s2g", "s3")):
                sbuf23("IND", [16, KC * 128], F32)
                sbuf23("maskneg", [128, 512], F32)
                sbuf23("cum_all", [16, LSEQ], F32)
                sbuf23("expcum_all", [16, LSEQ], F32)
                sbuf23("colT", [128, NCH * 2 * 3 * 16], F32)
                sbuf23("S_a", [128, NH * 128], F32)
                sbuf23("S_b", [128, NH * 128], F32)
                sbuf23("S_bf", [128, NH * 128], BF16)
                stage23_consts(C)
            if "s2a" in stages:
                stage2(C, 0)
                S.stage_end()
            if "s2b" in stages:
                stage2(C, 1)
                S.stage_end()
            if "s2g" in stages:
                stage2g(C)
                S.stage_end()
            if "s3" in stages:
                stage3(C)
                S.stage_end()
        if "s4" in stages:
            stage4(C)
            S.stage_end()
        if "s56" in stages:
            stage56(C)
            S.stage_end()
        def drain(q):
            for sname, v in S.cnt.items():
                if sname.startswith("d_") and v > 0 and S.waited.get(("sync", sname), 0) < v:
                    q.wait_ge(S.sems[sname], v)
                    S.waited[("sync", sname)] = v
        S.sec("sync", drain)
        S.flush()
    return nc


def stage0(C):
    nc, S, sb, ps, d = C.nc, C.S, C.sb, C.ps, C.d

    def ld(q):
        t = {}
        t["convw"] = S.dma(q, "sync", sb["convw_col"].ap().rearrange("p (cc k) -> p cc k", k=4),
                           d["conv_w"].rearrange("(cc p) k -> p cc k", p=128))
        t["cb"] = S.dma(q, "sync", sb["row24"][:, :],
                        d["conv_b"].rearrange("(cc p) -> cc p", p=128))
        t["al"] = S.dma(q, "sync", sb["acol"][:, :], d["a_log"].rearrange("(a o) -> a o", o=1))
        t["db"] = S.dma(q, "sync", sb["dtb_col"][:, :], d["dt_bias"].rearrange("(a o) -> a o", o=1))
        t["dr"] = S.dma(q, "sync", sb["drow"][:, :], d["d_vec"].rearrange("(o a) -> o a", o=1))
        return t
    t = S.sec("sync", ld)

    def io(q):
        i1 = q.iota(sb["irow"][:, :], pattern=[[1, 512]], base=0, channel_multiplier=0,
                    allow_small_or_imprecise_dtypes=True)
        t1 = S.done("gpsimd", i1)
        i2 = q.iota(sb["pcol"][:, :], pattern=[[0, 1]], base=0, channel_multiplier=1,
                    allow_small_or_imprecise_dtypes=True)
        return t1, S.done("gpsimd", i2)
    t_irow, t_pcol = S.sec("gpsimd", io)

    def consts(q):
        S.wait(q, "vector", t_irow)
        S.wait(q, "vector", t_pcol)
        inst = q.tensor_scalar(out=sb["I_f32"][:, :], in0=sb["irow"][:, 0:128],
                               scalar1=sb["pcol"][:, 0:1], scalar2=None, op0=ALU.is_equal)
        tI = S.done("vector", inst)
        inst = q.tensor_copy(out=sb["I_bf"][:, :], in_=sb["I_f32"][:, :])
        tIb = S.done("vector", inst)
        q.memset(sb["onesrow"][:, :], 1.0)
        q.memset(sb["onescol"][:, :], 1.0)
        q.memset(sb["ones16"][:, :], 1.0)
        q.memset(sb["cbias"][:, 0:1], EPS)
        q.memset(sb["cbias"][:, 1:2], 1.0)
        inst = q.memset(sb["zb"][:, :], 0.0)
        return tI, tIb, S.done("vector", inst)
    t_I, t_Ibf, t_ms = S.sec("vector", consts)
    C.t_Ibf = t_Ibf

    def tr2(q):
        S.wait(q, "tensor", t["cb"])
        S.wait(q, "tensor", t_I)
        inst = q.transpose(out=ps["p0"][0:128, 0:24], in_=sb["row24"][0:24, 0:128],
                           identity=sb["I_f32"][0:24, 0:24])
        return S.done("tensor", inst)
    t_cbT = S.sec("tensor", tr2)
    def ev2(q):
        S.wait(q, "vector", t_cbT)
        inst = q.tensor_copy(out=sb["convb_col"][:, :], in_=ps["p0"][0:128, 0:24])
        return S.done("vector", inst)
    t_cbfm = S.sec("vector", ev2)

    def ldnw(q):
        S.wait(q, "sync", t_cbT)
        return S.dma(q, "sync", sb["row24"][0:16, :],
                     d["normw"].rearrange("(a b) -> a b", b=128))
    t_nw = S.sec("sync", ldnw)
    def tr1(q):
        S.wait(q, "tensor", t_nw)
        S.wait(q, "tensor", t_cbfm)
        inst = q.transpose(out=ps["p0"][0:128, 0:16], in_=sb["row24"][0:16, 0:128],
                           identity=sb["I_f32"][0:16, 0:16])
        return S.done("tensor", inst)
    t_nwT = S.sec("tensor", tr1)
    def ev1(q):
        S.wait(q, "vector", t_nwT)
        inst = q.tensor_copy(out=sb["normw_fm"][:, :], in_=ps["p0"][0:128, 0:16])
        return S.done("vector", inst)
    t_nwfm = S.sec("vector", ev1)

    def mmD(q):
        S.wait(q, "tensor", t["dr"])
        S.wait(q, "tensor", t_ms)
        S.wait(q, "tensor", t_nwfm)
        inst = q.matmul(out=ps["p0"][0:128, 0:16], lhsT=sb["onesrow"][:, :],
                        rhs=sb["drow"][:, :], start=True, stop=True)
        return S.done("tensor", inst)
    t_dbc = S.sec("tensor", mmD)
    def ev3(q):
        S.wait(q, "vector", t_dbc)
        inst = q.tensor_copy(out=sb["D_bc"][:, :], in_=ps["p0"][0:128, 0:16])
        return S.done("vector", inst)
    t_dbcv = S.sec("vector", ev3)

    def sA(q):
        S.wait(q, "scalar", t["al"])
        inst = q.activation(out=sb["smalls"][0:16, 40:41], in_=sb["acol"][:, :], func=AF.Exp)
        return S.done("scalar", inst)
    t_ea = S.sec("scalar", sA)
    def vA(q):
        S.wait(q, "vector", t_ea)
        inst = q.tensor_scalar(out=sb["acol"][:, :], in0=sb["smalls"][0:16, 40:41],
                               scalar1=-1.0, scalar2=None, op0=ALU.mult)
        return S.done("vector", inst)
    S.sec("vector", vA)

    with contextlib.ExitStack() as es0:
        wld = [es0.enter_context(nc.sbuf_tensor(f"s0_wld{i}", [128, 2048], F32)) for i in range(2)]
        wjunk = es0.enter_context(nc.sbuf_tensor("s0_junk", [128, 2048], F32))
        wn = es0.enter_context(nc.sbuf_tensor("s0_wn", [128, 2048], F32))
        q2t = es0.enter_context(nc.sbuf_tensor("s0_q2", [128, 2048], F32))
        wqb = es0.enter_context(nc.sbuf_tensor("s0_wqb", [128, 2048], BF16))
        tq = es0.enter_context(nc.sbuf_tensor("s0_tq", [128, 16, 128], BF16))
        rs = es0.enter_context(nc.sbuf_tensor("s0_rs", [128, 16], F32))

        tload = [None, None]
        tuse = [None, None]
        for sl in range(16):
            b = sl % 2
            def ldw(q, b=b, sl=sl):
                S.wait(q, "sync", tuse[b])
                return S.dma(q, "sync", wld[b][:, :], d["w_ffn"][sl * 128:(sl + 1) * 128, :])
            tload[b] = S.sec("sync", ldw)
            def absum(q, b=b, sl=sl):
                S.wait(q, "scalar", tload[b])
                inst = q.activation(out=wjunk[:, :], in_=wld[b][:, :], func=AF.Abs,
                                    accum_out=rs[:, sl:sl + 1])
                return S.done("scalar", inst)
            tuse[b] = S.sec("scalar", absum)
        def vtot(q):
            S.wait(q, "vector", tuse[0])
            S.wait(q, "vector", tuse[1])
            inst = q.tensor_reduce(out=sb["smalls"][:, 32:33], in_=rs[:, :], axis=AX.X, op=ALU.add)
            return S.done("vector", inst)
        t_rtot = S.sec("vector", vtot)
        def mtot(q):
            S.wait(q, "tensor", t_rtot)
            S.wait(q, "tensor", t_dbcv)
            inst = q.matmul(out=ps["p1"][0:1, 0:1], lhsT=sb["smalls"][:, 32:33],
                            rhs=sb["onescol"][:, :], start=True, stop=True)
            return S.done("tensor", inst)
        t_tot = S.sec("tensor", mtot)
        def vsc(q):
            S.wait(q, "vector", t_tot)
            q.tensor_scalar(out=sb["smalls"][0:1, 33:34], in0=ps["p1"][0:1, 0:1],
                            scalar1=1.0 / (DM * DM), scalar2=1e-6, op0=ALU.mult, op1=ALU.max)
            inst = q.reciprocal(out=sb["smalls"][0:1, 34:35], in_=sb["smalls"][0:1, 33:34])
            return S.done("vector", inst)
        t_inv = S.sec("vector", vsc)
        def mbc(q):
            S.wait(q, "tensor", t_inv)
            q.matmul(out=ps["p1"][0:128, 2:3], lhsT=sb["onesrow"][:, :],
                     rhs=sb["smalls"][0:1, 33:34], start=True, stop=True)
            inst = q.matmul(out=ps["p1"][0:128, 3:4], lhsT=sb["onesrow"][:, :],
                            rhs=sb["smalls"][0:1, 34:35], start=True, stop=True)
            return S.done("tensor", inst)
        t_bc2 = S.sec("tensor", mbc)
        def vbc(q):
            S.wait(q, "vector", t_bc2)
            q.tensor_copy(out=sb["smalls"][:, 35:36], in_=ps["p1"][0:128, 2:3])
            inst = q.tensor_copy(out=sb["smalls"][:, 36:37], in_=ps["p1"][0:128, 3:4])
            return S.done("vector", inst)
        t_inv2 = S.sec("vector", vbc)

        tload = [None, None]
        t_tq = None
        t_st = None
        t_q2_prev = [None]
        for sl in range(16):
            b = sl % 2
            def ldw(q, b=b, sl=sl):
                S.wait(q, "sync", tuse[b])
                return S.dma(q, "sync", wld[b][:, :], d["w_ffn"][sl * 128:(sl + 1) * 128, :])
            tload[b] = S.sec("sync", ldw)
            def vwn(q, b=b, t_q2p=None):
                S.wait(q, "vector", tload[b])
                S.wait(q, "vector", t_inv2)
                S.wait(q, "vector", t_tq)
                S.wait(q, "vector", t_q2_prev[0])
                inst = q.tensor_scalar(out=wn[:, :], in0=wld[b][:, :],
                                       scalar1=sb["smalls"][:, 36:37], scalar2=None, op0=ALU.mult)
                return S.done("vector", inst)
            tuse[b] = S.sec("vector", vwn)
            def gq2(q, b=b):
                inst = q.tensor_scalar(out=q2t[:, :], in0=wn[:, :], scalar1=-0.5,
                                       scalar2=None, op0=ALU.is_le)
                return S.done("vector", inst)
            t_q2 = S.sec("vector", gq2)
            t_q2_prev[0] = t_q2
            def vwq(q, t_q2=t_q2):
                S.wait(q, "vector", t_q2)
                inst = q.scalar_tensor_tensor(out=wqb[:, :], in0=wn[:, :], scalar=0.5,
                                              in1=q2t[:, :], op0=ALU.is_ge, op1=ALU.subtract)
                return S.done("vector", inst)
            t_wq = S.sec("vector", vwq)
            def str_(q, t_wq=t_wq):
                S.wait(q, "scalar", t_wq)
                S.wait(q, "scalar", t_st)
                return S.dma(q, "scalar", tq[:, :, :], wqb[:, :], transpose=True)
            t_tq = S.sec("scalar", str_)
            def sto(q, t_tq=t_tq, sl=sl):
                S.wait(q, "sync", t_tq)
                return S.dma(q, "sync",
                             d["WQT"][:, :, sl, :].rearrange("ic i o -> i ic o"),
                             tq[:, :, :])
            t_st = S.sec("sync", sto)
        S.flush()


def stage1(C):
    nc, S, sb, ps, d = C.nc, C.S, C.sb, C.ps, C.d
    with contextlib.ExitStack() as es1:
        xt = [es1.enter_context(nc.sbuf_tensor(f"s1_x{i}", [128, 2048], F32)) for i in range(2)]
        bns = es1.enter_context(nc.sbuf_tensor("s1_bns", [128, 4, 6], F32))
        lns = [es1.enter_context(nc.sbuf_tensor(f"s1_l{i}", [128, 2048], BF16)) for i in range(2)]
        lnt = [es1.enter_context(nc.sbuf_tensor(f"s1_t{i}", [128, 16, 128], BF16)) for i in range(2)]
        t_xuse = [None, None]
        t_lnuse = [None, None]
        t_tuse = [None, None]
        for tt in range(16):
            b = tt % 2
            c0 = 8 * b
            def ldx(q, b=b, tt=tt):
                S.wait(q, "sync", t_xuse[b])
                return S.dma(q, "sync", xt[b][:, :], d["x"][tt * 128:(tt + 1) * 128, :])
            t_x = S.sec("sync", ldx)
            def vstats(q, b=b, c0=c0, t_x=t_x):
                S.wait(q, "vector", t_x)
                for j in range(4):
                    q.bn_stats(out=bns[:, j, :], in_=xt[b][:, j * 512:(j + 1) * 512])
                inst = q.bn_aggr(out=sb["smalls"][:, c0 + 2:c0 + 4],
                                 in_=bns.ap().rearrange("p a b -> p (a b)"))
                return S.done("vector", inst)
            t_var = S.sec("vector", vstats)
            def srstd(q, c0=c0, t_var=t_var):
                S.wait(q, "scalar", t_var)
                q.activation(out=sb["smalls"][:, c0 + 5:c0 + 6],
                             in_=sb["smalls"][:, c0 + 3:c0 + 4], func=AF.Ln,
                             bias=sb["cbias"][:, 0:1])
                inst = q.activation(out=sb["smalls"][:, c0 + 6:c0 + 7],
                                    in_=sb["smalls"][:, c0 + 5:c0 + 6], func=AF.Exp, scale=-0.5)
                return S.done("scalar", inst)
            t_rstd = S.sec("scalar", srstd)
            def vapply(q, b=b, c0=c0, t_rstd=t_rstd):
                S.wait(q, "vector", t_rstd)
                S.wait(q, "vector", t_lnuse[b])
                inst = q.tensor_scalar(out=lns[b][:, :], in0=xt[b][:, :],
                                       scalar1=sb["smalls"][:, c0 + 2:c0 + 3],
                                       scalar2=sb["smalls"][:, c0 + 6:c0 + 7],
                                       op0=ALU.subtract, op1=ALU.mult)
                return S.done("vector", inst)
            t_ln = S.sec("vector", vapply)
            t_xuse[b] = t_ln
            def strp(q, b=b, t_ln=t_ln):
                S.wait(q, "scalar", t_ln)
                S.wait(q, "scalar", t_tuse[b])
                return S.dma(q, "scalar", lnt[b][:, :, :], lns[b][:, :], transpose=True)
            t_lnuse[b] = S.sec("scalar", strp)
            def sto(q, b=b, tt=tt, tk=t_lnuse[b]):
                S.wait(q, "sync", tk)
                return S.dma(q, "sync",
                             d["LNF"][:, :, tt * 128:(tt + 1) * 128].rearrange("k f t -> f k t"),
                             lnt[b][:, :, :])
            t_tuse[b] = S.sec("sync", sto)
        S.flush()


def stage23_consts(C):
    S, sb, sb23 = C.S, C.sb, C.sb23
    def io(q):
        q.iota(sb23["maskneg"][:, 0:256], pattern=[[1, 256]], base=0, channel_multiplier=0,
               allow_small_or_imprecise_dtypes=True)
        i2 = q.iota(sb23["maskneg"][:, 256:512], pattern=[[1, 256]], base=-128,
                    channel_multiplier=0, allow_small_or_imprecise_dtypes=True)
        t1 = S.done("gpsimd", i2)
        i3 = q.iota(sb23["IND"][:, :], pattern=[[0, KC * 128]], base=0, channel_multiplier=1,
                    allow_small_or_imprecise_dtypes=True)
        return t1, S.done("gpsimd", i3)
    t_io, t_indk = S.sec("gpsimd", io)

    def vm(q):
        S.wait(q, "vector", t_io)
        q.tensor_scalar(out=sb23["maskneg"][:, 0:256], in0=sb23["maskneg"][:, 0:256],
                        scalar1=sb["pcol"][:, 0:1], scalar2=NEGBIG,
                        op0=ALU.is_lt, op1=ALU.mult)
        inst = q.tensor_scalar(out=sb23["maskneg"][:, 256:512], in0=sb23["maskneg"][:, 256:512],
                               scalar1=sb["pcol"][:, 0:1], scalar2=NEGBIG,
                               op0=ALU.is_lt, op1=ALU.mult)
        t1 = S.done("vector", inst)
        S.wait(q, "vector", t_indk)
        inst = None
        for h in range(NH):
            inst = q.tensor_scalar(out=sb23["IND"][:, h * 128:(h + 1) * 128],
                                   in0=sb23["IND"][:, h * 128:(h + 1) * 128],
                                   scalar1=float(h), scalar2=None, op0=ALU.is_equal)
        t2 = S.done("vector", inst)
        q.memset(sb23["S_a"][:, :], 0.0)
        inst = q.memset(sb23["S_bf"][:, :], 0.0)
        t3 = S.done("vector", inst)
        return t1, t2, t3
    C.t_mask, C.t_ind, C.t_szero = S.sec("vector", vm)


def stage2(C, half):
    nc, S, sb, sb23, ps, d = C.nc, C.S, C.sb, C.sb23, C.ps, C.d
    ncc = 12
    ccbase = half * 12
    col0 = DM + half * 1536
    ncols = 1536 + (16 if half == 1 else 0)
    with contextlib.ExitStack() as es2:
        pf = f"s2h{half}_"
        WB = es2.enter_context(nc.sbuf_tensor(pf + "wb", [128, 16, ncols], BF16))
        LNG = es2.enter_context(nc.sbuf_tensor(pf + "lng", [128, 16, TGRP], BF16))
        HBCR = es2.enter_context(nc.sbuf_tensor(pf + "hbcr", [128, ncc, TGRP + 3], BF16))
        XBCg = es2.enter_context(nc.sbuf_tensor(pf + "xbc", [128, ncc, TGRP], BF16))
        CVA = es2.enter_context(nc.sbuf_tensor(pf + "cva", [128, 2, TGRP], BF16))
        CVB = es2.enter_context(nc.sbuf_tensor(pf + "cvb", [128, ncc, TGRP], BF16))
        DTRAW = es2.enter_context(nc.sbuf_tensor(pf + "dtraw", [16, TGRP], F32))
        PAN = es2.enter_context(nc.sbuf_tensor(pf + "pan", [16, 6, TGRP], F32))

        def ldw(q):
            return [S.dma(q, "gpsimd", WB[:, kc, :],
                          d["w_in"][kc * 128:(kc + 1) * 128, col0:col0 + ncols])
                    for kc in range(16)]
        t_w = S.sec("gpsimd", ldw)

        t_lng = None
        t_lng_use = None
        t_evac = {}
        t_conv = {}
        t_store = None
        t_silu = {}
        t_silu_slot = {}
        t_halo = None
        t_dtevac = None
        t_panT_evac = None
        t_mm_last = None
        t_pe2_prev = None
        t_panT_prev = None
        for g in range(NGRP):
            def ldl(q, g=g):
                S.wait(q, "sync", t_lng_use)
                return S.dma(q, "sync", LNG[:, :, :],
                             d["LNF"][:, :, g * TGRP:(g + 1) * TGRP].rearrange("k f t -> f k t"))
            t_lng = S.sec("sync", ldl)

            def halo(q, g=g):
                for cc in range(ncc):
                    S.wait(q, "vector", t_conv.get(cc))
                if g == 0:
                    S.wait(q, "vector", C.t_mask)
                    inst = q.tensor_copy(out=HBCR[:, :, 0:3],
                                         in_=sb["zb"].ap().rearrange("p (a b) -> p a b", b=3)[:, 0:ncc, :])
                else:
                    inst = q.tensor_copy(out=HBCR[:, :, 0:3], in_=HBCR[:, :, TGRP:TGRP + 3])
                return S.done("vector", inst)
            t_halo = S.sec("vector", halo)

            for cc in range(ncc):
                gcc = g * ncc + cc
                bank = ps[f"p{cc % 2}"]
                def mm(q, cc=cc, gcc=gcc, bank=bank, t_lng=t_lng):
                    S.wait(q, "tensor", t_w)
                    S.wait(q, "tensor", t_lng)
                    S.wait(q, "tensor", t_evac.get(gcc - 2))
                    inst = None
                    for kc in range(16):
                        inst = q.matmul(out=bank[:, :], lhsT=WB[:, kc, cc * 128:(cc + 1) * 128],
                                        rhs=LNG[:, kc, :], start=(kc == 0), stop=(kc == 15))
                    return S.done("tensor", inst)
                t_mm = S.sec("tensor", mm)
                t_mm_last = t_mm
                eng = "scalar" if cc % 2 == 0 else "vector"
                def ev(q, cc=cc, bank=bank, t_mm=t_mm, eng=eng):
                    S.wait(q, eng, t_mm)
                    S.wait(q, eng, t_halo)
                    S.wait(q, eng, t_conv.get(cc))
                    if eng == "scalar":
                        inst = q.copy(out=HBCR[:, cc, 3:TGRP + 3], in_=bank[:, :])
                    else:
                        inst = q.tensor_copy(out=HBCR[:, cc, 3:TGRP + 3], in_=bank[:, :])
                    return S.done(eng, inst)
                t_evac[gcc] = S.sec(eng, ev)
            t_lng_use = t_mm_last

            if half == 1:
                def mmdt(q, t_lng=t_lng):
                    S.wait(q, "tensor", t_dtevac)
                    inst = None
                    for kc in range(16):
                        inst = q.matmul(out=ps["p2"][0:16, :], lhsT=WB[:, kc, 1536:1552],
                                        rhs=LNG[:, kc, :], start=(kc == 0), stop=(kc == 15))
                    return S.done("tensor", inst)
                t_mmdt = S.sec("tensor", mmdt)
                t_lng_use = t_mmdt
                def evdt(q, t_mmdt=t_mmdt):
                    S.wait(q, "vector", t_mmdt)
                    inst = q.tensor_copy(out=DTRAW[:, :], in_=ps["p2"][0:16, :])
                    return S.done("vector", inst)
                t_dtevac = S.sec("vector", evdt)

            for cc in range(ncc):
                ccg = ccbase + cc
                eng = "vector"
                ei = cc % 2
                def taps(q, cc=cc, ccg=ccg, ei=ei):
                    S.wait(q, "vector", t_evac[cc])
                    S.wait(q, "vector", t_silu_slot.get(ei))
                    wcol = lambda k: sb["convw_col"][:, 4 * ccg + k:4 * ccg + k + 1]
                    q.tensor_scalar(out=CVA[:, ei, :], in0=HBCR[:, cc, 0:TGRP],
                                    scalar1=wcol(0), scalar2=None, op0=ALU.mult)
                    q.scalar_tensor_tensor(out=CVB[:, cc, :], in0=HBCR[:, cc, 1:TGRP + 1],
                                           scalar=wcol(1), in1=CVA[:, ei, :],
                                           op0=ALU.mult, op1=ALU.add)
                    q.scalar_tensor_tensor(out=CVA[:, ei, :], in0=HBCR[:, cc, 2:TGRP + 2],
                                           scalar=wcol(2), in1=CVB[:, cc, :],
                                           op0=ALU.mult, op1=ALU.add)
                    inst = q.scalar_tensor_tensor(out=CVB[:, cc, :], in0=HBCR[:, cc, 3:TGRP + 3],
                                                  scalar=wcol(3), in1=CVA[:, ei, :],
                                                  op0=ALU.mult, op1=ALU.add)
                    return S.done("vector", inst)
                t_conv[cc] = S.sec("vector", taps)
                def silu(q, cc=cc, ccg=ccg, g=g):
                    S.wait(q, "scalar", t_store)
                    S.wait(q, "scalar", t_conv[cc])
                    inst = q.activation(out=XBCg[:, cc, :], in_=CVB[:, cc, :], func=AF.Silu,
                                        bias=sb["convb_col"][:, ccg:ccg + 1])
                    return S.done("scalar", inst)
                t_silu[cc] = S.sec("scalar", silu)
            t_silu_slot[0] = t_silu[ncc - 1]
            t_silu_slot[1] = t_silu[ncc - 1]

            if half == 1:
                def pan(q, g=g):
                    S.wait(q, "vector", t_dtevac)
                    S.wait(q, "vector", t_pe2_prev)
                    inst = q.tensor_scalar(out=PAN[:, 0, :], in0=DTRAW[:, :],
                                           scalar1=sb["dtb_col"][:, 0:1], scalar2=None, op0=ALU.add)
                    return S.done("vector", inst)
                t_z = S.sec("vector", pan)
                def pe1(q, t_z=t_z):
                    S.wait(q, "scalar", t_z)
                    q.activation(out=PAN[:, 1, :], in_=PAN[:, 0, :], func=AF.Abs)
                    q.activation(out=PAN[:, 2, :], in_=PAN[:, 1, :], func=AF.Exp, scale=-1.0)
                    q.activation(out=PAN[:, 3, :], in_=PAN[:, 2, :], func=AF.Ln,
                                 bias=sb["cbias"][0:16, 1:2])
                    inst = q.activation(out=PAN[:, 1, :], in_=PAN[:, 0, :], func=AF.Relu)
                    return S.done("scalar", inst)
                t_sp = S.sec("scalar", pe1)
                def pv1(q, t_sp=t_sp, g=g):
                    S.wait(q, "vector", t_sp)
                    q.tensor_tensor(out=PAN[:, 0, :], in0=PAN[:, 1, :], in1=PAN[:, 3, :], op=ALU.add)
                    inst = q.tensor_scalar(out=PAN[:, 2, :], in0=PAN[:, 0, :],
                                           scalar1=sb["acol"][:, 0:1], scalar2=None, op0=ALU.mult)
                    t_dt = S.done("vector", inst)
                    inst = None
                    for c01 in range(2):
                        cg = g * 2 + c01
                        inst = q.tensor_tensor_scan(out=sb23["cum_all"][:, cg * 256:(cg + 1) * 256],
                                                    data0=sb["ones16"][:, :],
                                                    data1=PAN[:, 2, c01 * 256:(c01 + 1) * 256],
                                                    initial=0.0, op0=ALU.mult, op1=ALU.add)
                    return t_dt, S.done("vector", inst)
                t_dt, t_cum = S.sec("vector", pv1)
                def pe2(q, g=g, t_cum=t_cum):
                    S.wait(q, "scalar", t_cum)
                    S.wait(q, "scalar", t_panT_prev)
                    q.activation(out=PAN[:, 3, :], in_=PAN[:, 0, :], func=AF.Ln)
                    inst = None
                    for c01 in range(2):
                        cg = g * 2 + c01
                        csl = sb23["cum_all"][:, cg * 256:(cg + 1) * 256]
                        q.activation(out=PAN[:, 4, c01 * 256:(c01 + 1) * 256], in_=csl,
                                     func=AF.Exp, scale=-1.0,
                                     bias=sb23["cum_all"][:, cg * 256 + 255:cg * 256 + 256])
                        inst = q.activation(out=sb23["expcum_all"][:, cg * 256:(cg + 1) * 256],
                                            in_=csl, func=AF.Exp)
                    return S.done("scalar", inst)
                t_pe2 = S.sec("scalar", pe2)
                t_pe2_prev = t_pe2
                def pv2(q, g=g, t_pe2=t_pe2):
                    S.wait(q, "vector", t_pe2)
                    S.wait(q, "vector", t_panT_prev)
                    q.tensor_tensor(out=PAN[:, 5, :], in0=PAN[:, 3, :],
                                    in1=sb23["cum_all"][:, g * TGRP:(g + 1) * TGRP], op=ALU.subtract)
                    inst = q.tensor_tensor(out=PAN[:, 4, :], in0=PAN[:, 0, :], in1=PAN[:, 4, :],
                                           op=ALU.mult)
                    t_w_ = S.done("vector", inst)
                    inst = None
                    for c01 in range(2):
                        cg = g * 2 + c01
                        inst = q.tensor_scalar(out=PAN[:, 3, c01 * 256:(c01 + 1) * 256],
                                               in0=sb["ones16"][:, :],
                                               scalar1=sb23["expcum_all"][:, cg * 256 + 255:cg * 256 + 256],
                                               scalar2=None, op0=ALU.mult)
                    return t_w_, S.done("vector", inst)
                t_wp, t_cd = S.sec("vector", pv2)
                def ptr(q, g=g, t_wp=t_wp, t_cd=t_cd):
                    S.wait(q, "tensor", t_wp)
                    S.wait(q, "tensor", t_cd)
                    S.wait(q, "tensor", t_panT_evac)
                    inst = None
                    for c01 in range(2):
                        for jt in range(2):
                            for qi, pan_i in ((0, 5), (1, 4), (2, 3)):
                                j = (c01 * 2 + jt) * 3 + qi
                                inst = q.transpose(out=ps["p3"][0:128, j * 16:(j + 1) * 16],
                                                   in_=PAN[:, pan_i, (c01 * 2 + jt) * 128:(c01 * 2 + jt + 1) * 128],
                                                   identity=sb["I_f32"][0:16, 0:16])
                    return S.done("tensor", inst)
                t_panT = S.sec("tensor", ptr)
                t_panT_prev = t_panT
                def pev(q, g=g, t_panT=t_panT):
                    S.wait(q, "vector", t_panT)
                    inst = q.tensor_copy(out=sb23["colT"][:, g * 192:(g + 1) * 192],
                                         in_=ps["p3"][0:128, 0:192])
                    return S.done("vector", inst)
                t_panT_evac = S.sec("vector", pev)

            def sto(q, g=g):
                for cc in range(ncc):
                    S.wait(q, "sync", t_silu[cc])
                return S.dma(q, "sync",
                             d["XBC"][ccbase:ccbase + ncc, :, g * TGRP:(g + 1) * TGRP].rearrange("c f t -> f c t"),
                             XBCg[:, :, :])
            t_store = S.sec("sync", sto)
        S.flush()


def stage2g(C):
    nc, S, sb, ps, d = C.nc, C.S, C.sb, C.ps, C.d
    with contextlib.ExitStack() as es2:
        WG = es2.enter_context(nc.sbuf_tensor("s2g_wg", [128, 16, 2048], BF16))
        LNG = es2.enter_context(nc.sbuf_tensor("s2g_lng", [128, 16, TGRP], BF16))
        SGG = es2.enter_context(nc.sbuf_tensor("s2g_sgg", [128, 16, TGRP], BF16))
        def ldw(q):
            return [S.dma(q, "gpsimd", WG[:, kc, :],
                          d["w_in"][kc * 128:(kc + 1) * 128, 0:DM])
                    for kc in range(16)]
        t_w = S.sec("gpsimd", ldw)
        t_lng_use = None
        t_store = None
        t_ev = {}
        for g in range(NGRP):
            def ldl(q, g=g):
                S.wait(q, "sync", t_lng_use)
                return S.dma(q, "sync", LNG[:, :, :],
                             d["LNF"][:, :, g * TGRP:(g + 1) * TGRP].rearrange("k f t -> f k t"))
            t_lng = S.sec("sync", ldl)
            for oc in range(16):
                bank = ps[f"p{oc % 4}"]
                def mm(q, oc=oc, bank=bank, t_lng=t_lng):
                    S.wait(q, "tensor", t_w)
                    S.wait(q, "tensor", t_lng)
                    S.wait(q, "tensor", t_ev.get(oc - 4))
                    inst = None
                    for kc in range(16):
                        inst = q.matmul(out=bank[:, :], lhsT=WG[:, kc, oc * 128:(oc + 1) * 128],
                                        rhs=LNG[:, kc, :], start=(kc == 0), stop=(kc == 15))
                    return S.done("tensor", inst)
                t_mm = S.sec("tensor", mm)
                def ev(q, oc=oc, bank=bank, t_mm=t_mm):
                    S.wait(q, "scalar", t_mm)
                    S.wait(q, "scalar", t_store)
                    inst = q.activation(out=SGG[:, oc, :], in_=bank[:, :], func=AF.Silu)
                    return S.done("scalar", inst)
                t_ev[oc] = S.sec("scalar", ev)
            t_lng_use = t_ev[15]
            def sto(q, g=g):
                for oc in range(16):
                    S.wait(q, "sync", t_ev[oc])
                return S.dma(q, "sync",
                             d["SG"][:, :, g * TGRP:(g + 1) * TGRP].rearrange("k f t -> f k t"),
                             SGG[:, :, :])
            t_store = S.sec("sync", sto)
        S.flush()


def stage3(C):
    nc, S, sb, sb23, ps, d = C.nc, C.S, C.sb, C.sb23, C.ps, C.d
    with contextlib.ExitStack() as es3:
        XBCc = [es3.enter_context(nc.sbuf_tensor(f"s3_xbc{i}", [128, 24, QCH], BF16)) for i in range(2)]
        SGc = [es3.enter_context(nc.sbuf_tensor(f"s3_sg{i}", [128, 16, QCH], BF16)) for i in range(2)]
        XT = es3.enter_context(nc.sbuf_tensor("s3_xt", [128, 8, 128], BF16))
        BT = es3.enter_context(nc.sbuf_tensor("s3_bt", [128, 8, 128], BF16))
        BW = es3.enter_context(nc.sbuf_tensor("s3_bw", [128, 4, 2, 128], BF16))
        MTD = es3.enter_context(nc.sbuf_tensor("s3_mtd", [128, 4, 512], BF16))
        EAR = es3.enter_context(nc.sbuf_tensor("s3_ear", [128, 4, 512], F32))
        ESB = es3.enter_context(nc.sbuf_tensor("s3_esb", [128, 4, 512], F32))
        CS = es3.enter_context(nc.sbuf_tensor("s3_cs", [128, 4, QCH], BF16))
        Y = es3.enter_context(nc.sbuf_tensor("s3_y", [128, 4, QCH], F32))
        G16 = es3.enter_context(nc.sbuf_tensor("s3_g16", [128, 16, QCH], BF16))
        GSQ = es3.enter_context(nc.sbuf_tensor("s3_gsq", [128, 4, QCH], F32))
        GSO = [es3.enter_context(nc.sbuf_tensor(f"s3_gso{i}", [128, 16, QCH], BF16)) for i in range(2)]
        RSUMS = es3.enter_context(nc.sbuf_tensor("s3_rsums", [1, 2, 512], F32))
        RSTDS = es3.enter_context(nc.sbuf_tensor("s3_rstds", [1, 2, 512], F32))
        RTMP = es3.enter_context(nc.sbuf_tensor("s3_rtmp", [1, 512], F32))
        RB = es3.enter_context(nc.sbuf_tensor("s3_rb", [128, 2, 512], F32))

        t_ld = [None, None]
        t_ld_sg = [None, None]
        t_xbc_use = [None, None]
        t_sg_use = [None, None]
        t_gso_use = [None, None]
        t_mtd_done = {}
        t_cs_done = {}
        t_yev = {}
        t_sup = {}
        t_su = {}
        t_gso = {}
        t_e_prev = {}
        t_y_prev = {}
        t_sc_prev = {}
        t_g_prev = {}
        t_gsq_prev = {}
        t_gsq_tok = {}
        t_xt_ev = None
        t_bt_ev = None
        t_sc_last = {}
        pend_sbf = [None]
        pend_epi = [None]
        t_mm3 = {}
        t_evac = {}
        t_srs = {}
        t_dma1 = {}
        t_dma2 = {}

        def emit_scalar_merged(q, h, fl, sl, t_ear):
            S.wait(q, "scalar", t_ear)
            inst = None
            for jt in range(2):
                ci = ((fl % NH * 0 + (fl // NH) * 2 + jt) * 3 + 0) * 16 + h
                inst = q.activation(out=ESB[:, sl, jt * 256:(jt + 1) * 256],
                                    in_=EAR[:, sl, jt * 256:(jt + 1) * 256], func=AF.Exp,
                                    bias=sb23["colT"][:, ci:ci + 1])
            return S.tok("scalar_c")

        for c in range(NCH):
            b = c % 2
            def ldc(q, c=c, b=b):
                S.wait(q, "sync", t_xbc_use[b])
                tk1 = S.dma(q, "sync", XBCc[b][:, :, :],
                            d["XBC"][:, :, c * QCH:(c + 1) * QCH].rearrange("a f t -> f a t"))
                S.wait(q, "sync", t_sg_use[b])
                tk2 = S.dma(q, "sync", SGc[b][:, :, :],
                            d["SG"][:, :, c * QCH:(c + 1) * QCH].rearrange("a f t -> f a t"))
                return tk1, tk2
            t_ld[b], t_ld_sg[b] = S.sec("sync", ldc)

            def trx(q, c=c, b=b):
                S.wait(q, "tensor", t_ld[b])
                S.wait(q, "tensor", t_xt_ev)
                S.wait(q, "tensor", C.t_Ibf)
                inst = None
                for gx in range(4):
                    for jt in range(2):
                        k = gx * 2 + jt
                        inst = q.transpose(out=ps["pt"][:, k * 128:(k + 1) * 128],
                                           in_=XBCc[b][:, gx, jt * 128:(jt + 1) * 128],
                                           identity=sb["I_bf"][:, :])
                return S.done("tensor", inst)
            t_trx = S.sec("tensor", trx)
            def evx(q, t_trx=t_trx, c=c):
                S.wait(q, "vector", t_trx)
                S.wait(q, "vector", t_sc_last.get(c - 1))
                inst = q.tensor_copy(out=XT.ap().rearrange("p a b -> p (a b)"),
                                     in_=ps["pt"][:, 0:1024])
                return S.done("vector", inst)
            t_xt_evac = S.sec("vector", evx)
            def trb(q, c=c, b=b, t_xt_evac=t_xt_evac):
                S.wait(q, "tensor", t_xt_evac)
                inst = None
                for gx in range(4):
                    for jt in range(2):
                        k = gx * 2 + jt
                        inst = q.transpose(out=ps["pt"][:, k * 128:(k + 1) * 128],
                                           in_=XBCc[b][:, 4 + gx, jt * 128:(jt + 1) * 128],
                                           identity=sb["I_bf"][:, :])
                return S.done("tensor", inst)
            t_trb = S.sec("tensor", trb)
            def evb(q, t_trb=t_trb, c=c):
                S.wait(q, "vector", t_trb)
                inst = q.tensor_copy(out=BT.ap().rearrange("p a b -> p (a b)"),
                                     in_=ps["pt"][:, 0:1024])
                return S.done("vector", inst)
            t_bt_ev = S.sec("vector", evb)
            t_xt_ev = t_bt_ev

            if pend_epi[0] is not None:
                pend_epi[0]()
                pend_epi[0] = None

            Sold = sb23["S_a"] if c % 2 == 0 else sb23["S_b"]
            Snew = sb23["S_b"] if c % 2 == 0 else sb23["S_a"]

            def emit_vgs(q, kv, b):
                S.wait(q, "vector", t_dma2[kv])
                S.wait(q, "vector", t_gso_use[b])
                toks = []
                for hh in range(2):
                    flv = kv * 2 + hh
                    hv = flv % NH
                    inst = q.scalar_tensor_tensor(out=GSO[b][:, hv, :], in0=G16[:, hv, :],
                                                  scalar=sb["normw_fm"][:, hv:hv + 1],
                                                  in1=RB[:, kv % 2, hh * 256:(hh + 1) * 256],
                                                  op0=ALU.mult, op1=ALU.mult)
                    toks.append(S.done("vector", inst))
                return toks

            def emit_rb_dma(kk):
                def dd1(q, kk=kk):
                    S.wait(q, "sync", t_srs[kk])
                    S.wait(q, "sync", t_dma2.get(kk - 2))
                    return S.dma(q, "sync", d["RSTDD"][kk % 2:kk % 2 + 1, :],
                                 RSTDS[0:1, kk % 2, :])
                t_dma1[kk] = S.sec("sync", dd1)
                def dd2(q, kk=kk):
                    S.wait(q, "sync", t_dma1[kk])
                    S.wait(q, "sync", t_gso.get(2 * (kk - 2) + 1))
                    return S.dma(q, "sync", RB[:, kk % 2, :],
                                 d["RSTDD"][kk % 2:kk % 2 + 1, :].partition_broadcast(128).squeeze(1))
                t_dma2[kk] = S.sec("sync", dd2)

            tok_mt = {}
            tok_cb = {}
            tok_ear = {}

            def e_mm1(h, c=c, b=b):
                fl = c * NH + h
                g4 = h // 4
                mt_bank = ps[f"p{h % 2}"]
                cb_bank = ps[f"p{2 + h % 2}"]
                def mm1(q, h=h, g4=g4, mt_bank=mt_bank, cb_bank=cb_bank, c=c, b=b, fl=fl):
                    S.wait(q, "tensor", t_ld[b])
                    S.wait(q, "tensor", t_mtd_done.get(fl - 2))
                    inst = None
                    for jt in range(2):
                        inst = q.matmul(out=mt_bank[:, jt * 256:(jt + 1) * 256],
                                        lhsT=XBCc[b][:, 4 + g4, jt * 128:(jt + 1) * 128],
                                        rhs=XBCc[b][:, 8 + h, :], start=True, stop=True)
                    tmt = S.done("tensor", inst)
                    S.wait(q, "tensor", t_cs_done.get(fl - 2))
                    q.matmul(out=cb_bank[:, 0:256], lhsT=sb23["IND"][:, h * 128:(h + 1) * 128],
                             rhs=sb23["cum_all"][:, c * 256:(c + 1) * 256], start=True, stop=True)
                    inst = q.matmul(out=cb_bank[:, 256:512], lhsT=sb23["IND"][:, h * 128:(h + 1) * 128],
                                    rhs=sb23["expcum_all"][:, c * 256:(c + 1) * 256], start=True, stop=True)
                    return tmt, S.done("tensor", inst)
                tok_mt[fl], tok_cb[fl] = S.sec("tensor", mm1)

            def e_vear(h, c=c):
                fl = c * NH + h
                sl = fl % 4
                cb_bank = ps[f"p{2 + h % 2}"]
                def vear(q, cb_bank=cb_bank, fl=fl, sl=sl):
                    S.wait(q, "vector", tok_cb[fl])
                    S.wait(q, "vector", t_e_prev.get(fl - 4))
                    q.tensor_tensor(out=EAR[:, sl, 0:256], in0=cb_bank[:, 0:256],
                                    in1=sb23["maskneg"][:, 0:256], op=ALU.add)
                    inst = q.tensor_tensor(out=EAR[:, sl, 256:512], in0=cb_bank[:, 0:256],
                                           in1=sb23["maskneg"][:, 256:512], op=ALU.add)
                    return S.done("vector", inst)
                tok_ear[fl] = S.sec("vector", vear)

            def e_sc(h, c=c):
                fl = c * NH + h
                sl = fl % 4
                t_e = S.sec("scalar", lambda q, h=h, fl=fl, sl=sl:
                            emit_scalar_merged(q, h, fl, sl, tok_ear[fl]))
                t_e_prev[fl] = t_e

            def e_vmid(h, c=c, b=b):
                fl = c * NH + h
                sl = fl % 4
                g4 = h // 4
                mt_bank = ps[f"p{h % 2}"]
                cb_bank = ps[f"p{2 + h % 2}"]
                def cidx(qi, jt, c=c, h=h):
                    return ((c * 2 + jt) * 3 + qi) * 16 + h
                def vmid(q, mt_bank=mt_bank, cb_bank=cb_bank,
                         h=h, g4=g4, b=b, fl=fl, sl=sl, cidx=cidx):
                    S.wait(q, "vector", t_e_prev[fl])
                    S.wait(q, "vector", tok_mt[fl])
                    S.wait(q, "vector", t_sc_prev.get(fl - 4))
                    inst = None
                    for jt in range(2):
                        inst = q.tensor_tensor(out=MTD[:, sl, jt * 256:(jt + 1) * 256],
                                               in0=mt_bank[:, jt * 256:(jt + 1) * 256],
                                               in1=ESB[:, sl, jt * 256:(jt + 1) * 256], op=ALU.mult)
                    t1 = S.done("vector", inst)
                    inst = q.tensor_tensor(out=CS[:, sl, :], in0=XBCc[b][:, 8 + h, :],
                                           in1=cb_bank[:, 256:512], op=ALU.mult)
                    t2 = S.done("vector", inst)
                    for jt in range(2):
                        inst = q.tensor_scalar(out=BW[:, sl, jt, :], in0=BT[:, g4 * 2 + jt, :],
                                               scalar1=sb23["colT"][:, cidx(1, jt):cidx(1, jt) + 1],
                                               scalar2=None, op0=ALU.mult)
                    return t1, t2, S.done("vector", inst)
                t_mtd, t_cs, t_bw = S.sec("vector", vmid)
                t_mtd_done[fl] = t_mtd
                t_cs_done[fl] = t_cs
                return t_bw

            def e_mm2(h, t_bw, c=c):
                fl = c * NH + h
                sl = fl % 4
                g4 = h // 4
                y_bank = ps[f"p{4 + h % 2}"]
                def mm2(q, h=h, g4=g4, y_bank=y_bank, t_bw=t_bw, c=c, fl=fl, sl=sl):
                    S.wait(q, "tensor", t_cs_done[fl])
                    S.wait(q, "tensor", t_yev.get(fl - 2))
                    S.wait(q, "tensor", t_sup.get(fl - NH) if c > 0 else C.t_szero)
                    q.matmul(out=y_bank[:, 0:256], lhsT=sb23["S_bf"][:, h * 128:(h + 1) * 128],
                             rhs=CS[:, sl, :], start=True, stop=False)
                    q.matmul(out=y_bank[:, 0:256], lhsT=XT[:, g4 * 2 + 0, :],
                             rhs=MTD[:, sl, 0:256], start=False, stop=False)
                    inst = q.matmul(out=y_bank[:, 0:256], lhsT=XT[:, g4 * 2 + 1, :],
                                    rhs=MTD[:, sl, 256:512], start=False, stop=True)
                    ty = S.done("tensor", inst)
                    S.wait(q, "tensor", t_bw)
                    q.matmul(out=y_bank[:, 256:384], lhsT=BW[:, sl, 0, :],
                             rhs=XT[:, g4 * 2 + 0, :], start=True, stop=False)
                    inst = q.matmul(out=y_bank[:, 256:384], lhsT=BW[:, sl, 1, :],
                                    rhs=XT[:, g4 * 2 + 1, :], start=False, stop=True)
                    return ty, S.done("tensor", inst)
                t_y, t_sc = S.sec("tensor", mm2)
                t_y_prev[fl] = t_y
                t_sc_prev[fl] = t_sc
                if h == NH - 1:
                    t_sc_last[c] = t_sc

            def e_vtail(h, c=c, b=b, Sold=Sold, Snew=Snew):
                fl = c * NH + h
                sl = fl % 4
                g4 = h // 4
                y_bank = ps[f"p{4 + h % 2}"]
                def cidx(qi, jt, c=c, h=h):
                    return ((c * 2 + jt) * 3 + qi) * 16 + h
                def vtail(q, h=h, g4=g4, y_bank=y_bank, b=b, fl=fl, sl=sl,
                          Sold=Sold, Snew=Snew, cidx=cidx):
                    S.wait(q, "vector", t_sc_prev[fl])
                    q.scalar_tensor_tensor(out=Y[:, sl, :], in0=XBCc[b][:, g4, :],
                                           scalar=sb["D_bc"][:, h:h + 1],
                                           in1=y_bank[:, 0:256], op0=ALU.mult, op1=ALU.add)
                    q.scalar_tensor_tensor(out=Snew[:, h * 128:(h + 1) * 128],
                                           in0=Sold[:, h * 128:(h + 1) * 128],
                                           scalar=sb23["colT"][:, cidx(2, 0):cidx(2, 0) + 1],
                                           in1=y_bank[:, 256:384], op0=ALU.mult, op1=ALU.add)
                    tsu = S.done("vector", None)
                    S.wait(q, "vector", t_ld_sg[b])
                    S.wait(q, "vector", t_gsq_tok.get(fl - NH))
                    inst = q.tensor_tensor(out=G16[:, h, :], in0=Y[:, sl, :],
                                           in1=SGc[b][:, h, :], op=ALU.mult)
                    tg = S.done("vector", inst)
                    kv = fl // 2 - 3
                    toks3 = emit_vgs(q, kv, b) if (h % 2 == 0 and h >= 6) else None
                    return tsu, tg, toks3
                t_suv, t_g, toks_vgs = S.sec("vector", vtail)
                t_su[fl] = t_suv
                t_yev[fl] = t_suv
                t_g_prev[fl] = t_g
                if toks_vgs is not None:
                    kv = fl // 2 - 3
                    t_gso[kv * 2], t_gso[kv * 2 + 1] = toks_vgs
                pend_sbf[0] = (fl, Snew)

            def e_stail(h, c=c):
                fl = c * NH + h
                sl = fl % 4
                k2 = fl // 2
                def stail(q, fl=fl, sl=sl, h=h, k2=k2):
                    S.wait(q, "scalar", t_g_prev[fl])
                    if pend_sbf[0] is not None:
                        pfl, Snew_ = pend_sbf[0]
                        ph = pfl % NH
                        q.copy(out=sb23["S_bf"][:, ph * 128:(ph + 1) * 128],
                               in_=Snew_[:, ph * 128:(ph + 1) * 128])
                        t_sup[pfl] = S.tok("scalar_c")
                        pend_sbf[0] = None
                    S.wait(q, "scalar", t_mm3.get(k2 - 2))
                    q.activation(out=GSQ[:, sl, :], in_=G16[:, h, :], func=AF.Square)
                    tq = S.tok("scalar_c")
                    te = ts = None
                    if h % 2 == 0 and h >= 4:
                        kk = k2 - 2
                        S.wait(q, "scalar", t_mm3[kk])
                        q.copy(out=RSUMS[0:1, kk % 2, :], in_=ps["p6"][0:1, 0:512])
                        te = S.tok("scalar_c")
                        S.wait(q, "scalar", t_dma1.get(kk - 2))
                        q.activation(out=RTMP[0:1, :], in_=RSUMS[0:1, kk % 2, :],
                                     func=AF.Ln, scale=1.0 / 128.0, bias=sb["cbias"][0:1, 0:1])
                        q.activation(out=RSTDS[0:1, kk % 2, :], in_=RTMP[0:1, :],
                                     func=AF.Exp, scale=-0.5)
                        ts = S.tok("scalar_c")
                    return tq, te, ts
                t_gsq, te_, ts_ = S.sec("scalar", stail)
                t_gsq_tok[fl] = t_gsq
                t_gsq_prev[fl] = t_gsq
                if te_ is not None:
                    t_evac[k2 - 2] = te_
                    t_srs[k2 - 2] = ts_
                    emit_rb_dma(k2 - 2)

            def e_mm3(j, c=c):
                def mm3(q, j=j):
                    S.wait(q, "tensor", t_gsq_tok[2 * j + 1])
                    S.wait(q, "tensor", t_evac.get(j - 1))
                    sg = (2 * j) % 4
                    inst = q.matmul(out=ps["p6"][0:1, 0:512], lhsT=sb["onescol"][:, :],
                                    rhs=GSQ[:, sg:sg + 2, :].rearrange("p a b -> p (a b)"),
                                    start=True, stop=True)
                    return S.done("tensor", inst)
                t_mm3[j] = S.sec("tensor", mm3)

            def e_evac_srs(kk):
                def sc_t(q, kk=kk):
                    S.wait(q, "scalar", t_mm3[kk])
                    q.copy(out=RSUMS[0:1, kk % 2, :], in_=ps["p6"][0:1, 0:512])
                    te = S.tok("scalar_c")
                    S.wait(q, "scalar", t_dma1.get(kk - 2))
                    q.activation(out=RTMP[0:1, :], in_=RSUMS[0:1, kk % 2, :],
                                 func=AF.Ln, scale=1.0 / 128.0, bias=sb["cbias"][0:1, 0:1])
                    q.activation(out=RSTDS[0:1, kk % 2, :], in_=RTMP[0:1, :],
                                 func=AF.Exp, scale=-0.5)
                    return te, S.tok("scalar_c")
                t_evac[kk], t_srs[kk] = S.sec("scalar", sc_t)

            e_mm1(0)
            e_vear(0)
            e_sc(0)
            e_mm1(1)
            e_vear(1)
            e_sc(1)
            vmid_bw = {0: e_vmid(0)}
            for h in range(NH):
                fl = c * NH + h
                k2 = fl // 2
                if h + 2 < NH:
                    e_mm1(h + 2)
                if h % 2 == 1 and h >= 3:
                    e_mm3(k2 - 1)
                e_mm2(h, vmid_bw.pop(h))
                if h + 2 < NH:
                    e_vear(h + 2)
                if h + 1 < NH:
                    vmid_bw[h + 1] = e_vmid(h + 1)
                e_vtail(h)
                if h + 2 < NH:
                    e_sc(h + 2)
                e_stail(h)

            def epilogue(c=c, b=b, emit_vgs=emit_vgs, emit_rb_dma=emit_rb_dma,
                         e_evac_srs=e_evac_srs, e_mm3=e_mm3):
                k5, k6, k7 = c * 8 + 5, c * 8 + 6, c * 8 + 7
                e_evac_srs(k6)
                emit_rb_dma(k6)
                e_mm3(k7)
                e_evac_srs(k7)
                def v_t1(q, b=b, k5=k5):
                    t_gso[k5 * 2], t_gso[k5 * 2 + 1] = emit_vgs(q, k5, b)
                S.sec("vector", v_t1)
                emit_rb_dma(k7)
                def v_t2(q, b=b, k6=k6, k7=k7):
                    for kv in (k6, k7):
                        t_gso[kv * 2], t_gso[kv * 2 + 1] = emit_vgs(q, kv, b)
                S.sec("vector", v_t2)

                t_xbc_use[b] = t_yev[c * NH + NH - 1]
                t_sg_use[b] = t_g_prev[c * NH + NH - 1]
                def sto(q, c=c, b=b):
                    for kv in range(c * 8, c * 8 + 8):
                        S.wait(q, "sync", t_gso[kv * 2 + 1])
                    return S.dma(q, "sync",
                                 d["GS"][:, :, c * QCH:(c + 1) * QCH].rearrange("k f t -> f k t"),
                                 GSO[b][:, :, :])
                t_store = S.sec("sync", sto)
                t_gso_use[b] = t_store
            pend_epi[0] = epilogue
        pend_epi[0]()
        pend_epi[0] = None
        pend_sbf[0] = None
        S.flush()


def stage4(C):
    nc, S, sb, ps, d = C.nc, C.S, C.sb, C.ps, C.d
    with contextlib.ExitStack() as es4:
        WO = es4.enter_context(nc.sbuf_tensor("s4_wo", [128, 16, 2048], BF16))
        GSG = [es4.enter_context(nc.sbuf_tensor(f"s4_gs{i}", [128, 16, TGRP], BF16)) for i in range(2)]
        ATT = es4.enter_context(nc.sbuf_tensor("s4_att", [128, 2, 4, 16, 128], BF16))
        XTI = [es4.enter_context(nc.sbuf_tensor(f"s4_x{i}", [128, 2048], F32)) for i in range(2)]
        ATTT = [es4.enter_context(nc.sbuf_tensor(f"s4_at{i}", [128, 16, 128], BF16)) for i in range(2)]
        X1O = [es4.enter_context(nc.sbuf_tensor(f"s4_o{i}", [128, 2048], F32)) for i in range(2)]
        def ldw(q):
            return [S.dma(q, "gpsimd", WO[:, kc, :], d["w_out"][kc * 128:(kc + 1) * 128, :])
                    for kc in range(16)]
        t_w = S.sec("gpsimd", ldw)
        t_gs_use = [None, None]
        t_ev = {}
        t_tr_last_p = [None, None]
        t_xti_use = [None, None]
        t_attt_use = [None, None]
        t_x1o_use = [None, None]
        for g in range(NGRP):
            b = g % 2
            def ldg(q, g=g, b=b):
                S.wait(q, "gpsimd", t_gs_use[b])
                return S.dma(q, "gpsimd", GSG[b][:, :, :],
                             d["GS"][:, :, g * TGRP:(g + 1) * TGRP].rearrange("k f t -> f k t"))
            t_gs = S.sec("gpsimd", ldg)
            for oc in range(16):
                goc = g * 16 + oc
                bank = ps[f"p{oc % 4}"]
                def mm(q, oc=oc, goc=goc, bank=bank, t_gs=t_gs, b=b):
                    S.wait(q, "tensor", t_w)
                    S.wait(q, "tensor", t_gs)
                    S.wait(q, "tensor", t_ev.get(goc - 4))
                    inst = None
                    for kc in range(16):
                        inst = q.matmul(out=bank[:, :], lhsT=WO[:, kc, oc * 128:(oc + 1) * 128],
                                        rhs=GSG[b][:, kc, :], start=(kc == 0), stop=(kc == 15))
                    return S.done("tensor", inst)
                t_mm = S.sec("tensor", mm)
                def ev(q, oc=oc, bank=bank, t_mm=t_mm, b=b):
                    S.wait(q, "scalar", t_mm)
                    S.wait(q, "scalar", t_tr_last_p[b])
                    inst = q.copy(out=ATT[:, b, :, oc, :], in_=bank[:, :].rearrange("p (a b) -> p a b", a=4))
                    return S.done("scalar", inst)
                t_ev[goc] = S.sec("scalar", ev)
            t_gs_use[b] = t_ev[g * 16 + 15]
            for tt in range(4):
                gtt = g * 4 + tt
                b2 = tt % 2
                def ldx(q, gtt=gtt, b2=b2):
                    S.wait(q, "gpsimd", t_xti_use[b2])
                    return S.dma(q, "gpsimd", XTI[b2][:, :], d["x"][gtt * 128:(gtt + 1) * 128, :])
                t_x = S.sec("gpsimd", ldx)
                def trp(q, tt=tt, b2=b2, b=b, g=g):
                    for oc in range(16):
                        S.wait(q, "scalar", t_ev[g * 16 + oc])
                    S.wait(q, "scalar", t_attt_use[b2])
                    return S.dma(q, "scalar", ATTT[b2][:, :, :], ATT[:, b, tt, :, :],
                                 transpose=True)
                t_tr = S.sec("scalar", trp)
                if tt == 3:
                    t_tr_last_p[b] = t_tr
                def vadd(q, b2=b2, t_x=t_x, t_tr=t_tr):
                    S.wait(q, "vector", t_x)
                    S.wait(q, "vector", t_tr)
                    S.wait(q, "vector", t_x1o_use[b2])
                    inst = q.tensor_tensor(out=X1O[b2][:, :], in0=XTI[b2][:, :],
                                           in1=ATTT[b2].ap().rearrange("p a b -> p (a b)"),
                                           op=ALU.add)
                    return S.done("vector", inst)
                t_add = S.sec("vector", vadd)
                t_xti_use[b2] = t_add
                t_attt_use[b2] = t_add
                def sto(q, gtt=gtt, b2=b2, t_add=t_add):
                    S.wait(q, "sync", t_add)
                    return S.dma(q, "sync", d["X1"][gtt * 128:(gtt + 1) * 128, :], X1O[b2][:, :])
                t_x1o_use[b2] = S.sec("sync", sto)
        S.flush() if (g == NGRP - 1 and tt == 3) else None


def stage56(C):
    nc, S, sb, ps, d = C.nc, C.S, C.sb, C.ps, C.d
    LT = [64.0, 32.0, 16.0, 8.0, 4.0, 2.0, 1.0]
    with contextlib.ExitStack() as es5:
        WQ = es5.enter_context(nc.sbuf_tensor("s5_wq", [128, 16, 2048], BF16))
        XTI = [es5.enter_context(nc.sbuf_tensor(f"s5_x{i}", [128, 2048], F32)) for i in range(2)]
        XTI2 = es5.enter_context(nc.sbuf_tensor("s5_x2", [128, 2048], F32))
        UU = es5.enter_context(nc.sbuf_tensor("s5_uu", [128, 2048], F32))
        UA = es5.enter_context(nc.sbuf_tensor("s5_ua", [128, 2048], F32))
        UB = es5.enter_context(nc.sbuf_tensor("s5_ub", [128, 2048], F32))
        MB = es5.enter_context(nc.sbuf_tensor("s5_mb", [128, 2048], F32))
        SGN = es5.enter_context(nc.sbuf_tensor("s5_sgn", [128, 2048], BF16))
        XQT = es5.enter_context(nc.sbuf_tensor("s5_xqt", [128, 2048], BF16))
        XQF = es5.enter_context(nc.sbuf_tensor("s5_xqf", [128, 2, 4, 16, 128], BF16))
        FF = es5.enter_context(nc.sbuf_tensor("s5_ff", [128, 4, 16, 128], BF16))
        FFT = [es5.enter_context(nc.sbuf_tensor(f"s5_ft{i}", [128, 16, 128], BF16)) for i in range(4)]
        OUTB = es5.enter_context(nc.sbuf_tensor("s5_ob", [128, 2048], F32))
        SCC = es5.enter_context(nc.sbuf_tensor("s5_scc", [128, 16], F32))
        SM = es5.enter_context(nc.sbuf_tensor("s5_sm", [128, 28], F32))
        BNS = es5.enter_context(nc.sbuf_tensor("s5_bns", [128, 4, 6], F32))

        t_wq = S.sec("scalar", lambda q: S.dma(q, "scalar", WQ[:, :, :],
                     d["WQT"].rearrange("ic i oc o -> i ic (oc o)")))

        t_x_use = [None, None]
        t_ffev = {}
        t_lad = None
        t_lad2 = None
        t_qtr = None
        t_xqf_use = [None, None]
        t_ff_use = None
        t_ftr_use = [None, None, None, None]
        t_out_use = None
        t_sgn_use = None
        t_ft_last = None
        t_lad_v = None
        def emit_out_block(gp, ffev_snap):
            def ftr(q, gp=gp, ffev_snap=ffev_snap):
                for oc in range(16):
                    S.wait(q, "scalar", ffev_snap[oc])
                toks = []
                for tt in range(4):
                    S.wait(q, "scalar", t_ftr_use[tt])
                    toks.append(S.dma(q, "scalar", FFT[tt][:, :, :], FF[:, tt, :, :],
                                      transpose=True))
                return toks
            t_fts = S.sec("scalar", ftr)
            st["t_ft_last"] = t_fts[3]
            for tt in range(4):
                gtt = gp * 4 + tt
                b2 = tt % 2
                def ldx2(q, gtt=gtt):
                    S.wait(q, "sync", st["t_x2_use"])
                    return S.dma(q, "sync", XTI2[:, :], d["X1"][gtt * 128:(gtt + 1) * 128, :])
                t_x2 = S.sec("sync", ldx2)
                def vout(q, gtt=gtt, tt=tt, t_x2=t_x2, t_ft=t_fts[tt]):
                    S.wait(q, "vector", t_x2)
                    S.wait(q, "vector", t_ft)
                    S.wait(q, "vector", st["t_out_use"])
                    inst = q.scalar_tensor_tensor(out=OUTB[:, :],
                                                  in0=FFT[tt].ap().rearrange("p a b -> p (a b)"),
                                                  scalar=SCC[:, gtt % 16:gtt % 16 + 1],
                                                  in1=XTI2[:, :], op0=ALU.mult, op1=ALU.add)
                    return S.done("vector", inst)
                t_o = S.sec("vector", vout)
                st["t_x2_use"] = t_o
                t_ftr_use[tt] = t_o
                def sto(q, gtt=gtt, t_o=t_o):
                    S.wait(q, "sync", t_o)
                    return S.dma(q, "sync", d["out"][gtt * 128:(gtt + 1) * 128, :], OUTB[:, :])
                st["t_out_use"] = S.sec("sync", sto)

        st = {"t_ft_last": None, "t_out_use": None, "t_x2_use": None}
        ffev_snaps = {}
        qtr_snaps = {}
        for g in range(NGRP):
            t_qtr_tiles = []
            for tt in range(4):
                gtt = g * 4 + tt
                b = tt % 2
                c0 = 12 * b
                def ldx(q, gtt=gtt, b=b):
                    S.wait(q, "gpsimd", t_x_use[b])
                    return S.dma(q, "gpsimd", XTI[b][:, :], d["X1"][gtt * 128:(gtt + 1) * 128, :])
                t_x = S.sec("gpsimd", ldx)
                def vstats(q, b=b, c0=c0, t_x=t_x):
                    S.wait(q, "vector", t_x)
                    for j in range(4):
                        q.bn_stats(out=BNS[:, j, :], in_=XTI[b][:, j * 512:(j + 1) * 512])
                    q.bn_aggr(out=SM[:, c0 + 2:c0 + 4],
                              in_=BNS.ap().rearrange("p a b -> p (a b)"))
                    q.tensor_reduce(out=SM[:, c0 + 7:c0 + 8], in_=XTI[b][:, :], axis=AX.X, op=ALU.max)
                    q.tensor_reduce(out=SM[:, c0 + 8:c0 + 9], in_=XTI[b][:, :], axis=AX.X, op=ALU.min)
                    inst = q.tensor_scalar(out=SM[:, c0 + 12:c0 + 13], in0=SM[:, c0 + 2:c0 + 3],
                                           scalar1=-1.0, scalar2=None, op0=ALU.mult)
                    return S.done("vector", inst)
                t_var = S.sec("vector", vstats)
                def srstd(q, c0=c0, b=b, t_var=t_var):
                    S.wait(q, "scalar", t_var)
                    q.activation(out=SM[:, c0 + 5:c0 + 6], in_=SM[:, c0 + 3:c0 + 4], func=AF.Ln,
                                 bias=sb["cbias"][:, 0:1])
                    q.activation(out=SM[:, c0 + 6:c0 + 7], in_=SM[:, c0 + 5:c0 + 6],
                                 func=AF.Exp, scale=-0.5)
                    t_re = S.tok("scalar_c")
                    S.wait(q, "scalar", t_sgn_use)
                    q.activation(out=SGN[:, :], in_=XTI[b][:, :], func=AF.Sign,
                                 bias=SM[:, c0 + 12:c0 + 13])
                    q.activation(out=UA[:, :], in_=XTI[b][:, :], func=AF.Abs,
                                 bias=SM[:, c0 + 12:c0 + 13])
                    return t_re, S.tok("scalar_c")
                t_re, t_rstd = S.sec("scalar", srstd)
                t_x_use[b] = t_rstd
                def vsc(q, c0=c0, gtt=gtt, t_re=t_re):
                    S.wait(q, "vector", t_re)
                    q.tensor_tensor(out=SM[:, c0 + 9:c0 + 10], in0=SM[:, c0 + 7:c0 + 8],
                                    in1=SM[:, c0 + 2:c0 + 3], op=ALU.subtract)
                    q.tensor_tensor(out=SM[:, c0 + 10:c0 + 11], in0=SM[:, c0 + 2:c0 + 3],
                                    in1=SM[:, c0 + 8:c0 + 9], op=ALU.subtract)
                    q.tensor_tensor(out=SM[:, c0 + 9:c0 + 10], in0=SM[:, c0 + 9:c0 + 10],
                                    in1=SM[:, c0 + 10:c0 + 11], op=ALU.max)
                    q.tensor_tensor(out=SM[:, c0 + 9:c0 + 10], in0=SM[:, c0 + 9:c0 + 10],
                                    in1=SM[:, c0 + 6:c0 + 7], op=ALU.mult)
                    q.tensor_single_scalar(out=SM[:, c0 + 9:c0 + 10], in_=SM[:, c0 + 9:c0 + 10],
                                           scalar=1e-8, op=ALU.max)
                    q.tensor_scalar(out=SCC[:, gtt % 16:gtt % 16 + 1], in0=SM[:, c0 + 9:c0 + 10],
                                    scalar1=sb["smalls"][:, 35:36], scalar2=1.0 / 127.0,
                                    op0=ALU.mult, op1=ALU.mult)
                    q.reciprocal(out=SM[:, c0 + 10:c0 + 11], in_=SM[:, c0 + 9:c0 + 10])
                    q.tensor_scalar(out=SM[:, c0 + 10:c0 + 11], in0=SM[:, c0 + 10:c0 + 11],
                                    scalar1=127.0, scalar2=None, op0=ALU.mult)
                    inst = q.tensor_tensor(out=SM[:, c0 + 11:c0 + 12], in0=SM[:, c0 + 6:c0 + 7],
                                           in1=SM[:, c0 + 10:c0 + 11], op=ALU.mult)
                    return S.done("vector", inst)
                t_s3 = S.sec("vector", vsc)
                def vladprep(q, b=b, c0=c0, t_s3=t_s3, t_rstd=t_rstd):
                    S.wait(q, "vector", t_s3)
                    S.wait(q, "vector", t_rstd)
                    S.wait(q, "vector", t_lad)
                    S.wait(q, "vector", t_lad2)
                    S.wait(q, "vector", t_qtr)
                    inst = q.tensor_scalar(out=UU[:, :], in0=UA[:, :],
                                           scalar1=SM[:, c0 + 11:c0 + 12], scalar2=None,
                                           op0=ALU.mult)
                    return S.done("vector", inst)
                t_prep = S.sec("vector", vladprep)
                SPL = 2048
                def ladv(q, t_prep=t_prep):
                    lo, hi, w = 0, SPL, SPL
                    S.wait(q, "vector", t_prep)
                    cur = UU
                    bufs = [UA, UB]
                    for li, T in enumerate(LT):
                        q.tensor_scalar(out=MB[:, lo:hi], in0=cur[:, lo:hi],
                                        scalar1=T - 0.5, scalar2=-T, op0=ALU.is_ge, op1=ALU.mult)
                        nxt = bufs[li % 2]
                        q.tensor_tensor(out=nxt[:, lo:hi], in0=cur[:, lo:hi],
                                        in1=MB[:, lo:hi], op=ALU.add)
                        cur = nxt
                    q.tensor_tensor(out=MB[:, lo:hi], in0=UU[:, lo:hi], in1=cur[:, lo:hi],
                                    op=ALU.subtract)
                    inst = q.tensor_tensor(out=XQT[:, lo:hi], in0=MB[:, lo:hi],
                                           in1=SGN[:, lo:hi], op=ALU.mult)
                    return S.done("vector", inst)
                t_lad = S.sec("vector", ladv)
                t_lad2 = t_lad
                t_lad_v = t_lad
                t_sgn_use = t_lad
                def qtr(q, tt=tt, g=g, t_lad=t_lad, t_lad2=t_lad2):
                    S.wait(q, "scalar", t_lad)
                    S.wait(q, "scalar", t_lad2)
                    S.wait(q, "scalar", t_xqf_use[g % 2])
                    return S.dma(q, "scalar", XQF[:, g % 2, tt, :, :], XQT[:, :],
                                 transpose=True)
                t_qtr = S.sec("scalar", qtr)
                t_qtr_tiles.append(t_qtr)
            qtr_snaps[g] = tuple(t_qtr_tiles)
            if g >= 2:
                emit_out_block(g - 2, ffev_snaps[g - 2])
            if g < 1:
                continue
            gm = g - 1
            for oc in range(16):
                goc = gm * 16 + oc
                bank = ps[f"p{oc % 2}"]
                def mm(q, oc=oc, goc=goc, bank=bank, g=gm, toks=qtr_snaps[gm]):
                    S.wait(q, "tensor", t_wq)
                    for tk in toks:
                        S.wait(q, "tensor", tk)
                    S.wait(q, "tensor", t_ffev.get(goc - 2))
                    inst = None
                    for ic in range(16):
                        inst = q.matmul(out=bank[:, :], lhsT=WQ[:, ic, oc * 128:(oc + 1) * 128],
                                        rhs=XQF[:, g % 2, :, ic, :], start=(ic == 0), stop=(ic == 15))
                    return S.done("tensor", inst)
                t_mm = S.sec("tensor", mm)
                def ev(q, oc=oc, bank=bank, t_mm=t_mm):
                    S.wait(q, "scalar", t_mm)
                    S.wait(q, "scalar", st["t_ft_last"])
                    inst = q.copy(out=FF[:, :, oc, :], in_=bank[:, :].rearrange("p (a b) -> p a b", a=4))
                    return S.done("scalar", inst)
                t_ffev[goc] = S.sec("scalar", ev)
            t_xqf_use[gm % 2] = t_ffev[goc]
            ffev_snaps[gm] = {oc: t_ffev[gm * 16 + oc] for oc in range(16)}
        emit_out_block(NGRP - 2, ffev_snaps[NGRP - 2])
        gm = NGRP - 1
        for oc in range(16):
            goc = gm * 16 + oc
            bank = ps[f"p{oc % 2}"]
            def mm(q, oc=oc, goc=goc, bank=bank, g=gm, toks=qtr_snaps[gm]):
                S.wait(q, "tensor", t_wq)
                for tk in toks:
                    S.wait(q, "tensor", tk)
                S.wait(q, "tensor", t_ffev.get(goc - 2))
                inst = None
                for ic in range(16):
                    inst = q.matmul(out=bank[:, :], lhsT=WQ[:, ic, oc * 128:(oc + 1) * 128],
                                    rhs=XQF[:, g % 2, :, ic, :], start=(ic == 0), stop=(ic == 15))
                return S.done("tensor", inst)
            t_mm = S.sec("tensor", mm)
            def ev(q, oc=oc, bank=bank, t_mm=t_mm):
                S.wait(q, "scalar", t_mm)
                S.wait(q, "scalar", st["t_ft_last"])
                inst = q.copy(out=FF[:, :, oc, :], in_=bank[:, :].rearrange("p (a b) -> p a b", a=4))
                return S.done("scalar", inst)
            t_ffev[goc] = S.sec("scalar", ev)
        ffev_snaps[gm] = {oc: t_ffev[gm * 16 + oc] for oc in range(16)}
        emit_out_block(NGRP - 1, ffev_snaps[NGRP - 1])
        S.flush()


_CACHE = {}

def kernel(L0_attn_hidden_states, L0_attn_W_in, L0_attn_conv_weight,
           L0_attn_conv_bias, L0_attn_A_log, L0_attn_D, L0_attn_dt_bias,
           L0_attn_norm_weight, L0_attn_W_out, L0_ffn_weight):
    x = np.ascontiguousarray(L0_attn_hidden_states, dtype=np.float32)
    key = (tuple(DEBUG_OUTS), tuple(BUILD_STAGES))
    nc = _CACHE.get(key)
    if nc is None:
        nc = build_program(tuple(DEBUG_OUTS), tuple(BUILD_STAGES))
        _CACHE[key] = nc
    base = {
        "w_in": np.ascontiguousarray(L0_attn_W_in, dtype=np.float32),
        "conv_w": np.ascontiguousarray(L0_attn_conv_weight, dtype=np.float32),
        "conv_b": np.ascontiguousarray(L0_attn_conv_bias, dtype=np.float32),
        "a_log": np.ascontiguousarray(L0_attn_A_log, dtype=np.float32),
        "d_vec": np.ascontiguousarray(L0_attn_D, dtype=np.float32),
        "dt_bias": np.ascontiguousarray(L0_attn_dt_bias, dtype=np.float32),
        "normw": np.ascontiguousarray(L0_attn_norm_weight, dtype=np.float32),
        "w_out": np.ascontiguousarray(L0_attn_W_out, dtype=np.float32),
        "w_ffn": np.ascontiguousarray(L0_ffn_weight, dtype=np.float32),
    }
    in_maps = [dict(base, x=np.ascontiguousarray(x[i])) for i in range(8)]
    res = run_bass_kernel_spmd(nc, in_maps, core_ids=list(range(8)))
    _CACHE["last_results"] = res
    return np.stack([np.asarray(r["out"]) for r in res.results], axis=0)


# version 18
# speedup vs baseline: 1.0370x; 1.0202x over previous
import contextlib
import numpy as np

import concourse.bass as bass
import concourse.mybir as mybir
from concourse.bass_utils import run_bass_kernel_spmd

F32 = mybir.dt.float32
BF16 = mybir.dt.bfloat16
AF = mybir.ActivationFunctionType
ALU = mybir.AluOpType
AX = mybir.AxisListType

LSEQ = 2048
DM = 2048
PROJ = 5136
CONVD = 3072
NH = 16
QCH = 256
NCH = LSEQ // QCH
KC = 16
TGRP = 512
NGRP = LSEQ // TGRP
EPS = 1e-5
NEGBIG = -1.0e30

ENG_SEM = {"tensor": "tensor", "vector": "vector", "scalar": "scalar_c",
           "gpsimd": "gpsimd_c"}
DMA_SEM = {"sync": "d_sync", "gpsimd": "d_gpsimd", "scalar": "d_scalar"}

CHAIN_OPS = {"tensor_scalar", "tensor_single_scalar", "tensor_tensor", "tensor_copy",
             "tensor_add", "tensor_sub", "tensor_mul", "tensor_max", "tensor_relu",
             "scalar_tensor_tensor", "tensor_tensor_scan", "tensor_reduce",
             "reduce_sum", "reduce_max", "reciprocal", "memset", "memzero", "iota",
             "activation", "copy", "select", "copy_predicated", "add", "mul", "sign",
             "sqrt", "square", "bn_stats", "bn_aggr", "stream_shuffle"}


class RInst:
    __slots__ = ("entry",)
    def __init__(self, entry):
        self.entry = entry

    def then_inc(self, sem, n):
        self.entry[3].append((sem, n))
        return self


class RQ:
    def __init__(self, S, eng):
        self._S = S
        self._eng = eng

    def __getattr__(self, name):
        S, eng = self._S, self._eng
        def rec(*a, **k):
            entry = [name, a, k, []]
            S.pending[eng].append(entry)
            return RInst(entry)
        return rec


class QW:
    def __init__(self, S, eng, q):
        self._S = S
        self._eng = eng
        self._q = q

    def __getattr__(self, name):
        f = getattr(self._q, name)
        if name not in CHAIN_OPS:
            return f
        S, eng, q = self._S, self._eng, self._q
        sem = ENG_SEM[eng]
        def wrap(*a, **k):
            S.wait(q, eng, (sem, S.cnt[sem]))
            inst = f(*a, **k)
            S.cnt[sem] += 1
            inst.then_inc(S.sems[sem], 1)
            return inst
        return wrap

DEBUG_OUTS = []
BUILD_STAGES = ["s0", "s1", "s2a", "s2b", "s2g", "s3", "s4", "s56"]


class Sched:
    def __init__(self, nc, es):
        self.nc = nc
        self.block = es.enter_context(nc.Block())
        self.sems = {}
        self.cnt = {}
        self.waited = {}
        self.bar_expected = 0
        self.bar_snapshot = {}
        self.stage_last = {}
        self.stage_dma = {}
        self.stage_first_wait_done = set()
        self.pending = {e: [] for e in ("tensor", "vector", "scalar", "gpsimd", "sync")}
        names = ["tensor", "vector", "scalar_c", "gpsimd_c"]
        for qn in ("sync", "gpsimd", "scalar"):
            for k in range(8):
                names.append(f"d_{qn}_{k}")
        for name in names:
            self.sems[name] = es.enter_context(nc.semaphore(name))
            self.cnt[name] = 0
        self.dma_rot = {"sync": 0, "gpsimd": 0, "scalar": 0}

    def sec(self, eng, body):
        q = RQ(self, eng)
        if eng not in self.stage_first_wait_done:
            self.stage_first_wait_done.add(eng)
            for sname, v in self.bar_snapshot.items():
                if v > 0 and self.waited.get((eng, sname), 0) < v:
                    q.wait_ge(self.sems[sname], v)
                    self.waited[(eng, sname)] = v
        qq = QW(self, eng, q) if eng in ("vector", "scalar", "gpsimd") else q
        return body(qq)

    def flush(self):
        for eng, entries in self.pending.items():
            if not entries:
                continue
            def fn(q, entries=tuple(entries)):
                for name, a, k, incs in entries:
                    inst = getattr(q, name)(*a, **k)
                    for sem, n in incs:
                        inst.then_inc(sem, n)
            getattr(self.block, eng)(fn)
            self.pending[eng] = []

    def wait(self, q, eng, token):
        if token is None:
            return
        if isinstance(token, (list, tuple)) and token and isinstance(token[0], (list, tuple)):
            for tk in token:
                self.wait(q, eng, tk)
            return
        sem, val = token
        if val <= 0:
            return
        k = (eng, sem)
        if self.waited.get(k, 0) >= val:
            return
        if isinstance(q, QW):
            q = q._q
        q.wait_ge(self.sems[sem], val)
        self.waited[k] = val

    dsc = None
    _dsc_init = False

    def _dsc_ensure(self, q):
        if not self._dsc_init:
            self._dsc_init = True
            q.memzero(self.dsc[0:1, 0:1])
            q.activation(out=self.dsc[0:1, 1:2], in_=self.dsc[0:1, 0:1], func=AF.Exp)

    def dum_pair(self, q):
        self._dsc_ensure(q)
        q.activation(out=self.dsc[0:1, 1:2], in_=self.dsc[0:1, 0:1], func=AF.Exp)
        q.activation(out=self.dsc[0:1, 0:1], in_=self.dsc[0:1, 1:2], func=AF.Ln)

    def dum_ln(self, q):
        self._dsc_ensure(q)
        q.activation(out=self.dsc[0:1, 0:1], in_=self.dsc[0:1, 1:2], func=AF.Ln)

    def dum_silu(self, q):
        self._dsc_ensure(q)
        q.activation(out=self.dsc[0:1, 2:3], in_=self.dsc[0:1, 0:1], func=AF.Silu)

    def tok(self, sem):
        return (sem, self.cnt[sem])

    def done(self, eng, inst):
        sem = ENG_SEM[eng]
        if eng == "tensor":
            self.cnt[sem] += 1
            inst.then_inc(self.sems[sem], 1)
        return (sem, self.cnt[sem])

    def note(self, eng, inst):
        self.stage_last[eng] = inst
        return inst

    def dma(self, q, eng, out, in_, transpose=False):
        if isinstance(q, QW):
            q = q._q
        k = self.dma_rot[eng]
        self.dma_rot[eng] = (k + 1) % 8
        sem = f"d_{eng}_{k}"
        self.wait(q, eng, (sem, self.cnt[sem]))
        if transpose:
            inst = q.dma_start_transpose(out=out, in_=in_)
        else:
            inst = q.dma_start(out=out, in_=in_)
        self.cnt[sem] += 16
        inst.then_inc(self.sems[sem], 16)
        return (sem, self.cnt[sem])

    def stage_end(self):
        self.flush()
        self.stage_first_wait_done = set()
        self.bar_snapshot = dict(self.cnt)


class Ctx:
    pass


def build_program(debug_outs=(), stages=None):
    stages = list(stages if stages is not None else BUILD_STAGES)
    nc = bass.Bass("TRN2", target_bir_lowering=False)
    C = Ctx()
    C.nc = nc
    d = {}
    d["x"] = nc.dram_tensor("x", [LSEQ, DM], F32, kind="ExternalInput").ap()
    d["w_in"] = nc.dram_tensor("w_in", [DM, PROJ], F32, kind="ExternalInput").ap()
    d["conv_w"] = nc.dram_tensor("conv_w", [CONVD, 4], F32, kind="ExternalInput").ap()
    d["conv_b"] = nc.dram_tensor("conv_b", [CONVD], F32, kind="ExternalInput").ap()
    d["a_log"] = nc.dram_tensor("a_log", [NH], F32, kind="ExternalInput").ap()
    d["d_vec"] = nc.dram_tensor("d_vec", [NH], F32, kind="ExternalInput").ap()
    d["dt_bias"] = nc.dram_tensor("dt_bias", [NH], F32, kind="ExternalInput").ap()
    d["normw"] = nc.dram_tensor("normw", [DM], F32, kind="ExternalInput").ap()
    d["w_out"] = nc.dram_tensor("w_out", [DM, DM], F32, kind="ExternalInput").ap()
    d["w_ffn"] = nc.dram_tensor("w_ffn", [DM, DM], F32, kind="ExternalInput").ap()
    d["out"] = nc.dram_tensor("out", [LSEQ, DM], F32, kind="ExternalOutput").ap()
    def scratch(name, shape, dt_):
        kind = "ExternalOutput" if name in debug_outs else "Internal"
        nm = ("dbg_" + name) if name in debug_outs else name
        return nc.dram_tensor(nm, shape, dt_, kind=kind).ap()
    d["LNF"] = scratch("LNF", [KC, 128, LSEQ], BF16)
    d["XBC"] = scratch("XBC", [24, 128, LSEQ], BF16)
    d["SG"] = scratch("SG", [KC, 128, LSEQ], BF16)
    d["GS"] = scratch("GS", [KC, 128, LSEQ], BF16)
    d["X1"] = scratch("X1", [LSEQ, DM], F32)
    d["WQT"] = scratch("WQT", [KC, 128, KC, 128], BF16)
    d["RSTDD"] = scratch("RSTDD", [2, 512], F32)
    C.d = d

    with contextlib.ExitStack() as es:
        S = Sched(nc, es)
        C.S = S
        sb = {}
        C.sb = sb
        ps = {}
        C.ps = ps
        def sbuf(name, shape, dt_):
            sb[name] = es.enter_context(nc.sbuf_tensor(name, shape, dt_))
            return sb[name]
        sbuf("I_f32", [128, 128], F32)
        sbuf("I_bf", [128, 128], BF16)
        sbuf("onesrow", [1, 128], F32)
        sbuf("onescol", [128, 1], F32)
        sbuf("ones16", [16, 256], F32)
        sbuf("D_bc", [128, 16], F32)
        sbuf("normw_fm", [128, 16], F32)
        sbuf("convw_col", [128, 96], F32)
        sbuf("convb_col", [128, 24], F32)
        sbuf("acol", [16, 1], F32)
        sbuf("dtb_col", [16, 1], F32)
        sbuf("drow", [1, 16], F32)
        sbuf("smalls", [128, 48], F32)
        sbuf("cbias", [128, 2], F32)
        sbuf("row24", [24, 128], F32)
        sbuf("zb", [128, 72], BF16)
        sbuf("pcol", [128, 1], F32)
        sbuf("irow", [128, 512], F32)
        sbuf("dsc", [1, 3], F32)
        S.dsc = sb["dsc"]
        for i in range(7):
            ps[f"p{i}"] = es.enter_context(nc.psum_tensor(f"p{i}", [128, 512], F32))
        ps["pt"] = es.enter_context(nc.psum_tensor("pt", [128, 1024], BF16))

        if "s0" in stages or "s1" in stages:
            stage0(C)
            S.stage_end()
        with contextlib.ExitStack() as es23:
            sb23 = {}
            C.sb23 = sb23
            def sbuf23(name, shape, dt_):
                sb23[name] = es23.enter_context(nc.sbuf_tensor(name, shape, dt_))
                return sb23[name]
            if any(s in stages for s in ("s2a", "s2b", "s2g", "s3")):
                sbuf23("IND", [16, KC * 128], F32)
                sbuf23("maskneg", [128, 512], F32)
                sbuf23("cum_all", [16, LSEQ], F32)
                sbuf23("expcum_all", [16, LSEQ], F32)
                sbuf23("colT", [128, NCH * 2 * 3 * 16], F32)
                sbuf23("S_a", [128, NH * 128], F32)
                sbuf23("S_b", [128, NH * 128], F32)
                sbuf23("S_bf", [128, NH * 128], BF16)
                stage23_consts(C)
            if "s2a" in stages:
                stage2(C, 0)
                S.stage_end()
            if "s2b" in stages:
                stage2(C, 1)
                S.stage_end()
            if "s2g" in stages:
                stage2g(C)
                S.stage_end()
            if "s3" in stages:
                stage3(C)
                S.stage_end()
        if "s4" in stages:
            stage4(C)
            S.stage_end()
        if "s56" in stages:
            stage56(C)
            S.stage_end()
        def drain(q):
            for sname, v in S.cnt.items():
                if sname.startswith("d_") and v > 0 and S.waited.get(("sync", sname), 0) < v:
                    q.wait_ge(S.sems[sname], v)
                    S.waited[("sync", sname)] = v
        S.sec("sync", drain)
        S.flush()
    return nc


def stage0(C):
    nc, S, sb, ps, d = C.nc, C.S, C.sb, C.ps, C.d

    def ld(q):
        t = {}
        t["convw"] = S.dma(q, "sync", sb["convw_col"].ap().rearrange("p (cc k) -> p cc k", k=4),
                           d["conv_w"].rearrange("(cc p) k -> p cc k", p=128))
        t["cb"] = S.dma(q, "sync", sb["row24"][:, :],
                        d["conv_b"].rearrange("(cc p) -> cc p", p=128))
        t["al"] = S.dma(q, "sync", sb["acol"][:, :], d["a_log"].rearrange("(a o) -> a o", o=1))
        t["db"] = S.dma(q, "sync", sb["dtb_col"][:, :], d["dt_bias"].rearrange("(a o) -> a o", o=1))
        t["dr"] = S.dma(q, "sync", sb["drow"][:, :], d["d_vec"].rearrange("(o a) -> o a", o=1))
        return t
    t = S.sec("sync", ld)

    def io(q):
        i1 = q.iota(sb["irow"][:, :], pattern=[[1, 512]], base=0, channel_multiplier=0,
                    allow_small_or_imprecise_dtypes=True)
        t1 = S.done("gpsimd", i1)
        i2 = q.iota(sb["pcol"][:, :], pattern=[[0, 1]], base=0, channel_multiplier=1,
                    allow_small_or_imprecise_dtypes=True)
        return t1, S.done("gpsimd", i2)
    t_irow, t_pcol = S.sec("gpsimd", io)

    def consts(q):
        S.wait(q, "vector", t_irow)
        S.wait(q, "vector", t_pcol)
        inst = q.tensor_scalar(out=sb["I_f32"][:, :], in0=sb["irow"][:, 0:128],
                               scalar1=sb["pcol"][:, 0:1], scalar2=None, op0=ALU.is_equal)
        tI = S.done("vector", inst)
        inst = q.tensor_copy(out=sb["I_bf"][:, :], in_=sb["I_f32"][:, :])
        tIb = S.done("vector", inst)
        q.memset(sb["onesrow"][:, :], 1.0)
        q.memset(sb["onescol"][:, :], 1.0)
        q.memset(sb["ones16"][:, :], 1.0)
        q.memset(sb["cbias"][:, 0:1], EPS)
        q.memset(sb["cbias"][:, 1:2], 1.0)
        inst = q.memset(sb["zb"][:, :], 0.0)
        return tI, tIb, S.done("vector", inst)
    t_I, t_Ibf, t_ms = S.sec("vector", consts)
    C.t_Ibf = t_Ibf

    def tr2(q):
        S.wait(q, "tensor", t["cb"])
        S.wait(q, "tensor", t_I)
        inst = q.transpose(out=ps["p0"][0:128, 0:24], in_=sb["row24"][0:24, 0:128],
                           identity=sb["I_f32"][0:24, 0:24])
        return S.done("tensor", inst)
    t_cbT = S.sec("tensor", tr2)
    def ev2(q):
        S.wait(q, "vector", t_cbT)
        inst = q.tensor_copy(out=sb["convb_col"][:, :], in_=ps["p0"][0:128, 0:24])
        return S.done("vector", inst)
    t_cbfm = S.sec("vector", ev2)

    def ldnw(q):
        S.wait(q, "sync", t_cbT)
        return S.dma(q, "sync", sb["row24"][0:16, :],
                     d["normw"].rearrange("(a b) -> a b", b=128))
    t_nw = S.sec("sync", ldnw)
    def tr1(q):
        S.wait(q, "tensor", t_nw)
        S.wait(q, "tensor", t_cbfm)
        inst = q.transpose(out=ps["p0"][0:128, 0:16], in_=sb["row24"][0:16, 0:128],
                           identity=sb["I_f32"][0:16, 0:16])
        return S.done("tensor", inst)
    t_nwT = S.sec("tensor", tr1)
    def ev1(q):
        S.wait(q, "vector", t_nwT)
        inst = q.tensor_copy(out=sb["normw_fm"][:, :], in_=ps["p0"][0:128, 0:16])
        return S.done("vector", inst)
    t_nwfm = S.sec("vector", ev1)

    def mmD(q):
        S.wait(q, "tensor", t["dr"])
        S.wait(q, "tensor", t_ms)
        S.wait(q, "tensor", t_nwfm)
        inst = q.matmul(out=ps["p0"][0:128, 0:16], lhsT=sb["onesrow"][:, :],
                        rhs=sb["drow"][:, :], start=True, stop=True)
        return S.done("tensor", inst)
    t_dbc = S.sec("tensor", mmD)
    def ev3(q):
        S.wait(q, "vector", t_dbc)
        inst = q.tensor_copy(out=sb["D_bc"][:, :], in_=ps["p0"][0:128, 0:16])
        return S.done("vector", inst)
    t_dbcv = S.sec("vector", ev3)

    def sA(q):
        S.wait(q, "scalar", t["al"])
        inst = q.activation(out=sb["smalls"][0:16, 40:41], in_=sb["acol"][:, :], func=AF.Exp)
        return S.done("scalar", inst)
    t_ea = S.sec("scalar", sA)
    def vA(q):
        S.wait(q, "vector", t_ea)
        inst = q.tensor_scalar(out=sb["acol"][:, :], in0=sb["smalls"][0:16, 40:41],
                               scalar1=-1.0, scalar2=None, op0=ALU.mult)
        return S.done("vector", inst)
    S.sec("vector", vA)

    with contextlib.ExitStack() as es0:
        wld = [es0.enter_context(nc.sbuf_tensor(f"s0_wld{i}", [128, 2048], F32)) for i in range(2)]
        wjunk = es0.enter_context(nc.sbuf_tensor("s0_junk", [128, 2048], F32))
        wn = es0.enter_context(nc.sbuf_tensor("s0_wn", [128, 2048], F32))
        q2t = es0.enter_context(nc.sbuf_tensor("s0_q2", [128, 2048], F32))
        wqb = es0.enter_context(nc.sbuf_tensor("s0_wqb", [128, 2048], BF16))
        tq = es0.enter_context(nc.sbuf_tensor("s0_tq", [128, 16, 128], BF16))
        rs = es0.enter_context(nc.sbuf_tensor("s0_rs", [128, 16], F32))
        xt = [es0.enter_context(nc.sbuf_tensor(f"s1_x{i}", [128, 2048], F32)) for i in range(2)]
        bns = es0.enter_context(nc.sbuf_tensor("s1_bns", [128, 4, 6], F32))
        lns = [es0.enter_context(nc.sbuf_tensor(f"s1_l{i}", [128, 2048], BF16)) for i in range(2)]
        lnt = [es0.enter_context(nc.sbuf_tensor(f"s1_t{i}", [128, 16, 128], BF16)) for i in range(2)]

        tload = [None, None]
        tuse = [None, None]
        st0 = {"t_tq": None, "t_st": None, "t_q2_prev": None,
               "t_rtot": None, "t_inv2": None}

        def emit_s0a(sl):
            b = sl % 2
            def ldw(q, b=b, sl=sl):
                S.wait(q, "sync", tuse[b])
                return S.dma(q, "sync", wld[b][:, :], d["w_ffn"][sl * 128:(sl + 1) * 128, :])
            tload[b] = S.sec("sync", ldw)
            def absum(q, b=b, sl=sl):
                S.wait(q, "scalar", tload[b])
                inst = q.activation(out=wjunk[:, :], in_=wld[b][:, :], func=AF.Abs,
                                    accum_out=rs[:, sl:sl + 1])
                return S.done("scalar", inst)
            tuse[b] = S.sec("scalar", absum)

        def emit_s0mid():
            def vtot(q):
                S.wait(q, "vector", tuse[0])
                S.wait(q, "vector", tuse[1])
                inst = q.tensor_reduce(out=sb["smalls"][:, 32:33], in_=rs[:, :], axis=AX.X, op=ALU.add)
                return S.done("vector", inst)
            t_rtot = S.sec("vector", vtot)
            def mtot(q, t_rtot=t_rtot):
                S.wait(q, "tensor", t_rtot)
                S.wait(q, "tensor", t_dbcv)
                inst = q.matmul(out=ps["p1"][0:1, 0:1], lhsT=sb["smalls"][:, 32:33],
                                rhs=sb["onescol"][:, :], start=True, stop=True)
                return S.done("tensor", inst)
            t_tot = S.sec("tensor", mtot)
            def vsc(q, t_tot=t_tot):
                S.wait(q, "vector", t_tot)
                q.tensor_scalar(out=sb["smalls"][0:1, 33:34], in0=ps["p1"][0:1, 0:1],
                                scalar1=1.0 / (DM * DM), scalar2=1e-6, op0=ALU.mult, op1=ALU.max)
                inst = q.reciprocal(out=sb["smalls"][0:1, 34:35], in_=sb["smalls"][0:1, 33:34])
                return S.done("vector", inst)
            t_inv = S.sec("vector", vsc)
            def mbc(q, t_inv=t_inv):
                S.wait(q, "tensor", t_inv)
                q.matmul(out=ps["p1"][0:128, 2:3], lhsT=sb["onesrow"][:, :],
                         rhs=sb["smalls"][0:1, 33:34], start=True, stop=True)
                inst = q.matmul(out=ps["p1"][0:128, 3:4], lhsT=sb["onesrow"][:, :],
                                rhs=sb["smalls"][0:1, 34:35], start=True, stop=True)
                return S.done("tensor", inst)
            t_bc2 = S.sec("tensor", mbc)
            def vbc(q, t_bc2=t_bc2):
                S.wait(q, "vector", t_bc2)
                q.tensor_copy(out=sb["smalls"][:, 35:36], in_=ps["p1"][0:128, 2:3])
                inst = q.tensor_copy(out=sb["smalls"][:, 36:37], in_=ps["p1"][0:128, 3:4])
                return S.done("vector", inst)
            st0["t_inv2"] = S.sec("vector", vbc)
            tload[0] = tload[1] = None

        def emit_s0b(sl):
            b = sl % 2
            def ldw(q, b=b, sl=sl):
                S.wait(q, "sync", tuse[b])
                return S.dma(q, "sync", wld[b][:, :], d["w_ffn"][sl * 128:(sl + 1) * 128, :])
            tload[b] = S.sec("sync", ldw)
            def vwn(q, b=b):
                S.wait(q, "vector", tload[b])
                S.wait(q, "vector", st0["t_inv2"])
                S.wait(q, "vector", st0["t_tq"])
                S.wait(q, "vector", st0["t_q2_prev"])
                inst = q.tensor_scalar(out=wn[:, :], in0=wld[b][:, :],
                                       scalar1=sb["smalls"][:, 36:37], scalar2=None, op0=ALU.mult)
                return S.done("vector", inst)
            tuse[b] = S.sec("vector", vwn)
            def gq2(q, b=b):
                inst = q.tensor_scalar(out=q2t[:, :], in0=wn[:, :], scalar1=-0.5,
                                       scalar2=None, op0=ALU.is_le)
                return S.done("vector", inst)
            t_q2 = S.sec("vector", gq2)
            st0["t_q2_prev"] = t_q2
            def vwq(q, t_q2=t_q2):
                S.wait(q, "vector", t_q2)
                inst = q.scalar_tensor_tensor(out=wqb[:, :], in0=wn[:, :], scalar=0.5,
                                              in1=q2t[:, :], op0=ALU.is_ge, op1=ALU.subtract)
                return S.done("vector", inst)
            t_wq = S.sec("vector", vwq)
            def str_(q, t_wq=t_wq):
                S.wait(q, "scalar", t_wq)
                S.wait(q, "scalar", st0["t_st"])
                return S.dma(q, "scalar", tq[:, :, :], wqb[:, :], transpose=True)
            st0["t_tq"] = S.sec("scalar", str_)
            def sto(q, sl=sl):
                S.wait(q, "sync", st0["t_tq"])
                return S.dma(q, "sync",
                             d["WQT"][:, :, sl, :].rearrange("ic i o -> i ic o"),
                             tq[:, :, :])
            st0["t_st"] = S.sec("sync", sto)

        t_xuse = [None, None]
        t_lnuse = [None, None]
        t_tuse = [None, None]

        def emit_s1(tt):
            b = tt % 2
            c0 = 8 * b
            def ldx(q, b=b, tt=tt):
                S.wait(q, "gpsimd", t_xuse[b])
                return S.dma(q, "gpsimd", xt[b][:, :], d["x"][tt * 128:(tt + 1) * 128, :])
            t_x = S.sec("gpsimd", ldx)
            def vstats(q, b=b, c0=c0, t_x=t_x):
                S.wait(q, "vector", t_x)
                for j in range(4):
                    q.bn_stats(out=bns[:, j, :], in_=xt[b][:, j * 512:(j + 1) * 512])
                inst = q.bn_aggr(out=sb["smalls"][:, c0 + 2:c0 + 4],
                                 in_=bns.ap().rearrange("p a b -> p (a b)"))
                return S.done("vector", inst)
            t_var = S.sec("vector", vstats)
            def srstd(q, c0=c0, t_var=t_var):
                S.wait(q, "scalar", t_var)
                q.activation(out=sb["smalls"][:, c0 + 5:c0 + 6],
                             in_=sb["smalls"][:, c0 + 3:c0 + 4], func=AF.Ln,
                             bias=sb["cbias"][:, 0:1])
                inst = q.activation(out=sb["smalls"][:, c0 + 6:c0 + 7],
                                    in_=sb["smalls"][:, c0 + 5:c0 + 6], func=AF.Exp, scale=-0.5)
                return S.done("scalar", inst)
            t_rstd = S.sec("scalar", srstd)
            def vapply(q, b=b, c0=c0, t_rstd=t_rstd):
                S.wait(q, "vector", t_rstd)
                S.wait(q, "vector", t_lnuse[b])
                inst = q.tensor_scalar(out=lns[b][:, :], in0=xt[b][:, :],
                                       scalar1=sb["smalls"][:, c0 + 2:c0 + 3],
                                       scalar2=sb["smalls"][:, c0 + 6:c0 + 7],
                                       op0=ALU.subtract, op1=ALU.mult)
                return S.done("vector", inst)
            t_ln = S.sec("vector", vapply)
            t_xuse[b] = t_ln
            def strp(q, b=b, t_ln=t_ln):
                S.wait(q, "scalar", t_ln)
                S.wait(q, "scalar", t_tuse[b])
                return S.dma(q, "scalar", lnt[b][:, :, :], lns[b][:, :], transpose=True)
            t_lnuse[b] = S.sec("scalar", strp)
            def sto(q, b=b, tt=tt, tk=t_lnuse[b]):
                S.wait(q, "sync", tk)
                return S.dma(q, "sync",
                             d["LNF"][:, :, tt * 128:(tt + 1) * 128].rearrange("k f t -> f k t"),
                             lnt[b][:, :, :])
            t_tuse[b] = S.sec("sync", sto)

        for tt in range(8):
            emit_s0a(2 * tt)
            emit_s0a(2 * tt + 1)
            emit_s1(tt)
        emit_s0mid()
        for tt in range(8, 16):
            emit_s0b(2 * (tt - 8))
            emit_s0b(2 * (tt - 8) + 1)
            emit_s1(tt)
        S.flush()


def stage23_consts(C):
    S, sb, sb23 = C.S, C.sb, C.sb23
    def io(q):
        q.iota(sb23["maskneg"][:, 0:256], pattern=[[1, 256]], base=0, channel_multiplier=0,
               allow_small_or_imprecise_dtypes=True)
        i2 = q.iota(sb23["maskneg"][:, 256:512], pattern=[[1, 256]], base=-128,
                    channel_multiplier=0, allow_small_or_imprecise_dtypes=True)
        t1 = S.done("gpsimd", i2)
        i3 = q.iota(sb23["IND"][:, :], pattern=[[0, KC * 128]], base=0, channel_multiplier=1,
                    allow_small_or_imprecise_dtypes=True)
        return t1, S.done("gpsimd", i3)
    t_io, t_indk = S.sec("gpsimd", io)

    def vm(q):
        S.wait(q, "vector", t_io)
        q.tensor_scalar(out=sb23["maskneg"][:, 0:256], in0=sb23["maskneg"][:, 0:256],
                        scalar1=sb["pcol"][:, 0:1], scalar2=NEGBIG,
                        op0=ALU.is_lt, op1=ALU.mult)
        inst = q.tensor_scalar(out=sb23["maskneg"][:, 256:512], in0=sb23["maskneg"][:, 256:512],
                               scalar1=sb["pcol"][:, 0:1], scalar2=NEGBIG,
                               op0=ALU.is_lt, op1=ALU.mult)
        t1 = S.done("vector", inst)
        S.wait(q, "vector", t_indk)
        inst = None
        for h in range(NH):
            inst = q.tensor_scalar(out=sb23["IND"][:, h * 128:(h + 1) * 128],
                                   in0=sb23["IND"][:, h * 128:(h + 1) * 128],
                                   scalar1=float(h), scalar2=None, op0=ALU.is_equal)
        t2 = S.done("vector", inst)
        q.memset(sb23["S_a"][:, :], 0.0)
        inst = q.memset(sb23["S_bf"][:, :], 0.0)
        t3 = S.done("vector", inst)
        return t1, t2, t3
    C.t_mask, C.t_ind, C.t_szero = S.sec("vector", vm)


def stage2(C, half):
    nc, S, sb, sb23, ps, d = C.nc, C.S, C.sb, C.sb23, C.ps, C.d
    ncc = 12
    ccbase = half * 12
    col0 = DM + half * 1536
    ncols = 1536 + (16 if half == 1 else 0)
    with contextlib.ExitStack() as es2:
        pf = f"s2h{half}_"
        WB = es2.enter_context(nc.sbuf_tensor(pf + "wb", [128, 16, ncols], BF16))
        LNG = es2.enter_context(nc.sbuf_tensor(pf + "lng", [128, 16, TGRP], BF16))
        HBCR = es2.enter_context(nc.sbuf_tensor(pf + "hbcr", [128, ncc, TGRP + 3], BF16))
        XBCg = es2.enter_context(nc.sbuf_tensor(pf + "xbc", [128, ncc, TGRP], BF16))
        CVA = es2.enter_context(nc.sbuf_tensor(pf + "cva", [128, 2, TGRP], BF16))
        CVB = es2.enter_context(nc.sbuf_tensor(pf + "cvb", [128, ncc, TGRP], BF16))
        DTRAW = es2.enter_context(nc.sbuf_tensor(pf + "dtraw", [16, TGRP], F32))
        PAN = es2.enter_context(nc.sbuf_tensor(pf + "pan", [16, 6, TGRP], F32))

        def ldw(q):
            return [S.dma(q, "gpsimd", WB[:, kc, :],
                          d["w_in"][kc * 128:(kc + 1) * 128, col0:col0 + ncols])
                    for kc in range(16)]
        t_w = S.sec("gpsimd", ldw)

        t_lng = None
        t_lng_use = None
        t_evac = {}
        t_conv = {}
        t_store = None
        t_silu = {}
        t_silu_slot = {}
        t_halo = None
        t_dtevac = None
        t_panT_evac = None
        t_mm_last = None
        t_pe2_prev = None
        t_panT_prev = None
        for g in range(NGRP):
            def ldl(q, g=g):
                S.wait(q, "sync", t_lng_use)
                return S.dma(q, "sync", LNG[:, :, :],
                             d["LNF"][:, :, g * TGRP:(g + 1) * TGRP].rearrange("k f t -> f k t"))
            t_lng = S.sec("sync", ldl)

            def halo(q, g=g):
                for cc in range(ncc):
                    S.wait(q, "vector", t_conv.get(cc))
                if g == 0:
                    S.wait(q, "vector", C.t_mask)
                    inst = q.tensor_copy(out=HBCR[:, :, 0:3],
                                         in_=sb["zb"].ap().rearrange("p (a b) -> p a b", b=3)[:, 0:ncc, :])
                else:
                    inst = q.tensor_copy(out=HBCR[:, :, 0:3], in_=HBCR[:, :, TGRP:TGRP + 3])
                return S.done("vector", inst)
            t_halo = S.sec("vector", halo)

            for cc in range(ncc):
                gcc = g * ncc + cc
                bank = ps[f"p{cc % 2}"]
                def mm(q, cc=cc, gcc=gcc, bank=bank, t_lng=t_lng):
                    S.wait(q, "tensor", t_w)
                    S.wait(q, "tensor", t_lng)
                    S.wait(q, "tensor", t_evac.get(gcc - 2))
                    inst = None
                    for kc in range(16):
                        inst = q.matmul(out=bank[:, :], lhsT=WB[:, kc, cc * 128:(cc + 1) * 128],
                                        rhs=LNG[:, kc, :], start=(kc == 0), stop=(kc == 15))
                    return S.done("tensor", inst)
                t_mm = S.sec("tensor", mm)
                t_mm_last = t_mm
                eng = "scalar" if cc % 2 == 0 else "vector"
                def ev(q, cc=cc, bank=bank, t_mm=t_mm, eng=eng):
                    S.wait(q, eng, t_mm)
                    S.wait(q, eng, t_halo)
                    S.wait(q, eng, t_conv.get(cc))
                    if eng == "scalar":
                        inst = q.copy(out=HBCR[:, cc, 3:TGRP + 3], in_=bank[:, :])
                    else:
                        inst = q.tensor_copy(out=HBCR[:, cc, 3:TGRP + 3], in_=bank[:, :])
                    return S.done(eng, inst)
                t_evac[gcc] = S.sec(eng, ev)
            t_lng_use = t_mm_last

            if half == 1:
                def mmdt(q, t_lng=t_lng):
                    S.wait(q, "tensor", t_dtevac)
                    inst = None
                    for kc in range(16):
                        inst = q.matmul(out=ps["p2"][0:16, :], lhsT=WB[:, kc, 1536:1552],
                                        rhs=LNG[:, kc, :], start=(kc == 0), stop=(kc == 15))
                    return S.done("tensor", inst)
                t_mmdt = S.sec("tensor", mmdt)
                t_lng_use = t_mmdt
                def evdt(q, t_mmdt=t_mmdt):
                    S.wait(q, "vector", t_mmdt)
                    inst = q.tensor_copy(out=DTRAW[:, :], in_=ps["p2"][0:16, :])
                    return S.done("vector", inst)
                t_dtevac = S.sec("vector", evdt)

            for cc in range(ncc):
                ccg = ccbase + cc
                eng = "vector"
                ei = cc % 2
                def taps(q, cc=cc, ccg=ccg, ei=ei):
                    S.wait(q, "vector", t_evac[cc])
                    S.wait(q, "vector", t_silu_slot.get(ei))
                    wcol = lambda k: sb["convw_col"][:, 4 * ccg + k:4 * ccg + k + 1]
                    q.tensor_scalar(out=CVA[:, ei, :], in0=HBCR[:, cc, 0:TGRP],
                                    scalar1=wcol(0), scalar2=None, op0=ALU.mult)
                    q.scalar_tensor_tensor(out=CVB[:, cc, :], in0=HBCR[:, cc, 1:TGRP + 1],
                                           scalar=wcol(1), in1=CVA[:, ei, :],
                                           op0=ALU.mult, op1=ALU.add)
                    q.scalar_tensor_tensor(out=CVA[:, ei, :], in0=HBCR[:, cc, 2:TGRP + 2],
                                           scalar=wcol(2), in1=CVB[:, cc, :],
                                           op0=ALU.mult, op1=ALU.add)
                    inst = q.scalar_tensor_tensor(out=CVB[:, cc, :], in0=HBCR[:, cc, 3:TGRP + 3],
                                                  scalar=wcol(3), in1=CVA[:, ei, :],
                                                  op0=ALU.mult, op1=ALU.add)
                    return S.done("vector", inst)
                t_conv[cc] = S.sec("vector", taps)
                def silu(q, cc=cc, ccg=ccg, g=g):
                    S.wait(q, "scalar", t_store)
                    S.wait(q, "scalar", t_conv[cc])
                    inst = q.activation(out=XBCg[:, cc, :], in_=CVB[:, cc, :], func=AF.Silu,
                                        bias=sb["convb_col"][:, ccg:ccg + 1])
                    return S.done("scalar", inst)
                t_silu[cc] = S.sec("scalar", silu)
            t_silu_slot[0] = t_silu[ncc - 1]
            t_silu_slot[1] = t_silu[ncc - 1]

            if half == 1:
                def pan(q, g=g):
                    S.wait(q, "vector", t_dtevac)
                    S.wait(q, "vector", t_pe2_prev)
                    inst = q.tensor_scalar(out=PAN[:, 0, :], in0=DTRAW[:, :],
                                           scalar1=sb["dtb_col"][:, 0:1], scalar2=None, op0=ALU.add)
                    return S.done("vector", inst)
                t_z = S.sec("vector", pan)
                def pe1(q, t_z=t_z):
                    S.wait(q, "scalar", t_z)
                    q.activation(out=PAN[:, 1, :], in_=PAN[:, 0, :], func=AF.Abs)
                    q.activation(out=PAN[:, 2, :], in_=PAN[:, 1, :], func=AF.Exp, scale=-1.0)
                    q.activation(out=PAN[:, 3, :], in_=PAN[:, 2, :], func=AF.Ln,
                                 bias=sb["cbias"][0:16, 1:2])
                    inst = q.activation(out=PAN[:, 1, :], in_=PAN[:, 0, :], func=AF.Relu)
                    return S.done("scalar", inst)
                t_sp = S.sec("scalar", pe1)
                def pv1(q, t_sp=t_sp, g=g):
                    S.wait(q, "vector", t_sp)
                    q.tensor_tensor(out=PAN[:, 0, :], in0=PAN[:, 1, :], in1=PAN[:, 3, :], op=ALU.add)
                    inst = q.tensor_scalar(out=PAN[:, 2, :], in0=PAN[:, 0, :],
                                           scalar1=sb["acol"][:, 0:1], scalar2=None, op0=ALU.mult)
                    t_dt = S.done("vector", inst)
                    inst = None
                    for c01 in range(2):
                        cg = g * 2 + c01
                        inst = q.tensor_tensor_scan(out=sb23["cum_all"][:, cg * 256:(cg + 1) * 256],
                                                    data0=sb["ones16"][:, :],
                                                    data1=PAN[:, 2, c01 * 256:(c01 + 1) * 256],
                                                    initial=0.0, op0=ALU.mult, op1=ALU.add)
                    return t_dt, S.done("vector", inst)
                t_dt, t_cum = S.sec("vector", pv1)
                def pe2(q, g=g, t_cum=t_cum):
                    S.wait(q, "scalar", t_cum)
                    S.wait(q, "scalar", t_panT_prev)
                    q.activation(out=PAN[:, 3, :], in_=PAN[:, 0, :], func=AF.Ln)
                    inst = None
                    for c01 in range(2):
                        cg = g * 2 + c01
                        csl = sb23["cum_all"][:, cg * 256:(cg + 1) * 256]
                        q.activation(out=PAN[:, 4, c01 * 256:(c01 + 1) * 256], in_=csl,
                                     func=AF.Exp, scale=-1.0,
                                     bias=sb23["cum_all"][:, cg * 256 + 255:cg * 256 + 256])
                        inst = q.activation(out=sb23["expcum_all"][:, cg * 256:(cg + 1) * 256],
                                            in_=csl, func=AF.Exp)
                    return S.done("scalar", inst)
                t_pe2 = S.sec("scalar", pe2)
                t_pe2_prev = t_pe2
                def pv2(q, g=g, t_pe2=t_pe2):
                    S.wait(q, "vector", t_pe2)
                    S.wait(q, "vector", t_panT_prev)
                    q.tensor_tensor(out=PAN[:, 5, :], in0=PAN[:, 3, :],
                                    in1=sb23["cum_all"][:, g * TGRP:(g + 1) * TGRP], op=ALU.subtract)
                    inst = q.tensor_tensor(out=PAN[:, 4, :], in0=PAN[:, 0, :], in1=PAN[:, 4, :],
                                           op=ALU.mult)
                    t_w_ = S.done("vector", inst)
                    inst = None
                    for c01 in range(2):
                        cg = g * 2 + c01
                        inst = q.tensor_scalar(out=PAN[:, 3, c01 * 256:(c01 + 1) * 256],
                                               in0=sb["ones16"][:, :],
                                               scalar1=sb23["expcum_all"][:, cg * 256 + 255:cg * 256 + 256],
                                               scalar2=None, op0=ALU.mult)
                    return t_w_, S.done("vector", inst)
                t_wp, t_cd = S.sec("vector", pv2)
                def ptr(q, g=g, t_wp=t_wp, t_cd=t_cd):
                    S.wait(q, "tensor", t_wp)
                    S.wait(q, "tensor", t_cd)
                    S.wait(q, "tensor", t_panT_evac)
                    inst = None
                    for c01 in range(2):
                        for jt in range(2):
                            for qi, pan_i in ((0, 5), (1, 4), (2, 3)):
                                j = (c01 * 2 + jt) * 3 + qi
                                inst = q.transpose(out=ps["p3"][0:128, j * 16:(j + 1) * 16],
                                                   in_=PAN[:, pan_i, (c01 * 2 + jt) * 128:(c01 * 2 + jt + 1) * 128],
                                                   identity=sb["I_f32"][0:16, 0:16])
                    return S.done("tensor", inst)
                t_panT = S.sec("tensor", ptr)
                t_panT_prev = t_panT
                def pev(q, g=g, t_panT=t_panT):
                    S.wait(q, "vector", t_panT)
                    inst = q.tensor_copy(out=sb23["colT"][:, g * 192:(g + 1) * 192],
                                         in_=ps["p3"][0:128, 0:192])
                    return S.done("vector", inst)
                t_panT_evac = S.sec("vector", pev)

            def sto(q, g=g):
                for cc in range(ncc):
                    S.wait(q, "sync", t_silu[cc])
                return S.dma(q, "sync",
                             d["XBC"][ccbase:ccbase + ncc, :, g * TGRP:(g + 1) * TGRP].rearrange("c f t -> f c t"),
                             XBCg[:, :, :])
            t_store = S.sec("sync", sto)
        S.flush()


def stage2g(C):
    nc, S, sb, ps, d = C.nc, C.S, C.sb, C.ps, C.d
    with contextlib.ExitStack() as es2:
        WG = es2.enter_context(nc.sbuf_tensor("s2g_wg", [128, 16, 2048], BF16))
        LNG = es2.enter_context(nc.sbuf_tensor("s2g_lng", [128, 16, TGRP], BF16))
        SGG = es2.enter_context(nc.sbuf_tensor("s2g_sgg", [128, 16, TGRP], BF16))
        def ldw(q):
            return [S.dma(q, "gpsimd", WG[:, kc, :],
                          d["w_in"][kc * 128:(kc + 1) * 128, 0:DM])
                    for kc in range(16)]
        t_w = S.sec("gpsimd", ldw)
        t_lng_use = None
        t_store = None
        t_ev = {}
        for g in range(NGRP):
            def ldl(q, g=g):
                S.wait(q, "sync", t_lng_use)
                return S.dma(q, "sync", LNG[:, :, :],
                             d["LNF"][:, :, g * TGRP:(g + 1) * TGRP].rearrange("k f t -> f k t"))
            t_lng = S.sec("sync", ldl)
            for oc in range(16):
                bank = ps[f"p{oc % 4}"]
                def mm(q, oc=oc, bank=bank, t_lng=t_lng):
                    S.wait(q, "tensor", t_w)
                    S.wait(q, "tensor", t_lng)
                    S.wait(q, "tensor", t_ev.get(oc - 4))
                    inst = None
                    for kc in range(16):
                        inst = q.matmul(out=bank[:, :], lhsT=WG[:, kc, oc * 128:(oc + 1) * 128],
                                        rhs=LNG[:, kc, :], start=(kc == 0), stop=(kc == 15))
                    return S.done("tensor", inst)
                t_mm = S.sec("tensor", mm)
                def ev(q, oc=oc, bank=bank, t_mm=t_mm):
                    S.wait(q, "scalar", t_mm)
                    S.wait(q, "scalar", t_store)
                    inst = q.activation(out=SGG[:, oc, :], in_=bank[:, :], func=AF.Silu)
                    return S.done("scalar", inst)
                t_ev[oc] = S.sec("scalar", ev)
            t_lng_use = t_ev[15]
            def sto(q, g=g):
                for oc in range(16):
                    S.wait(q, "sync", t_ev[oc])
                return S.dma(q, "sync",
                             d["SG"][:, :, g * TGRP:(g + 1) * TGRP].rearrange("k f t -> f k t"),
                             SGG[:, :, :])
            t_store = S.sec("sync", sto)
        S.flush()


def stage3(C):
    nc, S, sb, sb23, ps, d = C.nc, C.S, C.sb, C.sb23, C.ps, C.d
    with contextlib.ExitStack() as es3:
        XBCc = [es3.enter_context(nc.sbuf_tensor(f"s3_xbc{i}", [128, 24, QCH], BF16)) for i in range(2)]
        SGc = [es3.enter_context(nc.sbuf_tensor(f"s3_sg{i}", [128, 16, QCH], BF16)) for i in range(2)]
        XT = es3.enter_context(nc.sbuf_tensor("s3_xt", [128, 8, 128], BF16))
        BT = es3.enter_context(nc.sbuf_tensor("s3_bt", [128, 8, 128], BF16))
        BW = es3.enter_context(nc.sbuf_tensor("s3_bw", [128, 4, 2, 128], BF16))
        MTD = es3.enter_context(nc.sbuf_tensor("s3_mtd", [128, 4, 512], BF16))
        EAR = es3.enter_context(nc.sbuf_tensor("s3_ear", [128, 4, 512], F32))
        ESB = es3.enter_context(nc.sbuf_tensor("s3_esb", [128, 4, 512], F32))
        CS = es3.enter_context(nc.sbuf_tensor("s3_cs", [128, 4, QCH], BF16))
        Y = es3.enter_context(nc.sbuf_tensor("s3_y", [128, 4, QCH], F32))
        G16 = es3.enter_context(nc.sbuf_tensor("s3_g16", [128, 16, QCH], BF16))
        GSQ = es3.enter_context(nc.sbuf_tensor("s3_gsq", [128, 4, QCH], F32))
        GSO = [es3.enter_context(nc.sbuf_tensor(f"s3_gso{i}", [128, 16, QCH], BF16)) for i in range(2)]
        RSUMS = es3.enter_context(nc.sbuf_tensor("s3_rsums", [1, 2, 512], F32))
        RSTDS = es3.enter_context(nc.sbuf_tensor("s3_rstds", [1, 2, 512], F32))
        RTMP = es3.enter_context(nc.sbuf_tensor("s3_rtmp", [1, 512], F32))
        RB = es3.enter_context(nc.sbuf_tensor("s3_rb", [128, 2, 512], F32))

        t_ld = [None, None]
        t_ld_sg = [None, None]
        t_xbc_use = [None, None]
        t_sg_use = [None, None]
        t_gso_use = [None, None]
        t_mtd_done = {}
        t_cs_done = {}
        t_yev = {}
        t_sup = {}
        t_su = {}
        t_gso = {}
        t_e_prev = {}
        t_y_prev = {}
        t_sc_prev = {}
        t_g_prev = {}
        t_gsq_prev = {}
        t_gsq_tok = {}
        t_xt_ev = None
        t_bt_ev = None
        t_sc_last = {}
        pend_sbf = [None]
        pend_epi = [None]
        t_mm3 = {}
        t_evac = {}
        t_srs = {}
        t_dma1 = {}
        t_dma2 = {}

        def emit_scalar_merged(q, h, fl, sl, t_ear):
            S.wait(q, "scalar", t_ear)
            inst = None
            for jt in range(2):
                ci = ((fl % NH * 0 + (fl // NH) * 2 + jt) * 3 + 0) * 16 + h
                inst = q.activation(out=ESB[:, sl, jt * 256:(jt + 1) * 256],
                                    in_=EAR[:, sl, jt * 256:(jt + 1) * 256], func=AF.Exp,
                                    bias=sb23["colT"][:, ci:ci + 1])
            return S.tok("scalar_c")

        for c in range(NCH):
            b = c % 2
            def ldc(q, c=c, b=b):
                S.wait(q, "sync", t_xbc_use[b])
                tk1 = S.dma(q, "sync", XBCc[b][:, :, :],
                            d["XBC"][:, :, c * QCH:(c + 1) * QCH].rearrange("a f t -> f a t"))
                S.wait(q, "sync", t_sg_use[b])
                tk2 = S.dma(q, "sync", SGc[b][:, :, :],
                            d["SG"][:, :, c * QCH:(c + 1) * QCH].rearrange("a f t -> f a t"))
                return tk1, tk2
            t_ld[b], t_ld_sg[b] = S.sec("sync", ldc)

            def trx(q, c=c, b=b):
                S.wait(q, "tensor", t_ld[b])
                S.wait(q, "tensor", t_xt_ev)
                S.wait(q, "tensor", C.t_Ibf)
                inst = None
                for gx in range(4):
                    for jt in range(2):
                        k = gx * 2 + jt
                        inst = q.transpose(out=ps["pt"][:, k * 128:(k + 1) * 128],
                                           in_=XBCc[b][:, gx, jt * 128:(jt + 1) * 128],
                                           identity=sb["I_bf"][:, :])
                return S.done("tensor", inst)
            t_trx = S.sec("tensor", trx)
            def evx(q, t_trx=t_trx, c=c):
                S.wait(q, "vector", t_trx)
                S.wait(q, "vector", t_sc_last.get(c - 1))
                inst = q.tensor_copy(out=XT.ap().rearrange("p a b -> p (a b)"),
                                     in_=ps["pt"][:, 0:1024])
                return S.done("vector", inst)
            t_xt_evac = S.sec("vector", evx)
            def trb(q, c=c, b=b, t_xt_evac=t_xt_evac):
                S.wait(q, "tensor", t_xt_evac)
                inst = None
                for gx in range(4):
                    for jt in range(2):
                        k = gx * 2 + jt
                        inst = q.transpose(out=ps["pt"][:, k * 128:(k + 1) * 128],
                                           in_=XBCc[b][:, 4 + gx, jt * 128:(jt + 1) * 128],
                                           identity=sb["I_bf"][:, :])
                return S.done("tensor", inst)
            t_trb = S.sec("tensor", trb)
            def evb(q, t_trb=t_trb, c=c):
                S.wait(q, "vector", t_trb)
                inst = q.tensor_copy(out=BT.ap().rearrange("p a b -> p (a b)"),
                                     in_=ps["pt"][:, 0:1024])
                return S.done("vector", inst)
            t_bt_ev = S.sec("vector", evb)
            t_xt_ev = t_bt_ev

            if pend_epi[0] is not None:
                pend_epi[0]()
                pend_epi[0] = None

            Sold = sb23["S_a"] if c % 2 == 0 else sb23["S_b"]
            Snew = sb23["S_b"] if c % 2 == 0 else sb23["S_a"]

            def emit_vgs(q, kv, b):
                S.wait(q, "vector", t_dma2[kv])
                S.wait(q, "vector", t_gso_use[b])
                toks = []
                for hh in range(2):
                    flv = kv * 2 + hh
                    hv = flv % NH
                    inst = q.scalar_tensor_tensor(out=GSO[b][:, hv, :], in0=G16[:, hv, :],
                                                  scalar=sb["normw_fm"][:, hv:hv + 1],
                                                  in1=RB[:, kv % 2, hh * 256:(hh + 1) * 256],
                                                  op0=ALU.mult, op1=ALU.mult)
                    toks.append(S.done("vector", inst))
                return toks

            def emit_rb_dma(kk):
                def dd1(q, kk=kk):
                    S.wait(q, "sync", t_srs[kk])
                    S.wait(q, "sync", t_dma2.get(kk - 2))
                    return S.dma(q, "sync", d["RSTDD"][kk % 2:kk % 2 + 1, :],
                                 RSTDS[0:1, kk % 2, :])
                t_dma1[kk] = S.sec("sync", dd1)
                def dd2(q, kk=kk):
                    S.wait(q, "sync", t_dma1[kk])
                    S.wait(q, "sync", t_gso.get(2 * (kk - 2) + 1))
                    return S.dma(q, "sync", RB[:, kk % 2, :],
                                 d["RSTDD"][kk % 2:kk % 2 + 1, :].partition_broadcast(128).squeeze(1))
                t_dma2[kk] = S.sec("sync", dd2)

            tok_mt = {}
            tok_cb = {}
            tok_ear = {}

            def e_mm1(h, c=c, b=b):
                fl = c * NH + h
                g4 = h // 4
                mt_bank = ps[f"p{h % 2}"]
                cb_bank = ps[f"p{2 + h % 2}"]
                def mm1(q, h=h, g4=g4, mt_bank=mt_bank, cb_bank=cb_bank, c=c, b=b, fl=fl):
                    S.wait(q, "tensor", t_ld[b])
                    S.wait(q, "tensor", t_mtd_done.get(fl - 2))
                    inst = None
                    for jt in range(2):
                        inst = q.matmul(out=mt_bank[:, jt * 256:(jt + 1) * 256],
                                        lhsT=XBCc[b][:, 4 + g4, jt * 128:(jt + 1) * 128],
                                        rhs=XBCc[b][:, 8 + h, :], start=True, stop=True)
                    tmt = S.done("tensor", inst)
                    S.wait(q, "tensor", t_cs_done.get(fl - 2))
                    q.matmul(out=cb_bank[:, 0:256], lhsT=sb23["IND"][:, h * 128:(h + 1) * 128],
                             rhs=sb23["cum_all"][:, c * 256:(c + 1) * 256], start=True, stop=True)
                    inst = q.matmul(out=cb_bank[:, 256:512], lhsT=sb23["IND"][:, h * 128:(h + 1) * 128],
                                    rhs=sb23["expcum_all"][:, c * 256:(c + 1) * 256], start=True, stop=True)
                    return tmt, S.done("tensor", inst)
                tok_mt[fl], tok_cb[fl] = S.sec("tensor", mm1)

            def e_vear(h, c=c):
                fl = c * NH + h
                sl = fl % 4
                cb_bank = ps[f"p{2 + h % 2}"]
                def vear(q, cb_bank=cb_bank, fl=fl, sl=sl):
                    S.wait(q, "vector", tok_cb[fl])
                    S.wait(q, "vector", t_e_prev.get(fl - 4))
                    q.tensor_tensor(out=EAR[:, sl, 0:256], in0=cb_bank[:, 0:256],
                                    in1=sb23["maskneg"][:, 0:256], op=ALU.add)
                    inst = q.tensor_tensor(out=EAR[:, sl, 256:512], in0=cb_bank[:, 0:256],
                                           in1=sb23["maskneg"][:, 256:512], op=ALU.add)
                    return S.done("vector", inst)
                tok_ear[fl] = S.sec("vector", vear)

            def e_sc(h, c=c):
                fl = c * NH + h
                sl = fl % 4
                t_e = S.sec("scalar", lambda q, h=h, fl=fl, sl=sl:
                            emit_scalar_merged(q, h, fl, sl, tok_ear[fl]))
                t_e_prev[fl] = t_e

            def e_vmid(h, c=c, b=b):
                fl = c * NH + h
                sl = fl % 4
                g4 = h // 4
                mt_bank = ps[f"p{h % 2}"]
                cb_bank = ps[f"p{2 + h % 2}"]
                def cidx(qi, jt, c=c, h=h):
                    return ((c * 2 + jt) * 3 + qi) * 16 + h
                def vmid(q, mt_bank=mt_bank, cb_bank=cb_bank,
                         h=h, g4=g4, b=b, fl=fl, sl=sl, cidx=cidx):
                    S.wait(q, "vector", t_e_prev[fl])
                    S.wait(q, "vector", tok_mt[fl])
                    S.wait(q, "vector", t_sc_prev.get(fl - 4))
                    inst = None
                    for jt in range(2):
                        inst = q.tensor_tensor(out=MTD[:, sl, jt * 256:(jt + 1) * 256],
                                               in0=mt_bank[:, jt * 256:(jt + 1) * 256],
                                               in1=ESB[:, sl, jt * 256:(jt + 1) * 256], op=ALU.mult)
                    t1 = S.done("vector", inst)
                    inst = q.tensor_tensor(out=CS[:, sl, :], in0=XBCc[b][:, 8 + h, :],
                                           in1=cb_bank[:, 256:512], op=ALU.mult)
                    t2 = S.done("vector", inst)
                    for jt in range(2):
                        inst = q.tensor_scalar(out=BW[:, sl, jt, :], in0=BT[:, g4 * 2 + jt, :],
                                               scalar1=sb23["colT"][:, cidx(1, jt):cidx(1, jt) + 1],
                                               scalar2=None, op0=ALU.mult)
                    return t1, t2, S.done("vector", inst)
                t_mtd, t_cs, t_bw = S.sec("vector", vmid)
                t_mtd_done[fl] = t_mtd
                t_cs_done[fl] = t_cs
                return t_bw

            def e_mm2(h, t_bw, c=c):
                fl = c * NH + h
                sl = fl % 4
                g4 = h // 4
                y_bank = ps[f"p{4 + h % 2}"]
                def mm2(q, h=h, g4=g4, y_bank=y_bank, t_bw=t_bw, c=c, fl=fl, sl=sl):
                    S.wait(q, "tensor", t_cs_done[fl])
                    S.wait(q, "tensor", t_yev.get(fl - 2))
                    S.wait(q, "tensor", t_sup.get(fl - NH) if c > 0 else C.t_szero)
                    q.matmul(out=y_bank[:, 0:256], lhsT=sb23["S_bf"][:, h * 128:(h + 1) * 128],
                             rhs=CS[:, sl, :], start=True, stop=False)
                    q.matmul(out=y_bank[:, 0:256], lhsT=XT[:, g4 * 2 + 0, :],
                             rhs=MTD[:, sl, 0:256], start=False, stop=False)
                    inst = q.matmul(out=y_bank[:, 0:256], lhsT=XT[:, g4 * 2 + 1, :],
                                    rhs=MTD[:, sl, 256:512], start=False, stop=True)
                    ty = S.done("tensor", inst)
                    S.wait(q, "tensor", t_bw)
                    q.matmul(out=y_bank[:, 256:384], lhsT=BW[:, sl, 0, :],
                             rhs=XT[:, g4 * 2 + 0, :], start=True, stop=False)
                    inst = q.matmul(out=y_bank[:, 256:384], lhsT=BW[:, sl, 1, :],
                                    rhs=XT[:, g4 * 2 + 1, :], start=False, stop=True)
                    return ty, S.done("tensor", inst)
                t_y, t_sc = S.sec("tensor", mm2)
                t_y_prev[fl] = t_y
                t_sc_prev[fl] = t_sc
                if h == NH - 1:
                    t_sc_last[c] = t_sc

            def e_vtail(h, c=c, b=b, Sold=Sold, Snew=Snew):
                fl = c * NH + h
                sl = fl % 4
                g4 = h // 4
                y_bank = ps[f"p{4 + h % 2}"]
                def cidx(qi, jt, c=c, h=h):
                    return ((c * 2 + jt) * 3 + qi) * 16 + h
                def vtail(q, h=h, g4=g4, y_bank=y_bank, b=b, fl=fl, sl=sl,
                          Sold=Sold, Snew=Snew, cidx=cidx):
                    S.wait(q, "vector", t_sc_prev[fl])
                    q.scalar_tensor_tensor(out=Y[:, sl, :], in0=XBCc[b][:, g4, :],
                                           scalar=sb["D_bc"][:, h:h + 1],
                                           in1=y_bank[:, 0:256], op0=ALU.mult, op1=ALU.add)
                    q.scalar_tensor_tensor(out=Snew[:, h * 128:(h + 1) * 128],
                                           in0=Sold[:, h * 128:(h + 1) * 128],
                                           scalar=sb23["colT"][:, cidx(2, 0):cidx(2, 0) + 1],
                                           in1=y_bank[:, 256:384], op0=ALU.mult, op1=ALU.add)
                    tsu = S.done("vector", None)
                    S.wait(q, "vector", t_ld_sg[b])
                    S.wait(q, "vector", t_gsq_tok.get(fl - NH))
                    inst = q.tensor_tensor(out=G16[:, h, :], in0=Y[:, sl, :],
                                           in1=SGc[b][:, h, :], op=ALU.mult)
                    tg = S.done("vector", inst)
                    kv = fl // 2 - 3
                    toks3 = emit_vgs(q, kv, b) if (h % 2 == 0 and h >= 6) else None
                    return tsu, tg, toks3
                t_suv, t_g, toks_vgs = S.sec("vector", vtail)
                t_su[fl] = t_suv
                t_yev[fl] = t_suv
                t_g_prev[fl] = t_g
                if toks_vgs is not None:
                    kv = fl // 2 - 3
                    t_gso[kv * 2], t_gso[kv * 2 + 1] = toks_vgs
                pend_sbf[0] = (fl, Snew)

            def e_stail(h, c=c):
                fl = c * NH + h
                sl = fl % 4
                k2 = fl // 2
                def stail(q, fl=fl, sl=sl, h=h, k2=k2):
                    S.wait(q, "scalar", t_g_prev[fl])
                    if pend_sbf[0] is not None:
                        pfl, Snew_ = pend_sbf[0]
                        ph = pfl % NH
                        q.copy(out=sb23["S_bf"][:, ph * 128:(ph + 1) * 128],
                               in_=Snew_[:, ph * 128:(ph + 1) * 128])
                        t_sup[pfl] = S.tok("scalar_c")
                        pend_sbf[0] = None
                    S.wait(q, "scalar", t_mm3.get(k2 - 2))
                    q.activation(out=GSQ[:, sl, :], in_=G16[:, h, :], func=AF.Square)
                    tq = S.tok("scalar_c")
                    te = ts = None
                    if h % 2 == 0 and h >= 4:
                        kk = k2 - 2
                        S.wait(q, "scalar", t_mm3[kk])
                        q.copy(out=RSUMS[0:1, kk % 2, :], in_=ps["p6"][0:1, 0:512])
                        te = S.tok("scalar_c")
                        S.wait(q, "scalar", t_dma1.get(kk - 2))
                        q.activation(out=RTMP[0:1, :], in_=RSUMS[0:1, kk % 2, :],
                                     func=AF.Ln, scale=1.0 / 128.0, bias=sb["cbias"][0:1, 0:1])
                        q.activation(out=RSTDS[0:1, kk % 2, :], in_=RTMP[0:1, :],
                                     func=AF.Exp, scale=-0.5)
                        ts = S.tok("scalar_c")
                    return tq, te, ts
                t_gsq, te_, ts_ = S.sec("scalar", stail)
                t_gsq_tok[fl] = t_gsq
                t_gsq_prev[fl] = t_gsq
                if te_ is not None:
                    t_evac[k2 - 2] = te_
                    t_srs[k2 - 2] = ts_
                    emit_rb_dma(k2 - 2)

            def e_mm3(j, c=c):
                def mm3(q, j=j):
                    S.wait(q, "tensor", t_gsq_tok[2 * j + 1])
                    S.wait(q, "tensor", t_evac.get(j - 1))
                    sg = (2 * j) % 4
                    inst = q.matmul(out=ps["p6"][0:1, 0:512], lhsT=sb["onescol"][:, :],
                                    rhs=GSQ[:, sg:sg + 2, :].rearrange("p a b -> p (a b)"),
                                    start=True, stop=True)
                    return S.done("tensor", inst)
                t_mm3[j] = S.sec("tensor", mm3)

            def e_evac_srs(kk):
                def sc_t(q, kk=kk):
                    S.wait(q, "scalar", t_mm3[kk])
                    q.copy(out=RSUMS[0:1, kk % 2, :], in_=ps["p6"][0:1, 0:512])
                    te = S.tok("scalar_c")
                    S.wait(q, "scalar", t_dma1.get(kk - 2))
                    q.activation(out=RTMP[0:1, :], in_=RSUMS[0:1, kk % 2, :],
                                 func=AF.Ln, scale=1.0 / 128.0, bias=sb["cbias"][0:1, 0:1])
                    q.activation(out=RSTDS[0:1, kk % 2, :], in_=RTMP[0:1, :],
                                 func=AF.Exp, scale=-0.5)
                    return te, S.tok("scalar_c")
                t_evac[kk], t_srs[kk] = S.sec("scalar", sc_t)

            e_mm1(0)
            e_vear(0)
            e_sc(0)
            e_mm1(1)
            e_vear(1)
            e_sc(1)
            vmid_bw = {0: e_vmid(0)}
            for h in range(NH):
                fl = c * NH + h
                k2 = fl // 2
                if h + 2 < NH:
                    e_mm1(h + 2)
                if h % 2 == 1 and h >= 3:
                    e_mm3(k2 - 1)
                e_mm2(h, vmid_bw.pop(h))
                if h + 2 < NH:
                    e_vear(h + 2)
                if h + 1 < NH:
                    vmid_bw[h + 1] = e_vmid(h + 1)
                e_vtail(h)
                if h + 2 < NH:
                    e_sc(h + 2)
                e_stail(h)

            def epilogue(c=c, b=b, emit_vgs=emit_vgs, emit_rb_dma=emit_rb_dma,
                         e_evac_srs=e_evac_srs, e_mm3=e_mm3):
                k5, k6, k7 = c * 8 + 5, c * 8 + 6, c * 8 + 7
                e_evac_srs(k6)
                emit_rb_dma(k6)
                e_mm3(k7)
                e_evac_srs(k7)
                def v_t1(q, b=b, k5=k5):
                    t_gso[k5 * 2], t_gso[k5 * 2 + 1] = emit_vgs(q, k5, b)
                S.sec("vector", v_t1)
                emit_rb_dma(k7)
                def v_t2(q, b=b, k6=k6, k7=k7):
                    for kv in (k6, k7):
                        t_gso[kv * 2], t_gso[kv * 2 + 1] = emit_vgs(q, kv, b)
                S.sec("vector", v_t2)

                t_xbc_use[b] = t_yev[c * NH + NH - 1]
                t_sg_use[b] = t_g_prev[c * NH + NH - 1]
                def sto(q, c=c, b=b):
                    for kv in range(c * 8, c * 8 + 8):
                        S.wait(q, "sync", t_gso[kv * 2 + 1])
                    return S.dma(q, "sync",
                                 d["GS"][:, :, c * QCH:(c + 1) * QCH].rearrange("k f t -> f k t"),
                                 GSO[b][:, :, :])
                t_store = S.sec("sync", sto)
                t_gso_use[b] = t_store
            pend_epi[0] = epilogue
        pend_epi[0]()
        pend_epi[0] = None
        pend_sbf[0] = None
        S.flush()


def stage4(C):
    nc, S, sb, ps, d = C.nc, C.S, C.sb, C.ps, C.d
    with contextlib.ExitStack() as es4:
        WO = es4.enter_context(nc.sbuf_tensor("s4_wo", [128, 16, 2048], BF16))
        GSG = [es4.enter_context(nc.sbuf_tensor(f"s4_gs{i}", [128, 16, TGRP], BF16)) for i in range(2)]
        ATT = es4.enter_context(nc.sbuf_tensor("s4_att", [128, 2, 4, 16, 128], BF16))
        XTI = [es4.enter_context(nc.sbuf_tensor(f"s4_x{i}", [128, 2048], F32)) for i in range(2)]
        ATTT = [es4.enter_context(nc.sbuf_tensor(f"s4_at{i}", [128, 16, 128], BF16)) for i in range(2)]
        X1O = [es4.enter_context(nc.sbuf_tensor(f"s4_o{i}", [128, 2048], F32)) for i in range(2)]
        def ldw(q):
            return [S.dma(q, "gpsimd", WO[:, kc, :], d["w_out"][kc * 128:(kc + 1) * 128, :])
                    for kc in range(16)]
        t_w = S.sec("gpsimd", ldw)
        t_gs_use = [None, None]
        t_ev = {}
        t_tr_last_p = [None, None]
        t_xti_use = [None, None]
        t_attt_use = [None, None]
        t_x1o_use = [None, None]
        for g in range(NGRP):
            b = g % 2
            def ldg(q, g=g, b=b):
                S.wait(q, "gpsimd", t_gs_use[b])
                return S.dma(q, "gpsimd", GSG[b][:, :, :],
                             d["GS"][:, :, g * TGRP:(g + 1) * TGRP].rearrange("k f t -> f k t"))
            t_gs = S.sec("gpsimd", ldg)
            for oc in range(16):
                goc = g * 16 + oc
                bank = ps[f"p{oc % 4}"]
                def mm(q, oc=oc, goc=goc, bank=bank, t_gs=t_gs, b=b):
                    S.wait(q, "tensor", t_w)
                    S.wait(q, "tensor", t_gs)
                    S.wait(q, "tensor", t_ev.get(goc - 4))
                    inst = None
                    for kc in range(16):
                        inst = q.matmul(out=bank[:, :], lhsT=WO[:, kc, oc * 128:(oc + 1) * 128],
                                        rhs=GSG[b][:, kc, :], start=(kc == 0), stop=(kc == 15))
                    return S.done("tensor", inst)
                t_mm = S.sec("tensor", mm)
                def ev(q, oc=oc, bank=bank, t_mm=t_mm, b=b):
                    S.wait(q, "scalar", t_mm)
                    S.wait(q, "scalar", t_tr_last_p[b])
                    inst = q.copy(out=ATT[:, b, :, oc, :], in_=bank[:, :].rearrange("p (a b) -> p a b", a=4))
                    return S.done("scalar", inst)
                t_ev[goc] = S.sec("scalar", ev)
            t_gs_use[b] = t_ev[g * 16 + 15]
            for tt in range(4):
                gtt = g * 4 + tt
                b2 = tt % 2
                def ldx(q, gtt=gtt, b2=b2):
                    S.wait(q, "gpsimd", t_xti_use[b2])
                    return S.dma(q, "gpsimd", XTI[b2][:, :], d["x"][gtt * 128:(gtt + 1) * 128, :])
                t_x = S.sec("gpsimd", ldx)
                def trp(q, tt=tt, b2=b2, b=b, g=g):
                    for oc in range(16):
                        S.wait(q, "scalar", t_ev[g * 16 + oc])
                    S.wait(q, "scalar", t_attt_use[b2])
                    return S.dma(q, "scalar", ATTT[b2][:, :, :], ATT[:, b, tt, :, :],
                                 transpose=True)
                t_tr = S.sec("scalar", trp)
                if tt == 3:
                    t_tr_last_p[b] = t_tr
                def vadd(q, b2=b2, t_x=t_x, t_tr=t_tr):
                    S.wait(q, "vector", t_x)
                    S.wait(q, "vector", t_tr)
                    S.wait(q, "vector", t_x1o_use[b2])
                    inst = q.tensor_tensor(out=X1O[b2][:, :], in0=XTI[b2][:, :],
                                           in1=ATTT[b2].ap().rearrange("p a b -> p (a b)"),
                                           op=ALU.add)
                    return S.done("vector", inst)
                t_add = S.sec("vector", vadd)
                t_xti_use[b2] = t_add
                t_attt_use[b2] = t_add
                def sto(q, gtt=gtt, b2=b2, t_add=t_add):
                    S.wait(q, "sync", t_add)
                    return S.dma(q, "sync", d["X1"][gtt * 128:(gtt + 1) * 128, :], X1O[b2][:, :])
                t_x1o_use[b2] = S.sec("sync", sto)
        S.flush() if (g == NGRP - 1 and tt == 3) else None


def stage56(C):
    nc, S, sb, ps, d = C.nc, C.S, C.sb, C.ps, C.d
    LT = [64.0, 32.0, 16.0, 8.0, 4.0, 2.0, 1.0]
    with contextlib.ExitStack() as es5:
        WQ = es5.enter_context(nc.sbuf_tensor("s5_wq", [128, 16, 2048], BF16))
        XTI = [es5.enter_context(nc.sbuf_tensor(f"s5_x{i}", [128, 2048], F32)) for i in range(2)]
        XTI2 = es5.enter_context(nc.sbuf_tensor("s5_x2", [128, 2048], F32))
        UU = es5.enter_context(nc.sbuf_tensor("s5_uu", [128, 2048], F32))
        UA = es5.enter_context(nc.sbuf_tensor("s5_ua", [128, 2048], F32))
        UB = es5.enter_context(nc.sbuf_tensor("s5_ub", [128, 2048], F32))
        MB = es5.enter_context(nc.sbuf_tensor("s5_mb", [128, 2048], F32))
        SGN = es5.enter_context(nc.sbuf_tensor("s5_sgn", [128, 2048], BF16))
        XQT = es5.enter_context(nc.sbuf_tensor("s5_xqt", [128, 2048], BF16))
        XQF = es5.enter_context(nc.sbuf_tensor("s5_xqf", [128, 2, 4, 16, 128], BF16))
        FF = es5.enter_context(nc.sbuf_tensor("s5_ff", [128, 4, 16, 128], BF16))
        FFT = [es5.enter_context(nc.sbuf_tensor(f"s5_ft{i}", [128, 16, 128], BF16)) for i in range(4)]
        OUTB = es5.enter_context(nc.sbuf_tensor("s5_ob", [128, 2048], F32))
        SCC = es5.enter_context(nc.sbuf_tensor("s5_scc", [128, 16], F32))
        SM = es5.enter_context(nc.sbuf_tensor("s5_sm", [128, 28], F32))
        BNS = es5.enter_context(nc.sbuf_tensor("s5_bns", [128, 4, 6], F32))

        t_wq = S.sec("scalar", lambda q: S.dma(q, "scalar", WQ[:, :, :],
                     d["WQT"].rearrange("ic i oc o -> i ic (oc o)")))

        t_x_use = [None, None]
        t_ffev = {}
        t_lad = None
        t_lad2 = None
        t_qtr = None
        t_xqf_use = [None, None]
        t_ff_use = None
        t_ftr_use = [None, None, None, None]
        t_out_use = None
        t_sgn_use = None
        t_ft_last = None
        t_lad_v = None
        def emit_out_block(gp, ffev_snap):
            def ftr(q, gp=gp, ffev_snap=ffev_snap):
                for oc in range(16):
                    S.wait(q, "scalar", ffev_snap[oc])
                toks = []
                for tt in range(4):
                    S.wait(q, "scalar", t_ftr_use[tt])
                    toks.append(S.dma(q, "scalar", FFT[tt][:, :, :], FF[:, tt, :, :],
                                      transpose=True))
                return toks
            t_fts = S.sec("scalar", ftr)
            st["t_ft_last"] = t_fts[3]
            for tt in range(4):
                gtt = gp * 4 + tt
                b2 = tt % 2
                def ldx2(q, gtt=gtt):
                    S.wait(q, "sync", st["t_x2_use"])
                    return S.dma(q, "sync", XTI2[:, :], d["X1"][gtt * 128:(gtt + 1) * 128, :])
                t_x2 = S.sec("sync", ldx2)
                def vout(q, gtt=gtt, tt=tt, t_x2=t_x2, t_ft=t_fts[tt]):
                    S.wait(q, "vector", t_x2)
                    S.wait(q, "vector", t_ft)
                    S.wait(q, "vector", st["t_out_use"])
                    inst = q.scalar_tensor_tensor(out=OUTB[:, :],
                                                  in0=FFT[tt].ap().rearrange("p a b -> p (a b)"),
                                                  scalar=SCC[:, gtt % 16:gtt % 16 + 1],
                                                  in1=XTI2[:, :], op0=ALU.mult, op1=ALU.add)
                    return S.done("vector", inst)
                t_o = S.sec("vector", vout)
                st["t_x2_use"] = t_o
                t_ftr_use[tt] = t_o
                def sto(q, gtt=gtt, t_o=t_o):
                    S.wait(q, "sync", t_o)
                    return S.dma(q, "sync", d["out"][gtt * 128:(gtt + 1) * 128, :], OUTB[:, :])
                st["t_out_use"] = S.sec("sync", sto)

        st = {"t_ft_last": None, "t_out_use": None, "t_x2_use": None}
        ffev_snaps = {}
        qtr_snaps = {}
        for g in range(NGRP):
            t_qtr_tiles = []
            for tt in range(4):
                gtt = g * 4 + tt
                b = tt % 2
                c0 = 12 * b
                def ldx(q, gtt=gtt, b=b):
                    S.wait(q, "gpsimd", t_x_use[b])
                    return S.dma(q, "gpsimd", XTI[b][:, :], d["X1"][gtt * 128:(gtt + 1) * 128, :])
                t_x = S.sec("gpsimd", ldx)
                def vstats(q, b=b, c0=c0, t_x=t_x):
                    S.wait(q, "vector", t_x)
                    for j in range(4):
                        q.bn_stats(out=BNS[:, j, :], in_=XTI[b][:, j * 512:(j + 1) * 512])
                    q.bn_aggr(out=SM[:, c0 + 2:c0 + 4],
                              in_=BNS.ap().rearrange("p a b -> p (a b)"))
                    q.tensor_reduce(out=SM[:, c0 + 7:c0 + 8], in_=XTI[b][:, :], axis=AX.X, op=ALU.max)
                    q.tensor_reduce(out=SM[:, c0 + 8:c0 + 9], in_=XTI[b][:, :], axis=AX.X, op=ALU.min)
                    inst = q.tensor_scalar(out=SM[:, c0 + 12:c0 + 13], in0=SM[:, c0 + 2:c0 + 3],
                                           scalar1=-1.0, scalar2=None, op0=ALU.mult)
                    return S.done("vector", inst)
                t_var = S.sec("vector", vstats)
                def srstd(q, c0=c0, b=b, t_var=t_var):
                    S.wait(q, "scalar", t_var)
                    q.activation(out=SM[:, c0 + 5:c0 + 6], in_=SM[:, c0 + 3:c0 + 4], func=AF.Ln,
                                 bias=sb["cbias"][:, 0:1])
                    q.activation(out=SM[:, c0 + 6:c0 + 7], in_=SM[:, c0 + 5:c0 + 6],
                                 func=AF.Exp, scale=-0.5)
                    t_re = S.tok("scalar_c")
                    S.wait(q, "scalar", t_sgn_use)
                    q.activation(out=SGN[:, :], in_=XTI[b][:, :], func=AF.Sign,
                                 bias=SM[:, c0 + 12:c0 + 13])
                    q.activation(out=UA[:, :], in_=XTI[b][:, :], func=AF.Abs,
                                 bias=SM[:, c0 + 12:c0 + 13])
                    return t_re, S.tok("scalar_c")
                t_re, t_rstd = S.sec("scalar", srstd)
                t_x_use[b] = t_rstd
                def vsc(q, c0=c0, gtt=gtt, t_re=t_re):
                    S.wait(q, "vector", t_re)
                    q.tensor_tensor(out=SM[:, c0 + 9:c0 + 10], in0=SM[:, c0 + 7:c0 + 8],
                                    in1=SM[:, c0 + 2:c0 + 3], op=ALU.subtract)
                    q.tensor_tensor(out=SM[:, c0 + 10:c0 + 11], in0=SM[:, c0 + 2:c0 + 3],
                                    in1=SM[:, c0 + 8:c0 + 9], op=ALU.subtract)
                    q.tensor_tensor(out=SM[:, c0 + 9:c0 + 10], in0=SM[:, c0 + 9:c0 + 10],
                                    in1=SM[:, c0 + 10:c0 + 11], op=ALU.max)
                    q.tensor_tensor(out=SM[:, c0 + 9:c0 + 10], in0=SM[:, c0 + 9:c0 + 10],
                                    in1=SM[:, c0 + 6:c0 + 7], op=ALU.mult)
                    q.tensor_single_scalar(out=SM[:, c0 + 9:c0 + 10], in_=SM[:, c0 + 9:c0 + 10],
                                           scalar=1e-8, op=ALU.max)
                    q.tensor_scalar(out=SCC[:, gtt % 16:gtt % 16 + 1], in0=SM[:, c0 + 9:c0 + 10],
                                    scalar1=sb["smalls"][:, 35:36], scalar2=1.0 / 127.0,
                                    op0=ALU.mult, op1=ALU.mult)
                    q.reciprocal(out=SM[:, c0 + 10:c0 + 11], in_=SM[:, c0 + 9:c0 + 10])
                    q.tensor_scalar(out=SM[:, c0 + 10:c0 + 11], in0=SM[:, c0 + 10:c0 + 11],
                                    scalar1=127.0, scalar2=None, op0=ALU.mult)
                    inst = q.tensor_tensor(out=SM[:, c0 + 11:c0 + 12], in0=SM[:, c0 + 6:c0 + 7],
                                           in1=SM[:, c0 + 10:c0 + 11], op=ALU.mult)
                    return S.done("vector", inst)
                t_s3 = S.sec("vector", vsc)
                def vladprep(q, b=b, c0=c0, t_s3=t_s3, t_rstd=t_rstd):
                    S.wait(q, "vector", t_s3)
                    S.wait(q, "vector", t_rstd)
                    S.wait(q, "vector", t_lad)
                    S.wait(q, "vector", t_lad2)
                    S.wait(q, "vector", t_qtr)
                    inst = q.tensor_scalar(out=UU[:, :], in0=UA[:, :],
                                           scalar1=SM[:, c0 + 11:c0 + 12], scalar2=None,
                                           op0=ALU.mult)
                    return S.done("vector", inst)
                t_prep = S.sec("vector", vladprep)
                SPL = 2048
                def ladv(q, t_prep=t_prep):
                    lo, hi, w = 0, SPL, SPL
                    S.wait(q, "vector", t_prep)
                    cur = UU
                    bufs = [UA, UB]
                    for li, T in enumerate(LT):
                        q.tensor_scalar(out=MB[:, lo:hi], in0=cur[:, lo:hi],
                                        scalar1=T - 0.5, scalar2=-T, op0=ALU.is_ge, op1=ALU.mult)
                        nxt = bufs[li % 2]
                        q.tensor_tensor(out=nxt[:, lo:hi], in0=cur[:, lo:hi],
                                        in1=MB[:, lo:hi], op=ALU.add)
                        cur = nxt
                    q.tensor_tensor(out=MB[:, lo:hi], in0=UU[:, lo:hi], in1=cur[:, lo:hi],
                                    op=ALU.subtract)
                    inst = q.tensor_tensor(out=XQT[:, lo:hi], in0=MB[:, lo:hi],
                                           in1=SGN[:, lo:hi], op=ALU.mult)
                    return S.done("vector", inst)
                t_lad = S.sec("vector", ladv)
                t_lad2 = t_lad
                t_lad_v = t_lad
                t_sgn_use = t_lad
                def qtr(q, tt=tt, g=g, t_lad=t_lad, t_lad2=t_lad2):
                    S.wait(q, "scalar", t_lad)
                    S.wait(q, "scalar", t_lad2)
                    S.wait(q, "scalar", t_xqf_use[g % 2])
                    return S.dma(q, "scalar", XQF[:, g % 2, tt, :, :], XQT[:, :],
                                 transpose=True)
                t_qtr = S.sec("scalar", qtr)
                t_qtr_tiles.append(t_qtr)
            qtr_snaps[g] = tuple(t_qtr_tiles)
            if g >= 2:
                emit_out_block(g - 2, ffev_snaps[g - 2])
            if g < 1:
                continue
            gm = g - 1
            for oc in range(16):
                goc = gm * 16 + oc
                bank = ps[f"p{oc % 2}"]
                def mm(q, oc=oc, goc=goc, bank=bank, g=gm, toks=qtr_snaps[gm]):
                    S.wait(q, "tensor", t_wq)
                    for tk in toks:
                        S.wait(q, "tensor", tk)
                    S.wait(q, "tensor", t_ffev.get(goc - 2))
                    inst = None
                    for ic in range(16):
                        inst = q.matmul(out=bank[:, :], lhsT=WQ[:, ic, oc * 128:(oc + 1) * 128],
                                        rhs=XQF[:, g % 2, :, ic, :], start=(ic == 0), stop=(ic == 15))
                    return S.done("tensor", inst)
                t_mm = S.sec("tensor", mm)
                def ev(q, oc=oc, bank=bank, t_mm=t_mm):
                    S.wait(q, "scalar", t_mm)
                    S.wait(q, "scalar", st["t_ft_last"])
                    inst = q.copy(out=FF[:, :, oc, :], in_=bank[:, :].rearrange("p (a b) -> p a b", a=4))
                    return S.done("scalar", inst)
                t_ffev[goc] = S.sec("scalar", ev)
            t_xqf_use[gm % 2] = t_ffev[goc]
            ffev_snaps[gm] = {oc: t_ffev[gm * 16 + oc] for oc in range(16)}
        emit_out_block(NGRP - 2, ffev_snaps[NGRP - 2])
        gm = NGRP - 1
        for oc in range(16):
            goc = gm * 16 + oc
            bank = ps[f"p{oc % 2}"]
            def mm(q, oc=oc, goc=goc, bank=bank, g=gm, toks=qtr_snaps[gm]):
                S.wait(q, "tensor", t_wq)
                for tk in toks:
                    S.wait(q, "tensor", tk)
                S.wait(q, "tensor", t_ffev.get(goc - 2))
                inst = None
                for ic in range(16):
                    inst = q.matmul(out=bank[:, :], lhsT=WQ[:, ic, oc * 128:(oc + 1) * 128],
                                    rhs=XQF[:, g % 2, :, ic, :], start=(ic == 0), stop=(ic == 15))
                return S.done("tensor", inst)
            t_mm = S.sec("tensor", mm)
            def ev(q, oc=oc, bank=bank, t_mm=t_mm):
                S.wait(q, "scalar", t_mm)
                S.wait(q, "scalar", st["t_ft_last"])
                inst = q.copy(out=FF[:, :, oc, :], in_=bank[:, :].rearrange("p (a b) -> p a b", a=4))
                return S.done("scalar", inst)
            t_ffev[goc] = S.sec("scalar", ev)
        ffev_snaps[gm] = {oc: t_ffev[gm * 16 + oc] for oc in range(16)}
        emit_out_block(NGRP - 1, ffev_snaps[NGRP - 1])
        S.flush()


_CACHE = {}

def kernel(L0_attn_hidden_states, L0_attn_W_in, L0_attn_conv_weight,
           L0_attn_conv_bias, L0_attn_A_log, L0_attn_D, L0_attn_dt_bias,
           L0_attn_norm_weight, L0_attn_W_out, L0_ffn_weight):
    x = np.ascontiguousarray(L0_attn_hidden_states, dtype=np.float32)
    key = (tuple(DEBUG_OUTS), tuple(BUILD_STAGES))
    nc = _CACHE.get(key)
    if nc is None:
        nc = build_program(tuple(DEBUG_OUTS), tuple(BUILD_STAGES))
        _CACHE[key] = nc
    base = {
        "w_in": np.ascontiguousarray(L0_attn_W_in, dtype=np.float32),
        "conv_w": np.ascontiguousarray(L0_attn_conv_weight, dtype=np.float32),
        "conv_b": np.ascontiguousarray(L0_attn_conv_bias, dtype=np.float32),
        "a_log": np.ascontiguousarray(L0_attn_A_log, dtype=np.float32),
        "d_vec": np.ascontiguousarray(L0_attn_D, dtype=np.float32),
        "dt_bias": np.ascontiguousarray(L0_attn_dt_bias, dtype=np.float32),
        "normw": np.ascontiguousarray(L0_attn_norm_weight, dtype=np.float32),
        "w_out": np.ascontiguousarray(L0_attn_W_out, dtype=np.float32),
        "w_ffn": np.ascontiguousarray(L0_ffn_weight, dtype=np.float32),
    }
    in_maps = [dict(base, x=np.ascontiguousarray(x[i])) for i in range(8)]
    res = run_bass_kernel_spmd(nc, in_maps, core_ids=list(range(8)))
    _CACHE["last_results"] = res
    return np.stack([np.asarray(r["out"]) for r in res.results], axis=0)


# version 19
# speedup vs baseline: 1.0431x; 1.0058x over previous
import contextlib
import numpy as np

import concourse.bass as bass
import concourse.mybir as mybir
from concourse.bass_utils import run_bass_kernel_spmd

F32 = mybir.dt.float32
BF16 = mybir.dt.bfloat16
AF = mybir.ActivationFunctionType
ALU = mybir.AluOpType
AX = mybir.AxisListType

LSEQ = 2048
DM = 2048
PROJ = 5136
CONVD = 3072
NH = 16
QCH = 256
NCH = LSEQ // QCH
KC = 16
TGRP = 512
NGRP = LSEQ // TGRP
EPS = 1e-5
NEGBIG = -1.0e30

ENG_SEM = {"tensor": "tensor", "vector": "vector", "scalar": "scalar_c",
           "gpsimd": "gpsimd_c"}
DMA_SEM = {"sync": "d_sync", "gpsimd": "d_gpsimd", "scalar": "d_scalar"}

CHAIN_OPS = {"tensor_scalar", "tensor_single_scalar", "tensor_tensor", "tensor_copy",
             "tensor_add", "tensor_sub", "tensor_mul", "tensor_max", "tensor_relu",
             "scalar_tensor_tensor", "tensor_tensor_scan", "tensor_reduce",
             "reduce_sum", "reduce_max", "reciprocal", "memset", "memzero", "iota",
             "activation", "copy", "select", "copy_predicated", "add", "mul", "sign",
             "sqrt", "square", "bn_stats", "bn_aggr", "stream_shuffle"}


class RInst:
    __slots__ = ("entry",)
    def __init__(self, entry):
        self.entry = entry

    def then_inc(self, sem, n):
        self.entry[3].append((sem, n))
        return self


class RQ:
    def __init__(self, S, eng):
        self._S = S
        self._eng = eng

    def __getattr__(self, name):
        S, eng = self._S, self._eng
        def rec(*a, **k):
            entry = [name, a, k, []]
            S.pending[eng].append(entry)
            return RInst(entry)
        return rec


class QW:
    def __init__(self, S, eng, q):
        self._S = S
        self._eng = eng
        self._q = q

    def __getattr__(self, name):
        f = getattr(self._q, name)
        if name not in CHAIN_OPS:
            return f
        S, eng, q = self._S, self._eng, self._q
        sem = ENG_SEM[eng]
        def wrap(*a, **k):
            S.wait(q, eng, (sem, S.cnt[sem]))
            inst = f(*a, **k)
            S.cnt[sem] += 1
            inst.then_inc(S.sems[sem], 1)
            return inst
        return wrap

DEBUG_OUTS = []
BUILD_STAGES = ["s0", "s1", "s2a", "s2b", "s2g", "s3", "s4", "s56"]


class Sched:
    def __init__(self, nc, es):
        self.nc = nc
        self.block = es.enter_context(nc.Block())
        self.sems = {}
        self.cnt = {}
        self.waited = {}
        self.bar_expected = 0
        self.bar_snapshot = {}
        self.stage_last = {}
        self.stage_dma = {}
        self.stage_first_wait_done = set()
        self.pending = {e: [] for e in ("tensor", "vector", "scalar", "gpsimd", "sync")}
        names = ["tensor", "vector", "scalar_c", "gpsimd_c"]
        for qn in ("sync", "gpsimd", "scalar"):
            for k in range(8):
                names.append(f"d_{qn}_{k}")
        for name in names:
            self.sems[name] = es.enter_context(nc.semaphore(name))
            self.cnt[name] = 0
        self.dma_rot = {"sync": 0, "gpsimd": 0, "scalar": 0}

    def sec(self, eng, body):
        q = RQ(self, eng)
        if eng not in self.stage_first_wait_done:
            self.stage_first_wait_done.add(eng)
            for sname, v in self.bar_snapshot.items():
                if v > 0 and self.waited.get((eng, sname), 0) < v:
                    q.wait_ge(self.sems[sname], v)
                    self.waited[(eng, sname)] = v
        qq = QW(self, eng, q) if eng in ("vector", "scalar", "gpsimd") else q
        return body(qq)

    def flush(self):
        for eng, entries in self.pending.items():
            if not entries:
                continue
            def fn(q, entries=tuple(entries)):
                for name, a, k, incs in entries:
                    inst = getattr(q, name)(*a, **k)
                    for sem, n in incs:
                        inst.then_inc(sem, n)
            getattr(self.block, eng)(fn)
            self.pending[eng] = []

    def wait(self, q, eng, token):
        if token is None:
            return
        if isinstance(token, (list, tuple)) and token and isinstance(token[0], (list, tuple)):
            for tk in token:
                self.wait(q, eng, tk)
            return
        sem, val = token
        if val <= 0:
            return
        k = (eng, sem)
        if self.waited.get(k, 0) >= val:
            return
        if isinstance(q, QW):
            q = q._q
        q.wait_ge(self.sems[sem], val)
        self.waited[k] = val

    dsc = None
    _dsc_init = False

    def _dsc_ensure(self, q):
        if not self._dsc_init:
            self._dsc_init = True
            q.memzero(self.dsc[0:1, 0:1])
            q.activation(out=self.dsc[0:1, 1:2], in_=self.dsc[0:1, 0:1], func=AF.Exp)

    def dum_pair(self, q):
        self._dsc_ensure(q)
        q.activation(out=self.dsc[0:1, 1:2], in_=self.dsc[0:1, 0:1], func=AF.Exp)
        q.activation(out=self.dsc[0:1, 0:1], in_=self.dsc[0:1, 1:2], func=AF.Ln)

    def dum_ln(self, q):
        self._dsc_ensure(q)
        q.activation(out=self.dsc[0:1, 0:1], in_=self.dsc[0:1, 1:2], func=AF.Ln)

    def dum_silu(self, q):
        self._dsc_ensure(q)
        q.activation(out=self.dsc[0:1, 2:3], in_=self.dsc[0:1, 0:1], func=AF.Silu)

    def tok(self, sem):
        return (sem, self.cnt[sem])

    def done(self, eng, inst):
        sem = ENG_SEM[eng]
        if eng == "tensor":
            self.cnt[sem] += 1
            inst.then_inc(self.sems[sem], 1)
        return (sem, self.cnt[sem])

    def note(self, eng, inst):
        self.stage_last[eng] = inst
        return inst

    def dma(self, q, eng, out, in_, transpose=False):
        if isinstance(q, QW):
            q = q._q
        k = self.dma_rot[eng]
        self.dma_rot[eng] = (k + 1) % 8
        sem = f"d_{eng}_{k}"
        self.wait(q, eng, (sem, self.cnt[sem]))
        if transpose:
            inst = q.dma_start_transpose(out=out, in_=in_)
        else:
            inst = q.dma_start(out=out, in_=in_)
        self.cnt[sem] += 16
        inst.then_inc(self.sems[sem], 16)
        return (sem, self.cnt[sem])

    def stage_end(self):
        self.flush()
        self.stage_first_wait_done = set()
        self.bar_snapshot = dict(self.cnt)


class Ctx:
    pass


def build_program(debug_outs=(), stages=None):
    stages = list(stages if stages is not None else BUILD_STAGES)
    nc = bass.Bass("TRN2", target_bir_lowering=False)
    C = Ctx()
    C.nc = nc
    d = {}
    d["x"] = nc.dram_tensor("x", [LSEQ, DM], F32, kind="ExternalInput").ap()
    d["w_in"] = nc.dram_tensor("w_in", [DM, PROJ], F32, kind="ExternalInput").ap()
    d["conv_w"] = nc.dram_tensor("conv_w", [CONVD, 4], F32, kind="ExternalInput").ap()
    d["conv_b"] = nc.dram_tensor("conv_b", [CONVD], F32, kind="ExternalInput").ap()
    d["a_log"] = nc.dram_tensor("a_log", [NH], F32, kind="ExternalInput").ap()
    d["d_vec"] = nc.dram_tensor("d_vec", [NH], F32, kind="ExternalInput").ap()
    d["dt_bias"] = nc.dram_tensor("dt_bias", [NH], F32, kind="ExternalInput").ap()
    d["normw"] = nc.dram_tensor("normw", [DM], F32, kind="ExternalInput").ap()
    d["w_out"] = nc.dram_tensor("w_out", [DM, DM], F32, kind="ExternalInput").ap()
    d["w_ffn"] = nc.dram_tensor("w_ffn", [DM, DM], F32, kind="ExternalInput").ap()
    d["out"] = nc.dram_tensor("out", [LSEQ, DM], F32, kind="ExternalOutput").ap()
    def scratch(name, shape, dt_):
        kind = "ExternalOutput" if name in debug_outs else "Internal"
        nm = ("dbg_" + name) if name in debug_outs else name
        return nc.dram_tensor(nm, shape, dt_, kind=kind).ap()
    d["LNF"] = scratch("LNF", [KC, 128, LSEQ], BF16)
    d["XBC"] = scratch("XBC", [24, 128, LSEQ], BF16)
    d["SG"] = scratch("SG", [KC, 128, LSEQ], BF16)
    d["GS"] = scratch("GS", [KC, 128, LSEQ], BF16)
    d["X1"] = scratch("X1", [LSEQ, DM], F32)
    d["WQT"] = scratch("WQT", [KC, 128, KC, 128], BF16)
    d["RSTDD"] = scratch("RSTDD", [2, 512], F32)
    C.d = d

    with contextlib.ExitStack() as es:
        S = Sched(nc, es)
        C.S = S
        sb = {}
        C.sb = sb
        ps = {}
        C.ps = ps
        def sbuf(name, shape, dt_):
            sb[name] = es.enter_context(nc.sbuf_tensor(name, shape, dt_))
            return sb[name]
        sbuf("I_f32", [128, 128], F32)
        sbuf("I_bf", [128, 128], BF16)
        sbuf("onesrow", [1, 128], F32)
        sbuf("onescol", [128, 1], F32)
        sbuf("ones16", [16, 256], F32)
        sbuf("D_bc", [128, 16], F32)
        sbuf("normw_fm", [128, 16], F32)
        sbuf("convw_col", [128, 96], F32)
        sbuf("convb_col", [128, 24], F32)
        sbuf("acol", [16, 1], F32)
        sbuf("dtb_col", [16, 1], F32)
        sbuf("drow", [1, 16], F32)
        sbuf("smalls", [128, 48], F32)
        sbuf("cbias", [128, 2], F32)
        sbuf("row24", [24, 128], F32)
        sbuf("zb", [128, 72], BF16)
        sbuf("pcol", [128, 1], F32)
        sbuf("irow", [128, 512], F32)
        sbuf("dsc", [1, 3], F32)
        S.dsc = sb["dsc"]
        for i in range(7):
            ps[f"p{i}"] = es.enter_context(nc.psum_tensor(f"p{i}", [128, 512], F32))
        ps["pt"] = es.enter_context(nc.psum_tensor("pt", [128, 1024], BF16))

        if "s0" in stages or "s1" in stages:
            stage0(C)
            S.stage_end()
        with contextlib.ExitStack() as es23:
            sb23 = {}
            C.sb23 = sb23
            def sbuf23(name, shape, dt_):
                sb23[name] = es23.enter_context(nc.sbuf_tensor(name, shape, dt_))
                return sb23[name]
            if any(s in stages for s in ("s2a", "s2b", "s2g", "s3")):
                sbuf23("IND", [16, KC * 128], F32)
                sbuf23("maskneg", [128, 512], F32)
                sbuf23("cum_all", [16, LSEQ], F32)
                sbuf23("expcum_all", [16, LSEQ], F32)
                sbuf23("colT", [128, NCH * 2 * 3 * 16], F32)
                sbuf23("S_a", [128, NH * 128], F32)
                sbuf23("S_b", [128, NH * 128], F32)
                sbuf23("S_bf", [128, NH * 128], BF16)
                stage23_consts(C)
            if "s2a" in stages:
                stage2(C, 0)
                S.stage_end()
            if "s2b" in stages:
                stage2(C, 1)
                S.stage_end()
            if "s2g" in stages:
                stage2g(C)
                S.stage_end()
            if "s3" in stages:
                stage3(C)
                S.stage_end()
        if "s4" in stages:
            stage4(C)
            S.stage_end()
        if "s56" in stages:
            stage56(C)
            S.stage_end()
        def drain(q):
            for sname, v in S.cnt.items():
                if sname.startswith("d_") and v > 0 and S.waited.get(("sync", sname), 0) < v:
                    q.wait_ge(S.sems[sname], v)
                    S.waited[("sync", sname)] = v
        S.sec("sync", drain)
        S.flush()
    return nc


def stage0(C):
    nc, S, sb, ps, d = C.nc, C.S, C.sb, C.ps, C.d

    def ld(q):
        t = {}
        t["convw"] = S.dma(q, "sync", sb["convw_col"].ap().rearrange("p (cc k) -> p cc k", k=4),
                           d["conv_w"].rearrange("(cc p) k -> p cc k", p=128))
        t["cb"] = S.dma(q, "sync", sb["row24"][:, :],
                        d["conv_b"].rearrange("(cc p) -> cc p", p=128))
        t["al"] = S.dma(q, "sync", sb["acol"][:, :], d["a_log"].rearrange("(a o) -> a o", o=1))
        t["db"] = S.dma(q, "sync", sb["dtb_col"][:, :], d["dt_bias"].rearrange("(a o) -> a o", o=1))
        t["dr"] = S.dma(q, "sync", sb["drow"][:, :], d["d_vec"].rearrange("(o a) -> o a", o=1))
        return t
    t = S.sec("sync", ld)

    def io(q):
        i1 = q.iota(sb["irow"][:, :], pattern=[[1, 512]], base=0, channel_multiplier=0,
                    allow_small_or_imprecise_dtypes=True)
        t1 = S.done("gpsimd", i1)
        i2 = q.iota(sb["pcol"][:, :], pattern=[[0, 1]], base=0, channel_multiplier=1,
                    allow_small_or_imprecise_dtypes=True)
        return t1, S.done("gpsimd", i2)
    t_irow, t_pcol = S.sec("gpsimd", io)

    def consts(q):
        S.wait(q, "vector", t_irow)
        S.wait(q, "vector", t_pcol)
        inst = q.tensor_scalar(out=sb["I_f32"][:, :], in0=sb["irow"][:, 0:128],
                               scalar1=sb["pcol"][:, 0:1], scalar2=None, op0=ALU.is_equal)
        tI = S.done("vector", inst)
        inst = q.tensor_copy(out=sb["I_bf"][:, :], in_=sb["I_f32"][:, :])
        tIb = S.done("vector", inst)
        q.memset(sb["onesrow"][:, :], 1.0)
        q.memset(sb["onescol"][:, :], 1.0)
        q.memset(sb["ones16"][:, :], 1.0)
        q.memset(sb["cbias"][:, 0:1], EPS)
        q.memset(sb["cbias"][:, 1:2], 1.0)
        inst = q.memset(sb["zb"][:, :], 0.0)
        return tI, tIb, S.done("vector", inst)
    t_I, t_Ibf, t_ms = S.sec("vector", consts)
    C.t_Ibf = t_Ibf

    def tr2(q):
        S.wait(q, "tensor", t["cb"])
        S.wait(q, "tensor", t_I)
        inst = q.transpose(out=ps["p0"][0:128, 0:24], in_=sb["row24"][0:24, 0:128],
                           identity=sb["I_f32"][0:24, 0:24])
        return S.done("tensor", inst)
    t_cbT = S.sec("tensor", tr2)
    def ev2(q):
        S.wait(q, "vector", t_cbT)
        inst = q.tensor_copy(out=sb["convb_col"][:, :], in_=ps["p0"][0:128, 0:24])
        return S.done("vector", inst)
    t_cbfm = S.sec("vector", ev2)

    def ldnw(q):
        S.wait(q, "sync", t_cbT)
        return S.dma(q, "sync", sb["row24"][0:16, :],
                     d["normw"].rearrange("(a b) -> a b", b=128))
    t_nw = S.sec("sync", ldnw)
    def tr1(q):
        S.wait(q, "tensor", t_nw)
        S.wait(q, "tensor", t_cbfm)
        inst = q.transpose(out=ps["p0"][0:128, 0:16], in_=sb["row24"][0:16, 0:128],
                           identity=sb["I_f32"][0:16, 0:16])
        return S.done("tensor", inst)
    t_nwT = S.sec("tensor", tr1)
    def ev1(q):
        S.wait(q, "vector", t_nwT)
        inst = q.tensor_copy(out=sb["normw_fm"][:, :], in_=ps["p0"][0:128, 0:16])
        return S.done("vector", inst)
    t_nwfm = S.sec("vector", ev1)

    def mmD(q):
        S.wait(q, "tensor", t["dr"])
        S.wait(q, "tensor", t_ms)
        S.wait(q, "tensor", t_nwfm)
        inst = q.matmul(out=ps["p0"][0:128, 0:16], lhsT=sb["onesrow"][:, :],
                        rhs=sb["drow"][:, :], start=True, stop=True)
        return S.done("tensor", inst)
    t_dbc = S.sec("tensor", mmD)
    def ev3(q):
        S.wait(q, "vector", t_dbc)
        inst = q.tensor_copy(out=sb["D_bc"][:, :], in_=ps["p0"][0:128, 0:16])
        return S.done("vector", inst)
    t_dbcv = S.sec("vector", ev3)

    def sA(q):
        S.wait(q, "scalar", t["al"])
        inst = q.activation(out=sb["smalls"][0:16, 40:41], in_=sb["acol"][:, :], func=AF.Exp)
        return S.done("scalar", inst)
    t_ea = S.sec("scalar", sA)
    def vA(q):
        S.wait(q, "vector", t_ea)
        inst = q.tensor_scalar(out=sb["acol"][:, :], in0=sb["smalls"][0:16, 40:41],
                               scalar1=-1.0, scalar2=None, op0=ALU.mult)
        return S.done("vector", inst)
    S.sec("vector", vA)

    with contextlib.ExitStack() as es0:
        wld = [es0.enter_context(nc.sbuf_tensor(f"s0_wld{i}", [128, 2048], F32)) for i in range(2)]
        wjunk = es0.enter_context(nc.sbuf_tensor("s0_junk", [128, 2048], F32))
        wn = es0.enter_context(nc.sbuf_tensor("s0_wn", [128, 2048], F32))
        q2t = es0.enter_context(nc.sbuf_tensor("s0_q2", [128, 2048], F32))
        wqb = es0.enter_context(nc.sbuf_tensor("s0_wqb", [128, 2048], BF16))
        tq = es0.enter_context(nc.sbuf_tensor("s0_tq", [128, 16, 128], BF16))
        rs = es0.enter_context(nc.sbuf_tensor("s0_rs", [128, 16], F32))
        xt = [es0.enter_context(nc.sbuf_tensor(f"s1_x{i}", [128, 2048], F32)) for i in range(2)]
        bns = es0.enter_context(nc.sbuf_tensor("s1_bns", [128, 4, 6], F32))
        lns = [es0.enter_context(nc.sbuf_tensor(f"s1_l{i}", [128, 2048], BF16)) for i in range(2)]
        lnt = [es0.enter_context(nc.sbuf_tensor(f"s1_t{i}", [128, 16, 128], BF16)) for i in range(2)]

        tload = [None, None]
        tuse = [None, None]
        st0 = {"t_tq": None, "t_st": None, "t_q2_prev": None,
               "t_rtot": None, "t_inv2": None}

        def emit_s0a(sl):
            b = sl % 2
            def ldw(q, b=b, sl=sl):
                S.wait(q, "sync", tuse[b])
                return S.dma(q, "sync", wld[b][:, :], d["w_ffn"][sl * 128:(sl + 1) * 128, :])
            tload[b] = S.sec("sync", ldw)
            def absum(q, b=b, sl=sl):
                S.wait(q, "scalar", tload[b])
                inst = q.activation(out=wjunk[:, :], in_=wld[b][:, :], func=AF.Abs,
                                    accum_out=rs[:, sl:sl + 1])
                return S.done("scalar", inst)
            tuse[b] = S.sec("scalar", absum)

        def emit_s0mid():
            def vtot(q):
                S.wait(q, "vector", tuse[0])
                S.wait(q, "vector", tuse[1])
                inst = q.tensor_reduce(out=sb["smalls"][:, 32:33], in_=rs[:, :], axis=AX.X, op=ALU.add)
                return S.done("vector", inst)
            t_rtot = S.sec("vector", vtot)
            def mtot(q, t_rtot=t_rtot):
                S.wait(q, "tensor", t_rtot)
                S.wait(q, "tensor", t_dbcv)
                inst = q.matmul(out=ps["p1"][0:1, 0:1], lhsT=sb["smalls"][:, 32:33],
                                rhs=sb["onescol"][:, :], start=True, stop=True)
                return S.done("tensor", inst)
            t_tot = S.sec("tensor", mtot)
            def vsc(q, t_tot=t_tot):
                S.wait(q, "vector", t_tot)
                q.tensor_scalar(out=sb["smalls"][0:1, 33:34], in0=ps["p1"][0:1, 0:1],
                                scalar1=1.0 / (DM * DM), scalar2=1e-6, op0=ALU.mult, op1=ALU.max)
                inst = q.reciprocal(out=sb["smalls"][0:1, 34:35], in_=sb["smalls"][0:1, 33:34])
                return S.done("vector", inst)
            t_inv = S.sec("vector", vsc)
            def mbc(q, t_inv=t_inv):
                S.wait(q, "tensor", t_inv)
                q.matmul(out=ps["p1"][0:128, 2:3], lhsT=sb["onesrow"][:, :],
                         rhs=sb["smalls"][0:1, 33:34], start=True, stop=True)
                inst = q.matmul(out=ps["p1"][0:128, 3:4], lhsT=sb["onesrow"][:, :],
                                rhs=sb["smalls"][0:1, 34:35], start=True, stop=True)
                return S.done("tensor", inst)
            t_bc2 = S.sec("tensor", mbc)
            def vbc(q, t_bc2=t_bc2):
                S.wait(q, "vector", t_bc2)
                q.tensor_copy(out=sb["smalls"][:, 35:36], in_=ps["p1"][0:128, 2:3])
                inst = q.tensor_copy(out=sb["smalls"][:, 36:37], in_=ps["p1"][0:128, 3:4])
                return S.done("vector", inst)
            st0["t_inv2"] = S.sec("vector", vbc)
            tload[0] = tload[1] = None

        def emit_s0b(sl):
            b = sl % 2
            def ldw(q, b=b, sl=sl):
                S.wait(q, "sync", tuse[b])
                return S.dma(q, "sync", wld[b][:, :], d["w_ffn"][sl * 128:(sl + 1) * 128, :])
            tload[b] = S.sec("sync", ldw)
            def vwn(q, b=b):
                S.wait(q, "vector", tload[b])
                S.wait(q, "vector", st0["t_inv2"])
                S.wait(q, "vector", st0["t_tq"])
                S.wait(q, "vector", st0["t_q2_prev"])
                inst = q.tensor_scalar(out=wn[:, :], in0=wld[b][:, :],
                                       scalar1=sb["smalls"][:, 36:37], scalar2=None, op0=ALU.mult)
                return S.done("vector", inst)
            tuse[b] = S.sec("vector", vwn)
            def gq2(q, b=b):
                inst = q.tensor_scalar(out=q2t[:, :], in0=wn[:, :], scalar1=-0.5,
                                       scalar2=None, op0=ALU.is_le)
                return S.done("vector", inst)
            t_q2 = S.sec("vector", gq2)
            st0["t_q2_prev"] = t_q2
            def vwq(q, t_q2=t_q2):
                S.wait(q, "vector", t_q2)
                inst = q.scalar_tensor_tensor(out=wqb[:, :], in0=wn[:, :], scalar=0.5,
                                              in1=q2t[:, :], op0=ALU.is_ge, op1=ALU.subtract)
                return S.done("vector", inst)
            t_wq = S.sec("vector", vwq)
            def str_(q, t_wq=t_wq):
                S.wait(q, "scalar", t_wq)
                S.wait(q, "scalar", st0["t_st"])
                return S.dma(q, "scalar", tq[:, :, :], wqb[:, :], transpose=True)
            st0["t_tq"] = S.sec("scalar", str_)
            def sto(q, sl=sl):
                S.wait(q, "sync", st0["t_tq"])
                return S.dma(q, "sync",
                             d["WQT"][:, :, sl, :].rearrange("ic i o -> i ic o"),
                             tq[:, :, :])
            st0["t_st"] = S.sec("sync", sto)

        t_xuse = [None, None]
        t_lnuse = [None, None]
        t_tuse = [None, None]

        def emit_s1(tt):
            b = tt % 2
            c0 = 8 * b
            def ldx(q, b=b, tt=tt):
                S.wait(q, "gpsimd", t_xuse[b])
                return S.dma(q, "gpsimd", xt[b][:, :], d["x"][tt * 128:(tt + 1) * 128, :])
            t_x = S.sec("gpsimd", ldx)
            def vstats(q, b=b, c0=c0, t_x=t_x):
                S.wait(q, "vector", t_x)
                for j in range(4):
                    q.bn_stats(out=bns[:, j, :], in_=xt[b][:, j * 512:(j + 1) * 512])
                inst = q.bn_aggr(out=sb["smalls"][:, c0 + 2:c0 + 4],
                                 in_=bns.ap().rearrange("p a b -> p (a b)"))
                return S.done("vector", inst)
            t_var = S.sec("vector", vstats)
            def srstd(q, c0=c0, t_var=t_var):
                S.wait(q, "scalar", t_var)
                q.activation(out=sb["smalls"][:, c0 + 5:c0 + 6],
                             in_=sb["smalls"][:, c0 + 3:c0 + 4], func=AF.Ln,
                             bias=sb["cbias"][:, 0:1])
                inst = q.activation(out=sb["smalls"][:, c0 + 6:c0 + 7],
                                    in_=sb["smalls"][:, c0 + 5:c0 + 6], func=AF.Exp, scale=-0.5)
                return S.done("scalar", inst)
            t_rstd = S.sec("scalar", srstd)
            def vapply(q, b=b, c0=c0, t_rstd=t_rstd):
                S.wait(q, "vector", t_rstd)
                S.wait(q, "vector", t_lnuse[b])
                inst = q.tensor_scalar(out=lns[b][:, :], in0=xt[b][:, :],
                                       scalar1=sb["smalls"][:, c0 + 2:c0 + 3],
                                       scalar2=sb["smalls"][:, c0 + 6:c0 + 7],
                                       op0=ALU.subtract, op1=ALU.mult)
                return S.done("vector", inst)
            t_ln = S.sec("vector", vapply)
            t_xuse[b] = t_ln
            def strp(q, b=b, t_ln=t_ln):
                S.wait(q, "scalar", t_ln)
                S.wait(q, "scalar", t_tuse[b])
                return S.dma(q, "scalar", lnt[b][:, :, :], lns[b][:, :], transpose=True)
            t_lnuse[b] = S.sec("scalar", strp)
            def sto(q, b=b, tt=tt, tk=t_lnuse[b]):
                S.wait(q, "sync", tk)
                return S.dma(q, "sync",
                             d["LNF"][:, :, tt * 128:(tt + 1) * 128].rearrange("k f t -> f k t"),
                             lnt[b][:, :, :])
            t_tuse[b] = S.sec("sync", sto)

        for tt in range(8):
            emit_s0a(2 * tt)
            emit_s0a(2 * tt + 1)
            emit_s1(tt)
        emit_s0mid()
        for tt in range(8, 16):
            emit_s0b(2 * (tt - 8))
            emit_s0b(2 * (tt - 8) + 1)
            emit_s1(tt)
        S.flush()


def stage23_consts(C):
    S, sb, sb23 = C.S, C.sb, C.sb23
    def io(q):
        q.iota(sb23["maskneg"][:, 0:256], pattern=[[1, 256]], base=0, channel_multiplier=0,
               allow_small_or_imprecise_dtypes=True)
        i2 = q.iota(sb23["maskneg"][:, 256:512], pattern=[[1, 256]], base=-128,
                    channel_multiplier=0, allow_small_or_imprecise_dtypes=True)
        t1 = S.done("gpsimd", i2)
        i3 = q.iota(sb23["IND"][:, :], pattern=[[0, KC * 128]], base=0, channel_multiplier=1,
                    allow_small_or_imprecise_dtypes=True)
        return t1, S.done("gpsimd", i3)
    t_io, t_indk = S.sec("gpsimd", io)

    def vm(q):
        S.wait(q, "vector", t_io)
        q.tensor_scalar(out=sb23["maskneg"][:, 0:256], in0=sb23["maskneg"][:, 0:256],
                        scalar1=sb["pcol"][:, 0:1], scalar2=NEGBIG,
                        op0=ALU.is_lt, op1=ALU.mult)
        inst = q.tensor_scalar(out=sb23["maskneg"][:, 256:512], in0=sb23["maskneg"][:, 256:512],
                               scalar1=sb["pcol"][:, 0:1], scalar2=NEGBIG,
                               op0=ALU.is_lt, op1=ALU.mult)
        t1 = S.done("vector", inst)
        S.wait(q, "vector", t_indk)
        inst = None
        for h in range(NH):
            inst = q.tensor_scalar(out=sb23["IND"][:, h * 128:(h + 1) * 128],
                                   in0=sb23["IND"][:, h * 128:(h + 1) * 128],
                                   scalar1=float(h), scalar2=None, op0=ALU.is_equal)
        t2 = S.done("vector", inst)
        q.memset(sb23["S_a"][:, :], 0.0)
        inst = q.memset(sb23["S_bf"][:, :], 0.0)
        t3 = S.done("vector", inst)
        return t1, t2, t3
    C.t_mask, C.t_ind, C.t_szero = S.sec("vector", vm)


def stage2(C, half):
    nc, S, sb, sb23, ps, d = C.nc, C.S, C.sb, C.sb23, C.ps, C.d
    ncc = 12
    ccbase = half * 12
    col0 = DM + half * 1536
    ncols = 1536 + (16 if half == 1 else 0)
    with contextlib.ExitStack() as es2:
        pf = f"s2h{half}_"
        WB = es2.enter_context(nc.sbuf_tensor(pf + "wb", [128, 16, ncols], BF16))
        LNG = es2.enter_context(nc.sbuf_tensor(pf + "lng", [128, 16, TGRP], BF16))
        HBCR = es2.enter_context(nc.sbuf_tensor(pf + "hbcr", [128, ncc, TGRP + 3], BF16))
        XBCg = es2.enter_context(nc.sbuf_tensor(pf + "xbc", [128, ncc, TGRP], BF16))
        CVA = es2.enter_context(nc.sbuf_tensor(pf + "cva", [128, 2, TGRP], BF16))
        CVB = es2.enter_context(nc.sbuf_tensor(pf + "cvb", [128, ncc, TGRP], BF16))
        DTRAW = es2.enter_context(nc.sbuf_tensor(pf + "dtraw", [16, TGRP], F32))
        PAN = es2.enter_context(nc.sbuf_tensor(pf + "pan", [16, 6, TGRP], F32))

        def ldw(q):
            return [S.dma(q, "gpsimd", WB[:, kc, :],
                          d["w_in"][kc * 128:(kc + 1) * 128, col0:col0 + ncols])
                    for kc in range(16)]
        t_w = S.sec("gpsimd", ldw)

        t_lng = None
        t_lng_use = None
        t_evac = {}
        t_conv = {}
        t_store = None
        t_silu = {}
        t_silu_slot = {}
        t_halo = None
        t_dtevac = None
        t_panT_evac = None
        t_mm_last = None
        t_pe2_prev = None
        t_panT_prev = None
        for g in range(NGRP):
            def ldl(q, g=g):
                S.wait(q, "sync", t_lng_use)
                return S.dma(q, "sync", LNG[:, :, :],
                             d["LNF"][:, :, g * TGRP:(g + 1) * TGRP].rearrange("k f t -> f k t"))
            t_lng = S.sec("sync", ldl)

            def halo(q, g=g):
                for cc in range(ncc):
                    S.wait(q, "vector", t_conv.get(cc))
                if g == 0:
                    S.wait(q, "vector", C.t_mask)
                    inst = q.tensor_copy(out=HBCR[:, :, 0:3],
                                         in_=sb["zb"].ap().rearrange("p (a b) -> p a b", b=3)[:, 0:ncc, :])
                else:
                    inst = q.tensor_copy(out=HBCR[:, :, 0:3], in_=HBCR[:, :, TGRP:TGRP + 3])
                return S.done("vector", inst)
            t_halo = S.sec("vector", halo)

            for cc in range(ncc):
                gcc = g * ncc + cc
                bank = ps[f"p{cc % 2}"]
                def mm(q, cc=cc, gcc=gcc, bank=bank, t_lng=t_lng):
                    S.wait(q, "tensor", t_w)
                    S.wait(q, "tensor", t_lng)
                    S.wait(q, "tensor", t_evac.get(gcc - 2))
                    inst = None
                    for kc in range(16):
                        inst = q.matmul(out=bank[:, :], lhsT=WB[:, kc, cc * 128:(cc + 1) * 128],
                                        rhs=LNG[:, kc, :], start=(kc == 0), stop=(kc == 15))
                    return S.done("tensor", inst)
                t_mm = S.sec("tensor", mm)
                t_mm_last = t_mm
                eng = "scalar" if cc % 2 == 0 else "vector"
                def ev(q, cc=cc, bank=bank, t_mm=t_mm, eng=eng):
                    S.wait(q, eng, t_mm)
                    S.wait(q, eng, t_halo)
                    S.wait(q, eng, t_conv.get(cc))
                    if eng == "scalar":
                        inst = q.copy(out=HBCR[:, cc, 3:TGRP + 3], in_=bank[:, :])
                    else:
                        inst = q.tensor_copy(out=HBCR[:, cc, 3:TGRP + 3], in_=bank[:, :])
                    return S.done(eng, inst)
                t_evac[gcc] = S.sec(eng, ev)
            t_lng_use = t_mm_last

            if half == 1:
                def mmdt(q, t_lng=t_lng):
                    S.wait(q, "tensor", t_dtevac)
                    inst = None
                    for kc in range(16):
                        inst = q.matmul(out=ps["p2"][0:16, :], lhsT=WB[:, kc, 1536:1552],
                                        rhs=LNG[:, kc, :], start=(kc == 0), stop=(kc == 15))
                    return S.done("tensor", inst)
                t_mmdt = S.sec("tensor", mmdt)
                t_lng_use = t_mmdt
                def evdt(q, t_mmdt=t_mmdt):
                    S.wait(q, "vector", t_mmdt)
                    inst = q.tensor_copy(out=DTRAW[:, :], in_=ps["p2"][0:16, :])
                    return S.done("vector", inst)
                t_dtevac = S.sec("vector", evdt)

            for cc in range(ncc):
                ccg = ccbase + cc
                eng = "vector"
                ei = cc % 2
                def taps(q, cc=cc, ccg=ccg, ei=ei, g=g):
                    S.wait(q, "vector", t_evac[g * ncc + cc])
                    S.wait(q, "vector", t_silu_slot.get(ei))
                    wcol = lambda k: sb["convw_col"][:, 4 * ccg + k:4 * ccg + k + 1]
                    q.tensor_scalar(out=CVA[:, ei, :], in0=HBCR[:, cc, 0:TGRP],
                                    scalar1=wcol(0), scalar2=None, op0=ALU.mult)
                    q.scalar_tensor_tensor(out=CVB[:, cc, :], in0=HBCR[:, cc, 1:TGRP + 1],
                                           scalar=wcol(1), in1=CVA[:, ei, :],
                                           op0=ALU.mult, op1=ALU.add)
                    q.scalar_tensor_tensor(out=CVA[:, ei, :], in0=HBCR[:, cc, 2:TGRP + 2],
                                           scalar=wcol(2), in1=CVB[:, cc, :],
                                           op0=ALU.mult, op1=ALU.add)
                    inst = q.scalar_tensor_tensor(out=CVB[:, cc, :], in0=HBCR[:, cc, 3:TGRP + 3],
                                                  scalar=wcol(3), in1=CVA[:, ei, :],
                                                  op0=ALU.mult, op1=ALU.add)
                    return S.done("vector", inst)
                t_conv[cc] = S.sec("vector", taps)
                def silu(q, cc=cc, ccg=ccg, g=g):
                    S.wait(q, "scalar", t_store)
                    S.wait(q, "scalar", t_conv[cc])
                    inst = q.activation(out=XBCg[:, cc, :], in_=CVB[:, cc, :], func=AF.Silu,
                                        bias=sb["convb_col"][:, ccg:ccg + 1])
                    return S.done("scalar", inst)
                t_silu[cc] = S.sec("scalar", silu)
            t_silu_slot[0] = t_silu[ncc - 1]
            t_silu_slot[1] = t_silu[ncc - 1]

            if half == 1:
                def pan(q, g=g):
                    S.wait(q, "vector", t_dtevac)
                    S.wait(q, "vector", t_pe2_prev)
                    inst = q.tensor_scalar(out=PAN[:, 0, :], in0=DTRAW[:, :],
                                           scalar1=sb["dtb_col"][:, 0:1], scalar2=None, op0=ALU.add)
                    return S.done("vector", inst)
                t_z = S.sec("vector", pan)
                def pe1(q, t_z=t_z):
                    S.wait(q, "scalar", t_z)
                    q.activation(out=PAN[:, 1, :], in_=PAN[:, 0, :], func=AF.Abs)
                    q.activation(out=PAN[:, 2, :], in_=PAN[:, 1, :], func=AF.Exp, scale=-1.0)
                    q.activation(out=PAN[:, 3, :], in_=PAN[:, 2, :], func=AF.Ln,
                                 bias=sb["cbias"][0:16, 1:2])
                    inst = q.activation(out=PAN[:, 1, :], in_=PAN[:, 0, :], func=AF.Relu)
                    return S.done("scalar", inst)
                t_sp = S.sec("scalar", pe1)
                def pv1(q, t_sp=t_sp, g=g):
                    S.wait(q, "vector", t_sp)
                    q.tensor_tensor(out=PAN[:, 0, :], in0=PAN[:, 1, :], in1=PAN[:, 3, :], op=ALU.add)
                    inst = q.tensor_scalar(out=PAN[:, 2, :], in0=PAN[:, 0, :],
                                           scalar1=sb["acol"][:, 0:1], scalar2=None, op0=ALU.mult)
                    t_dt = S.done("vector", inst)
                    inst = None
                    for c01 in range(2):
                        cg = g * 2 + c01
                        inst = q.tensor_tensor_scan(out=sb23["cum_all"][:, cg * 256:(cg + 1) * 256],
                                                    data0=sb["ones16"][:, :],
                                                    data1=PAN[:, 2, c01 * 256:(c01 + 1) * 256],
                                                    initial=0.0, op0=ALU.mult, op1=ALU.add)
                    return t_dt, S.done("vector", inst)
                t_dt, t_cum = S.sec("vector", pv1)
                def pe2(q, g=g, t_cum=t_cum):
                    S.wait(q, "scalar", t_cum)
                    S.wait(q, "scalar", t_panT_prev)
                    q.activation(out=PAN[:, 3, :], in_=PAN[:, 0, :], func=AF.Ln)
                    inst = None
                    for c01 in range(2):
                        cg = g * 2 + c01
                        csl = sb23["cum_all"][:, cg * 256:(cg + 1) * 256]
                        q.activation(out=PAN[:, 4, c01 * 256:(c01 + 1) * 256], in_=csl,
                                     func=AF.Exp, scale=-1.0,
                                     bias=sb23["cum_all"][:, cg * 256 + 255:cg * 256 + 256])
                        inst = q.activation(out=sb23["expcum_all"][:, cg * 256:(cg + 1) * 256],
                                            in_=csl, func=AF.Exp)
                    return S.done("scalar", inst)
                t_pe2 = S.sec("scalar", pe2)
                t_pe2_prev = t_pe2
                def pv2(q, g=g, t_pe2=t_pe2):
                    S.wait(q, "vector", t_pe2)
                    S.wait(q, "vector", t_panT_prev)
                    q.tensor_tensor(out=PAN[:, 5, :], in0=PAN[:, 3, :],
                                    in1=sb23["cum_all"][:, g * TGRP:(g + 1) * TGRP], op=ALU.subtract)
                    inst = q.tensor_tensor(out=PAN[:, 4, :], in0=PAN[:, 0, :], in1=PAN[:, 4, :],
                                           op=ALU.mult)
                    t_w_ = S.done("vector", inst)
                    inst = None
                    for c01 in range(2):
                        cg = g * 2 + c01
                        inst = q.tensor_scalar(out=PAN[:, 3, c01 * 256:(c01 + 1) * 256],
                                               in0=sb["ones16"][:, :],
                                               scalar1=sb23["expcum_all"][:, cg * 256 + 255:cg * 256 + 256],
                                               scalar2=None, op0=ALU.mult)
                    return t_w_, S.done("vector", inst)
                t_wp, t_cd = S.sec("vector", pv2)
                def ptr(q, g=g, t_wp=t_wp, t_cd=t_cd):
                    S.wait(q, "tensor", t_wp)
                    S.wait(q, "tensor", t_cd)
                    S.wait(q, "tensor", t_panT_evac)
                    inst = None
                    for c01 in range(2):
                        for jt in range(2):
                            for qi, pan_i in ((0, 5), (1, 4), (2, 3)):
                                j = (c01 * 2 + jt) * 3 + qi
                                inst = q.transpose(out=ps["p3"][0:128, j * 16:(j + 1) * 16],
                                                   in_=PAN[:, pan_i, (c01 * 2 + jt) * 128:(c01 * 2 + jt + 1) * 128],
                                                   identity=sb["I_f32"][0:16, 0:16])
                    return S.done("tensor", inst)
                t_panT = S.sec("tensor", ptr)
                t_panT_prev = t_panT
                def pev(q, g=g, t_panT=t_panT):
                    S.wait(q, "vector", t_panT)
                    inst = q.tensor_copy(out=sb23["colT"][:, g * 192:(g + 1) * 192],
                                         in_=ps["p3"][0:128, 0:192])
                    return S.done("vector", inst)
                t_panT_evac = S.sec("vector", pev)

            def sto(q, g=g):
                for cc in range(ncc):
                    S.wait(q, "sync", t_silu[cc])
                return S.dma(q, "sync",
                             d["XBC"][ccbase:ccbase + ncc, :, g * TGRP:(g + 1) * TGRP].rearrange("c f t -> f c t"),
                             XBCg[:, :, :])
            t_store = S.sec("sync", sto)
        S.flush()


def stage2g(C):
    nc, S, sb, ps, d = C.nc, C.S, C.sb, C.ps, C.d
    with contextlib.ExitStack() as es2:
        WG = es2.enter_context(nc.sbuf_tensor("s2g_wg", [128, 16, 2048], BF16))
        LNG = es2.enter_context(nc.sbuf_tensor("s2g_lng", [128, 16, TGRP], BF16))
        SGG = es2.enter_context(nc.sbuf_tensor("s2g_sgg", [128, 16, TGRP], BF16))
        def ldw(q):
            return [S.dma(q, "gpsimd", WG[:, kc, :],
                          d["w_in"][kc * 128:(kc + 1) * 128, 0:DM])
                    for kc in range(16)]
        t_w = S.sec("gpsimd", ldw)
        t_lng_use = None
        t_store = None
        t_ev = {}
        for g in range(NGRP):
            def ldl(q, g=g):
                S.wait(q, "sync", t_lng_use)
                return S.dma(q, "sync", LNG[:, :, :],
                             d["LNF"][:, :, g * TGRP:(g + 1) * TGRP].rearrange("k f t -> f k t"))
            t_lng = S.sec("sync", ldl)
            for oc in range(16):
                bank = ps[f"p{oc % 4}"]
                def mm(q, oc=oc, bank=bank, t_lng=t_lng):
                    S.wait(q, "tensor", t_w)
                    S.wait(q, "tensor", t_lng)
                    S.wait(q, "tensor", t_ev.get(oc - 4))
                    inst = None
                    for kc in range(16):
                        inst = q.matmul(out=bank[:, :], lhsT=WG[:, kc, oc * 128:(oc + 1) * 128],
                                        rhs=LNG[:, kc, :], start=(kc == 0), stop=(kc == 15))
                    return S.done("tensor", inst)
                t_mm = S.sec("tensor", mm)
                def ev(q, oc=oc, bank=bank, t_mm=t_mm):
                    S.wait(q, "scalar", t_mm)
                    S.wait(q, "scalar", t_store)
                    inst = q.activation(out=SGG[:, oc, :], in_=bank[:, :], func=AF.Silu)
                    return S.done("scalar", inst)
                t_ev[oc] = S.sec("scalar", ev)
            t_lng_use = t_ev[15]
            def sto(q, g=g):
                for oc in range(16):
                    S.wait(q, "sync", t_ev[oc])
                return S.dma(q, "sync",
                             d["SG"][:, :, g * TGRP:(g + 1) * TGRP].rearrange("k f t -> f k t"),
                             SGG[:, :, :])
            t_store = S.sec("sync", sto)
        S.flush()


def stage3(C):
    nc, S, sb, sb23, ps, d = C.nc, C.S, C.sb, C.sb23, C.ps, C.d
    with contextlib.ExitStack() as es3:
        XBCc = [es3.enter_context(nc.sbuf_tensor(f"s3_xbc{i}", [128, 24, QCH], BF16)) for i in range(2)]
        SGc = [es3.enter_context(nc.sbuf_tensor(f"s3_sg{i}", [128, 16, QCH], BF16)) for i in range(2)]
        XT = es3.enter_context(nc.sbuf_tensor("s3_xt", [128, 8, 128], BF16))
        BT = es3.enter_context(nc.sbuf_tensor("s3_bt", [128, 8, 128], BF16))
        BW = es3.enter_context(nc.sbuf_tensor("s3_bw", [128, 4, 2, 128], BF16))
        MTD = es3.enter_context(nc.sbuf_tensor("s3_mtd", [128, 4, 512], BF16))
        EAR = es3.enter_context(nc.sbuf_tensor("s3_ear", [128, 4, 512], F32))
        ESB = es3.enter_context(nc.sbuf_tensor("s3_esb", [128, 4, 512], F32))
        CS = es3.enter_context(nc.sbuf_tensor("s3_cs", [128, 4, QCH], BF16))
        Y = es3.enter_context(nc.sbuf_tensor("s3_y", [128, 4, QCH], F32))
        G16 = es3.enter_context(nc.sbuf_tensor("s3_g16", [128, 16, QCH], BF16))
        GSQ = es3.enter_context(nc.sbuf_tensor("s3_gsq", [128, 4, QCH], F32))
        GSO = [es3.enter_context(nc.sbuf_tensor(f"s3_gso{i}", [128, 16, QCH], BF16)) for i in range(2)]
        RSUMS = es3.enter_context(nc.sbuf_tensor("s3_rsums", [1, 2, 512], F32))
        RSTDS = es3.enter_context(nc.sbuf_tensor("s3_rstds", [1, 2, 512], F32))
        RTMP = es3.enter_context(nc.sbuf_tensor("s3_rtmp", [1, 512], F32))
        RB = es3.enter_context(nc.sbuf_tensor("s3_rb", [128, 2, 512], F32))

        t_ld = [None, None]
        t_ld_sg = [None, None]
        t_xbc_use = [None, None]
        t_sg_use = [None, None]
        t_gso_use = [None, None]
        t_mtd_done = {}
        t_cs_done = {}
        t_yev = {}
        t_sup = {}
        t_su = {}
        t_gso = {}
        t_e_prev = {}
        t_y_prev = {}
        t_sc_prev = {}
        t_g_prev = {}
        t_gsq_prev = {}
        t_gsq_tok = {}
        t_xt_ev = None
        t_bt_ev = None
        t_sc_last = {}
        pend_sbf = [None]
        pend_epi = [None]
        t_mm3 = {}
        t_evac = {}
        t_srs = {}
        t_dma1 = {}
        t_dma2 = {}

        def emit_scalar_merged(q, h, fl, sl, t_ear):
            S.wait(q, "scalar", t_ear)
            inst = None
            for jt in range(2):
                ci = ((fl % NH * 0 + (fl // NH) * 2 + jt) * 3 + 0) * 16 + h
                inst = q.activation(out=ESB[:, sl, jt * 256:(jt + 1) * 256],
                                    in_=EAR[:, sl, jt * 256:(jt + 1) * 256], func=AF.Exp,
                                    bias=sb23["colT"][:, ci:ci + 1])
            return S.tok("scalar_c")

        for c in range(NCH):
            b = c % 2
            def ldc(q, c=c, b=b):
                S.wait(q, "sync", t_xbc_use[b])
                tk1 = S.dma(q, "sync", XBCc[b][:, :, :],
                            d["XBC"][:, :, c * QCH:(c + 1) * QCH].rearrange("a f t -> f a t"))
                S.wait(q, "sync", t_sg_use[b])
                tk2 = S.dma(q, "sync", SGc[b][:, :, :],
                            d["SG"][:, :, c * QCH:(c + 1) * QCH].rearrange("a f t -> f a t"))
                return tk1, tk2
            t_ld[b], t_ld_sg[b] = S.sec("sync", ldc)

            def trx(q, c=c, b=b):
                S.wait(q, "tensor", t_ld[b])
                S.wait(q, "tensor", t_xt_ev)
                S.wait(q, "tensor", C.t_Ibf)
                inst = None
                for gx in range(4):
                    for jt in range(2):
                        k = gx * 2 + jt
                        inst = q.transpose(out=ps["pt"][:, k * 128:(k + 1) * 128],
                                           in_=XBCc[b][:, gx, jt * 128:(jt + 1) * 128],
                                           identity=sb["I_bf"][:, :])
                return S.done("tensor", inst)
            t_trx = S.sec("tensor", trx)
            def evx(q, t_trx=t_trx, c=c):
                S.wait(q, "vector", t_trx)
                S.wait(q, "vector", t_sc_last.get(c - 1))
                inst = q.tensor_copy(out=XT.ap().rearrange("p a b -> p (a b)"),
                                     in_=ps["pt"][:, 0:1024])
                return S.done("vector", inst)
            t_xt_evac = S.sec("vector", evx)
            def trb(q, c=c, b=b, t_xt_evac=t_xt_evac):
                S.wait(q, "tensor", t_xt_evac)
                inst = None
                for gx in range(4):
                    for jt in range(2):
                        k = gx * 2 + jt
                        inst = q.transpose(out=ps["pt"][:, k * 128:(k + 1) * 128],
                                           in_=XBCc[b][:, 4 + gx, jt * 128:(jt + 1) * 128],
                                           identity=sb["I_bf"][:, :])
                return S.done("tensor", inst)
            t_trb = S.sec("tensor", trb)
            def evb(q, t_trb=t_trb, c=c):
                S.wait(q, "vector", t_trb)
                inst = q.tensor_copy(out=BT.ap().rearrange("p a b -> p (a b)"),
                                     in_=ps["pt"][:, 0:1024])
                return S.done("vector", inst)
            t_bt_ev = S.sec("vector", evb)
            t_xt_ev = t_bt_ev

            if pend_epi[0] is not None:
                pend_epi[0]()
                pend_epi[0] = None

            Sold = sb23["S_a"] if c % 2 == 0 else sb23["S_b"]
            Snew = sb23["S_b"] if c % 2 == 0 else sb23["S_a"]

            def emit_vgs(q, kv, b):
                S.wait(q, "vector", t_dma2[kv])
                S.wait(q, "vector", t_gso_use[b])
                toks = []
                for hh in range(2):
                    flv = kv * 2 + hh
                    hv = flv % NH
                    inst = q.scalar_tensor_tensor(out=GSO[b][:, hv, :], in0=G16[:, hv, :],
                                                  scalar=sb["normw_fm"][:, hv:hv + 1],
                                                  in1=RB[:, kv % 2, hh * 256:(hh + 1) * 256],
                                                  op0=ALU.mult, op1=ALU.mult)
                    toks.append(S.done("vector", inst))
                return toks

            def emit_rb_dma(kk):
                def dd1(q, kk=kk):
                    S.wait(q, "sync", t_srs[kk])
                    S.wait(q, "sync", t_dma2.get(kk - 2))
                    return S.dma(q, "sync", d["RSTDD"][kk % 2:kk % 2 + 1, :],
                                 RSTDS[0:1, kk % 2, :])
                t_dma1[kk] = S.sec("sync", dd1)
                def dd2(q, kk=kk):
                    S.wait(q, "sync", t_dma1[kk])
                    S.wait(q, "sync", t_gso.get(2 * (kk - 2) + 1))
                    return S.dma(q, "sync", RB[:, kk % 2, :],
                                 d["RSTDD"][kk % 2:kk % 2 + 1, :].partition_broadcast(128).squeeze(1))
                t_dma2[kk] = S.sec("sync", dd2)

            tok_mt = {}
            tok_cb = {}
            tok_ear = {}

            def e_mm1(h, c=c, b=b):
                fl = c * NH + h
                g4 = h // 4
                mt_bank = ps[f"p{h % 2}"]
                cb_bank = ps[f"p{2 + h % 2}"]
                def mm1(q, h=h, g4=g4, mt_bank=mt_bank, cb_bank=cb_bank, c=c, b=b, fl=fl):
                    S.wait(q, "tensor", t_ld[b])
                    S.wait(q, "tensor", t_mtd_done.get(fl - 2))
                    inst = None
                    for jt in range(2):
                        inst = q.matmul(out=mt_bank[:, jt * 256:(jt + 1) * 256],
                                        lhsT=XBCc[b][:, 4 + g4, jt * 128:(jt + 1) * 128],
                                        rhs=XBCc[b][:, 8 + h, :], start=True, stop=True)
                    tmt = S.done("tensor", inst)
                    S.wait(q, "tensor", t_cs_done.get(fl - 2))
                    q.matmul(out=cb_bank[:, 0:256], lhsT=sb23["IND"][:, h * 128:(h + 1) * 128],
                             rhs=sb23["cum_all"][:, c * 256:(c + 1) * 256], start=True, stop=True)
                    inst = q.matmul(out=cb_bank[:, 256:512], lhsT=sb23["IND"][:, h * 128:(h + 1) * 128],
                                    rhs=sb23["expcum_all"][:, c * 256:(c + 1) * 256], start=True, stop=True)
                    return tmt, S.done("tensor", inst)
                tok_mt[fl], tok_cb[fl] = S.sec("tensor", mm1)

            def e_vear(h, c=c):
                fl = c * NH + h
                sl = fl % 4
                cb_bank = ps[f"p{2 + h % 2}"]
                def vear(q, cb_bank=cb_bank, fl=fl, sl=sl):
                    S.wait(q, "vector", tok_cb[fl])
                    S.wait(q, "vector", t_e_prev.get(fl - 4))
                    q.tensor_tensor(out=EAR[:, sl, 0:256], in0=cb_bank[:, 0:256],
                                    in1=sb23["maskneg"][:, 0:256], op=ALU.add)
                    inst = q.tensor_tensor(out=EAR[:, sl, 256:512], in0=cb_bank[:, 0:256],
                                           in1=sb23["maskneg"][:, 256:512], op=ALU.add)
                    return S.done("vector", inst)
                tok_ear[fl] = S.sec("vector", vear)

            def e_sc(h, c=c):
                fl = c * NH + h
                sl = fl % 4
                t_e = S.sec("scalar", lambda q, h=h, fl=fl, sl=sl:
                            emit_scalar_merged(q, h, fl, sl, tok_ear[fl]))
                t_e_prev[fl] = t_e

            def e_vmid(h, c=c, b=b):
                fl = c * NH + h
                sl = fl % 4
                g4 = h // 4
                mt_bank = ps[f"p{h % 2}"]
                cb_bank = ps[f"p{2 + h % 2}"]
                def cidx(qi, jt, c=c, h=h):
                    return ((c * 2 + jt) * 3 + qi) * 16 + h
                def vmid(q, mt_bank=mt_bank, cb_bank=cb_bank,
                         h=h, g4=g4, b=b, fl=fl, sl=sl, cidx=cidx):
                    S.wait(q, "vector", t_e_prev[fl])
                    S.wait(q, "vector", tok_mt[fl])
                    S.wait(q, "vector", t_sc_prev.get(fl - 4))
                    inst = None
                    for jt in range(2):
                        inst = q.tensor_tensor(out=MTD[:, sl, jt * 256:(jt + 1) * 256],
                                               in0=mt_bank[:, jt * 256:(jt + 1) * 256],
                                               in1=ESB[:, sl, jt * 256:(jt + 1) * 256], op=ALU.mult)
                    t1 = S.done("vector", inst)
                    inst = q.tensor_tensor(out=CS[:, sl, :], in0=XBCc[b][:, 8 + h, :],
                                           in1=cb_bank[:, 256:512], op=ALU.mult)
                    t2 = S.done("vector", inst)
                    for jt in range(2):
                        inst = q.tensor_scalar(out=BW[:, sl, jt, :], in0=BT[:, g4 * 2 + jt, :],
                                               scalar1=sb23["colT"][:, cidx(1, jt):cidx(1, jt) + 1],
                                               scalar2=None, op0=ALU.mult)
                    return t1, t2, S.done("vector", inst)
                t_mtd, t_cs, t_bw = S.sec("vector", vmid)
                t_mtd_done[fl] = t_mtd
                t_cs_done[fl] = t_cs
                return t_bw

            def e_mm2(h, t_bw, c=c):
                fl = c * NH + h
                sl = fl % 4
                g4 = h // 4
                y_bank = ps[f"p{4 + h % 2}"]
                def mm2(q, h=h, g4=g4, y_bank=y_bank, t_bw=t_bw, c=c, fl=fl, sl=sl):
                    S.wait(q, "tensor", t_cs_done[fl])
                    S.wait(q, "tensor", t_yev.get(fl - 2))
                    S.wait(q, "tensor", t_sup.get(fl - NH) if c > 0 else C.t_szero)
                    q.matmul(out=y_bank[:, 0:256], lhsT=sb23["S_bf"][:, h * 128:(h + 1) * 128],
                             rhs=CS[:, sl, :], start=True, stop=False)
                    q.matmul(out=y_bank[:, 0:256], lhsT=XT[:, g4 * 2 + 0, :],
                             rhs=MTD[:, sl, 0:256], start=False, stop=False)
                    inst = q.matmul(out=y_bank[:, 0:256], lhsT=XT[:, g4 * 2 + 1, :],
                                    rhs=MTD[:, sl, 256:512], start=False, stop=True)
                    ty = S.done("tensor", inst)
                    S.wait(q, "tensor", t_bw)
                    q.matmul(out=y_bank[:, 256:384], lhsT=BW[:, sl, 0, :],
                             rhs=XT[:, g4 * 2 + 0, :], start=True, stop=False)
                    inst = q.matmul(out=y_bank[:, 256:384], lhsT=BW[:, sl, 1, :],
                                    rhs=XT[:, g4 * 2 + 1, :], start=False, stop=True)
                    return ty, S.done("tensor", inst)
                t_y, t_sc = S.sec("tensor", mm2)
                t_y_prev[fl] = t_y
                t_sc_prev[fl] = t_sc
                if h == NH - 1:
                    t_sc_last[c] = t_sc

            def e_vtail(h, c=c, b=b, Sold=Sold, Snew=Snew):
                fl = c * NH + h
                sl = fl % 4
                g4 = h // 4
                y_bank = ps[f"p{4 + h % 2}"]
                def cidx(qi, jt, c=c, h=h):
                    return ((c * 2 + jt) * 3 + qi) * 16 + h
                def vtail(q, h=h, g4=g4, y_bank=y_bank, b=b, fl=fl, sl=sl,
                          Sold=Sold, Snew=Snew, cidx=cidx):
                    S.wait(q, "vector", t_sc_prev[fl])
                    q.scalar_tensor_tensor(out=Y[:, sl, :], in0=XBCc[b][:, g4, :],
                                           scalar=sb["D_bc"][:, h:h + 1],
                                           in1=y_bank[:, 0:256], op0=ALU.mult, op1=ALU.add)
                    q.scalar_tensor_tensor(out=Snew[:, h * 128:(h + 1) * 128],
                                           in0=Sold[:, h * 128:(h + 1) * 128],
                                           scalar=sb23["colT"][:, cidx(2, 0):cidx(2, 0) + 1],
                                           in1=y_bank[:, 256:384], op0=ALU.mult, op1=ALU.add)
                    tsu = S.done("vector", None)
                    S.wait(q, "vector", t_ld_sg[b])
                    S.wait(q, "vector", t_gsq_tok.get(fl - NH))
                    inst = q.tensor_tensor(out=G16[:, h, :], in0=Y[:, sl, :],
                                           in1=SGc[b][:, h, :], op=ALU.mult)
                    tg = S.done("vector", inst)
                    kv = fl // 2 - 3
                    toks3 = emit_vgs(q, kv, b) if (h % 2 == 0 and h >= 6) else None
                    return tsu, tg, toks3
                t_suv, t_g, toks_vgs = S.sec("vector", vtail)
                t_su[fl] = t_suv
                t_yev[fl] = t_suv
                t_g_prev[fl] = t_g
                if toks_vgs is not None:
                    kv = fl // 2 - 3
                    t_gso[kv * 2], t_gso[kv * 2 + 1] = toks_vgs
                pend_sbf[0] = (fl, Snew)

            def e_stail(h, c=c):
                fl = c * NH + h
                sl = fl % 4
                k2 = fl // 2
                def stail(q, fl=fl, sl=sl, h=h, k2=k2):
                    S.wait(q, "scalar", t_g_prev[fl])
                    if pend_sbf[0] is not None:
                        pfl, Snew_ = pend_sbf[0]
                        ph = pfl % NH
                        q.copy(out=sb23["S_bf"][:, ph * 128:(ph + 1) * 128],
                               in_=Snew_[:, ph * 128:(ph + 1) * 128])
                        t_sup[pfl] = S.tok("scalar_c")
                        pend_sbf[0] = None
                    S.wait(q, "scalar", t_mm3.get(k2 - 2))
                    q.activation(out=GSQ[:, sl, :], in_=G16[:, h, :], func=AF.Square)
                    tq = S.tok("scalar_c")
                    te = ts = None
                    if h % 2 == 0 and h >= 4:
                        kk = k2 - 2
                        S.wait(q, "scalar", t_mm3[kk])
                        q.copy(out=RSUMS[0:1, kk % 2, :], in_=ps["p6"][0:1, 0:512])
                        te = S.tok("scalar_c")
                        S.wait(q, "scalar", t_dma1.get(kk - 2))
                        q.activation(out=RTMP[0:1, :], in_=RSUMS[0:1, kk % 2, :],
                                     func=AF.Ln, scale=1.0 / 128.0, bias=sb["cbias"][0:1, 0:1])
                        q.activation(out=RSTDS[0:1, kk % 2, :], in_=RTMP[0:1, :],
                                     func=AF.Exp, scale=-0.5)
                        ts = S.tok("scalar_c")
                    return tq, te, ts
                t_gsq, te_, ts_ = S.sec("scalar", stail)
                t_gsq_tok[fl] = t_gsq
                t_gsq_prev[fl] = t_gsq
                if te_ is not None:
                    t_evac[k2 - 2] = te_
                    t_srs[k2 - 2] = ts_
                    emit_rb_dma(k2 - 2)

            def e_mm3(j, c=c):
                def mm3(q, j=j):
                    S.wait(q, "tensor", t_gsq_tok[2 * j + 1])
                    S.wait(q, "tensor", t_evac.get(j - 1))
                    sg = (2 * j) % 4
                    inst = q.matmul(out=ps["p6"][0:1, 0:512], lhsT=sb["onescol"][:, :],
                                    rhs=GSQ[:, sg:sg + 2, :].rearrange("p a b -> p (a b)"),
                                    start=True, stop=True)
                    return S.done("tensor", inst)
                t_mm3[j] = S.sec("tensor", mm3)

            def e_evac_srs(kk):
                def sc_t(q, kk=kk):
                    S.wait(q, "scalar", t_mm3[kk])
                    q.copy(out=RSUMS[0:1, kk % 2, :], in_=ps["p6"][0:1, 0:512])
                    te = S.tok("scalar_c")
                    S.wait(q, "scalar", t_dma1.get(kk - 2))
                    q.activation(out=RTMP[0:1, :], in_=RSUMS[0:1, kk % 2, :],
                                 func=AF.Ln, scale=1.0 / 128.0, bias=sb["cbias"][0:1, 0:1])
                    q.activation(out=RSTDS[0:1, kk % 2, :], in_=RTMP[0:1, :],
                                 func=AF.Exp, scale=-0.5)
                    return te, S.tok("scalar_c")
                t_evac[kk], t_srs[kk] = S.sec("scalar", sc_t)

            e_mm1(0)
            e_vear(0)
            e_sc(0)
            e_mm1(1)
            e_vear(1)
            e_sc(1)
            vmid_bw = {0: e_vmid(0)}
            for h in range(NH):
                fl = c * NH + h
                k2 = fl // 2
                if h + 2 < NH:
                    e_mm1(h + 2)
                if h % 2 == 1 and h >= 3:
                    e_mm3(k2 - 1)
                e_mm2(h, vmid_bw.pop(h))
                if h + 2 < NH:
                    e_vear(h + 2)
                if h + 1 < NH:
                    vmid_bw[h + 1] = e_vmid(h + 1)
                e_vtail(h)
                if h + 2 < NH:
                    e_sc(h + 2)
                e_stail(h)

            def epilogue(c=c, b=b, emit_vgs=emit_vgs, emit_rb_dma=emit_rb_dma,
                         e_evac_srs=e_evac_srs, e_mm3=e_mm3):
                k5, k6, k7 = c * 8 + 5, c * 8 + 6, c * 8 + 7
                e_evac_srs(k6)
                emit_rb_dma(k6)
                e_mm3(k7)
                e_evac_srs(k7)
                def v_t1(q, b=b, k5=k5):
                    t_gso[k5 * 2], t_gso[k5 * 2 + 1] = emit_vgs(q, k5, b)
                S.sec("vector", v_t1)
                emit_rb_dma(k7)
                def v_t2(q, b=b, k6=k6, k7=k7):
                    for kv in (k6, k7):
                        t_gso[kv * 2], t_gso[kv * 2 + 1] = emit_vgs(q, kv, b)
                S.sec("vector", v_t2)

                t_xbc_use[b] = t_yev[c * NH + NH - 1]
                t_sg_use[b] = t_g_prev[c * NH + NH - 1]
                def sto(q, c=c, b=b):
                    for kv in range(c * 8, c * 8 + 8):
                        S.wait(q, "sync", t_gso[kv * 2 + 1])
                    return S.dma(q, "sync",
                                 d["GS"][:, :, c * QCH:(c + 1) * QCH].rearrange("k f t -> f k t"),
                                 GSO[b][:, :, :])
                t_store = S.sec("sync", sto)
                t_gso_use[b] = t_store
            pend_epi[0] = epilogue
        pend_epi[0]()
        pend_epi[0] = None
        pend_sbf[0] = None
        S.flush()


def stage4(C):
    nc, S, sb, ps, d = C.nc, C.S, C.sb, C.ps, C.d
    with contextlib.ExitStack() as es4:
        WO = es4.enter_context(nc.sbuf_tensor("s4_wo", [128, 16, 2048], BF16))
        GSG = [es4.enter_context(nc.sbuf_tensor(f"s4_gs{i}", [128, 16, TGRP], BF16)) for i in range(2)]
        ATT = es4.enter_context(nc.sbuf_tensor("s4_att", [128, 2, 4, 16, 128], BF16))
        XTI = [es4.enter_context(nc.sbuf_tensor(f"s4_x{i}", [128, 2048], F32)) for i in range(2)]
        ATTT = [es4.enter_context(nc.sbuf_tensor(f"s4_at{i}", [128, 16, 128], BF16)) for i in range(2)]
        X1O = [es4.enter_context(nc.sbuf_tensor(f"s4_o{i}", [128, 2048], F32)) for i in range(2)]
        def ldw(q):
            return [S.dma(q, "gpsimd", WO[:, kc, :], d["w_out"][kc * 128:(kc + 1) * 128, :])
                    for kc in range(16)]
        t_w = S.sec("gpsimd", ldw)
        t_gs_use = [None, None]
        t_ev = {}
        t_tr_last_p = [None, None]
        t_xti_use = [None, None]
        t_attt_use = [None, None]
        t_x1o_use = [None, None]
        for g in range(NGRP):
            b = g % 2
            def ldg(q, g=g, b=b):
                S.wait(q, "gpsimd", t_gs_use[b])
                return S.dma(q, "gpsimd", GSG[b][:, :, :],
                             d["GS"][:, :, g * TGRP:(g + 1) * TGRP].rearrange("k f t -> f k t"))
            t_gs = S.sec("gpsimd", ldg)
            for oc in range(16):
                goc = g * 16 + oc
                bank = ps[f"p{oc % 4}"]
                def mm(q, oc=oc, goc=goc, bank=bank, t_gs=t_gs, b=b):
                    S.wait(q, "tensor", t_w)
                    S.wait(q, "tensor", t_gs)
                    S.wait(q, "tensor", t_ev.get(goc - 4))
                    inst = None
                    for kc in range(16):
                        inst = q.matmul(out=bank[:, :], lhsT=WO[:, kc, oc * 128:(oc + 1) * 128],
                                        rhs=GSG[b][:, kc, :], start=(kc == 0), stop=(kc == 15))
                    return S.done("tensor", inst)
                t_mm = S.sec("tensor", mm)
                def ev(q, oc=oc, bank=bank, t_mm=t_mm, b=b):
                    S.wait(q, "scalar", t_mm)
                    S.wait(q, "scalar", t_tr_last_p[b])
                    inst = q.copy(out=ATT[:, b, :, oc, :], in_=bank[:, :].rearrange("p (a b) -> p a b", a=4))
                    return S.done("scalar", inst)
                t_ev[goc] = S.sec("scalar", ev)
            t_gs_use[b] = t_ev[g * 16 + 15]
            for tt in range(4):
                gtt = g * 4 + tt
                b2 = tt % 2
                def ldx(q, gtt=gtt, b2=b2):
                    S.wait(q, "gpsimd", t_xti_use[b2])
                    return S.dma(q, "gpsimd", XTI[b2][:, :], d["x"][gtt * 128:(gtt + 1) * 128, :])
                t_x = S.sec("gpsimd", ldx)
                def trp(q, tt=tt, b2=b2, b=b, g=g):
                    for oc in range(16):
                        S.wait(q, "scalar", t_ev[g * 16 + oc])
                    S.wait(q, "scalar", t_attt_use[b2])
                    return S.dma(q, "scalar", ATTT[b2][:, :, :], ATT[:, b, tt, :, :],
                                 transpose=True)
                t_tr = S.sec("scalar", trp)
                if tt == 3:
                    t_tr_last_p[b] = t_tr
                def vadd(q, b2=b2, t_x=t_x, t_tr=t_tr):
                    S.wait(q, "vector", t_x)
                    S.wait(q, "vector", t_tr)
                    S.wait(q, "vector", t_x1o_use[b2])
                    inst = q.tensor_tensor(out=X1O[b2][:, :], in0=XTI[b2][:, :],
                                           in1=ATTT[b2].ap().rearrange("p a b -> p (a b)"),
                                           op=ALU.add)
                    return S.done("vector", inst)
                t_add = S.sec("vector", vadd)
                t_xti_use[b2] = t_add
                t_attt_use[b2] = t_add
                def sto(q, gtt=gtt, b2=b2, t_add=t_add):
                    S.wait(q, "sync", t_add)
                    return S.dma(q, "sync", d["X1"][gtt * 128:(gtt + 1) * 128, :], X1O[b2][:, :])
                t_x1o_use[b2] = S.sec("sync", sto)
        S.flush() if (g == NGRP - 1 and tt == 3) else None


def stage56(C):
    nc, S, sb, ps, d = C.nc, C.S, C.sb, C.ps, C.d
    LT = [64.0, 32.0, 16.0, 8.0, 4.0, 2.0, 1.0]
    with contextlib.ExitStack() as es5:
        WQ = es5.enter_context(nc.sbuf_tensor("s5_wq", [128, 16, 2048], BF16))
        XTI = [es5.enter_context(nc.sbuf_tensor(f"s5_x{i}", [128, 2048], F32)) for i in range(2)]
        XTI2 = es5.enter_context(nc.sbuf_tensor("s5_x2", [128, 2048], F32))
        UU = es5.enter_context(nc.sbuf_tensor("s5_uu", [128, 2048], F32))
        UA = es5.enter_context(nc.sbuf_tensor("s5_ua", [128, 2048], F32))
        UB = es5.enter_context(nc.sbuf_tensor("s5_ub", [128, 2048], F32))
        MB = es5.enter_context(nc.sbuf_tensor("s5_mb", [128, 2048], F32))
        SGN = es5.enter_context(nc.sbuf_tensor("s5_sgn", [128, 2048], BF16))
        XQT = es5.enter_context(nc.sbuf_tensor("s5_xqt", [128, 2048], BF16))
        XQF = es5.enter_context(nc.sbuf_tensor("s5_xqf", [128, 2, 4, 16, 128], BF16))
        FF = es5.enter_context(nc.sbuf_tensor("s5_ff", [128, 4, 16, 128], BF16))
        FFT = [es5.enter_context(nc.sbuf_tensor(f"s5_ft{i}", [128, 16, 128], BF16)) for i in range(4)]
        OUTB = es5.enter_context(nc.sbuf_tensor("s5_ob", [128, 2048], F32))
        SCC = es5.enter_context(nc.sbuf_tensor("s5_scc", [128, 16], F32))
        SM = es5.enter_context(nc.sbuf_tensor("s5_sm", [128, 28], F32))
        BNS = es5.enter_context(nc.sbuf_tensor("s5_bns", [128, 4, 6], F32))

        t_wq = S.sec("scalar", lambda q: S.dma(q, "scalar", WQ[:, :, :],
                     d["WQT"].rearrange("ic i oc o -> i ic (oc o)")))

        t_x_use = [None, None]
        t_ffev = {}
        t_lad = None
        t_lad2 = None
        t_qtr = None
        t_xqf_use = [None, None]
        t_ff_use = None
        t_ftr_use = [None, None, None, None]
        t_out_use = None
        t_sgn_use = None
        t_ft_last = None
        t_lad_v = None
        def emit_out_block(gp, ffev_snap):
            def ftr(q, gp=gp, ffev_snap=ffev_snap):
                for oc in range(16):
                    S.wait(q, "scalar", ffev_snap[oc])
                toks = []
                for tt in range(4):
                    S.wait(q, "scalar", t_ftr_use[tt])
                    toks.append(S.dma(q, "scalar", FFT[tt][:, :, :], FF[:, tt, :, :],
                                      transpose=True))
                return toks
            t_fts = S.sec("scalar", ftr)
            st["t_ft_last"] = list(t_fts)
            for tt in range(4):
                gtt = gp * 4 + tt
                b2 = tt % 2
                def ldx2(q, gtt=gtt):
                    S.wait(q, "sync", st["t_x2_use"])
                    return S.dma(q, "sync", XTI2[:, :], d["X1"][gtt * 128:(gtt + 1) * 128, :])
                t_x2 = S.sec("sync", ldx2)
                def vout(q, gtt=gtt, tt=tt, t_x2=t_x2, t_ft=t_fts[tt]):
                    S.wait(q, "vector", t_x2)
                    S.wait(q, "vector", t_ft)
                    S.wait(q, "vector", st["t_out_use"])
                    inst = q.scalar_tensor_tensor(out=OUTB[:, :],
                                                  in0=FFT[tt].ap().rearrange("p a b -> p (a b)"),
                                                  scalar=SCC[:, gtt % 16:gtt % 16 + 1],
                                                  in1=XTI2[:, :], op0=ALU.mult, op1=ALU.add)
                    return S.done("vector", inst)
                t_o = S.sec("vector", vout)
                st["t_x2_use"] = t_o
                t_ftr_use[tt] = t_o
                def sto(q, gtt=gtt, t_o=t_o):
                    S.wait(q, "sync", t_o)
                    return S.dma(q, "sync", d["out"][gtt * 128:(gtt + 1) * 128, :], OUTB[:, :])
                st["t_out_use"] = S.sec("sync", sto)

        st = {"t_ft_last": None, "t_out_use": None, "t_x2_use": None}
        ffev_snaps = {}
        qtr_snaps = {}
        for g in range(NGRP):
            t_qtr_tiles = []
            for tt in range(4):
                gtt = g * 4 + tt
                b = tt % 2
                c0 = 12 * b
                def ldx(q, gtt=gtt, b=b):
                    S.wait(q, "gpsimd", t_x_use[b])
                    return S.dma(q, "gpsimd", XTI[b][:, :], d["X1"][gtt * 128:(gtt + 1) * 128, :])
                t_x = S.sec("gpsimd", ldx)
                def vstats(q, b=b, c0=c0, t_x=t_x):
                    S.wait(q, "vector", t_x)
                    for j in range(4):
                        q.bn_stats(out=BNS[:, j, :], in_=XTI[b][:, j * 512:(j + 1) * 512])
                    q.bn_aggr(out=SM[:, c0 + 2:c0 + 4],
                              in_=BNS.ap().rearrange("p a b -> p (a b)"))
                    q.tensor_reduce(out=SM[:, c0 + 7:c0 + 8], in_=XTI[b][:, :], axis=AX.X, op=ALU.max)
                    q.tensor_reduce(out=SM[:, c0 + 8:c0 + 9], in_=XTI[b][:, :], axis=AX.X, op=ALU.min)
                    inst = q.tensor_scalar(out=SM[:, c0 + 12:c0 + 13], in0=SM[:, c0 + 2:c0 + 3],
                                           scalar1=-1.0, scalar2=None, op0=ALU.mult)
                    return S.done("vector", inst)
                t_var = S.sec("vector", vstats)
                def srstd(q, c0=c0, b=b, t_var=t_var):
                    S.wait(q, "scalar", t_var)
                    q.activation(out=SM[:, c0 + 5:c0 + 6], in_=SM[:, c0 + 3:c0 + 4], func=AF.Ln,
                                 bias=sb["cbias"][:, 0:1])
                    q.activation(out=SM[:, c0 + 6:c0 + 7], in_=SM[:, c0 + 5:c0 + 6],
                                 func=AF.Exp, scale=-0.5)
                    t_re = S.tok("scalar_c")
                    S.wait(q, "scalar", t_sgn_use)
                    q.activation(out=SGN[:, :], in_=XTI[b][:, :], func=AF.Sign,
                                 bias=SM[:, c0 + 12:c0 + 13])
                    q.activation(out=UA[:, :], in_=XTI[b][:, :], func=AF.Abs,
                                 bias=SM[:, c0 + 12:c0 + 13])
                    return t_re, S.tok("scalar_c")
                t_re, t_rstd = S.sec("scalar", srstd)
                t_x_use[b] = t_rstd
                def vsc(q, c0=c0, gtt=gtt, t_re=t_re):
                    S.wait(q, "vector", t_re)
                    q.tensor_tensor(out=SM[:, c0 + 9:c0 + 10], in0=SM[:, c0 + 7:c0 + 8],
                                    in1=SM[:, c0 + 2:c0 + 3], op=ALU.subtract)
                    q.tensor_tensor(out=SM[:, c0 + 10:c0 + 11], in0=SM[:, c0 + 2:c0 + 3],
                                    in1=SM[:, c0 + 8:c0 + 9], op=ALU.subtract)
                    q.tensor_tensor(out=SM[:, c0 + 9:c0 + 10], in0=SM[:, c0 + 9:c0 + 10],
                                    in1=SM[:, c0 + 10:c0 + 11], op=ALU.max)
                    q.tensor_tensor(out=SM[:, c0 + 9:c0 + 10], in0=SM[:, c0 + 9:c0 + 10],
                                    in1=SM[:, c0 + 6:c0 + 7], op=ALU.mult)
                    q.tensor_single_scalar(out=SM[:, c0 + 9:c0 + 10], in_=SM[:, c0 + 9:c0 + 10],
                                           scalar=1e-8, op=ALU.max)
                    q.tensor_scalar(out=SCC[:, gtt % 16:gtt % 16 + 1], in0=SM[:, c0 + 9:c0 + 10],
                                    scalar1=sb["smalls"][:, 35:36], scalar2=1.0 / 127.0,
                                    op0=ALU.mult, op1=ALU.mult)
                    q.reciprocal(out=SM[:, c0 + 10:c0 + 11], in_=SM[:, c0 + 9:c0 + 10])
                    q.tensor_scalar(out=SM[:, c0 + 10:c0 + 11], in0=SM[:, c0 + 10:c0 + 11],
                                    scalar1=127.0, scalar2=None, op0=ALU.mult)
                    inst = q.tensor_tensor(out=SM[:, c0 + 11:c0 + 12], in0=SM[:, c0 + 6:c0 + 7],
                                           in1=SM[:, c0 + 10:c0 + 11], op=ALU.mult)
                    return S.done("vector", inst)
                t_s3 = S.sec("vector", vsc)
                def vladprep(q, b=b, c0=c0, t_s3=t_s3, t_rstd=t_rstd):
                    S.wait(q, "vector", t_s3)
                    S.wait(q, "vector", t_rstd)
                    S.wait(q, "vector", t_lad)
                    S.wait(q, "vector", t_lad2)
                    S.wait(q, "vector", t_qtr)
                    inst = q.tensor_scalar(out=UU[:, :], in0=UA[:, :],
                                           scalar1=SM[:, c0 + 11:c0 + 12], scalar2=None,
                                           op0=ALU.mult)
                    return S.done("vector", inst)
                t_prep = S.sec("vector", vladprep)
                SPL = 2048
                def ladv(q, t_prep=t_prep):
                    lo, hi, w = 0, SPL, SPL
                    S.wait(q, "vector", t_prep)
                    cur = UU
                    bufs = [UA, UB]
                    for li, T in enumerate(LT):
                        q.tensor_scalar(out=MB[:, lo:hi], in0=cur[:, lo:hi],
                                        scalar1=T - 0.5, scalar2=-T, op0=ALU.is_ge, op1=ALU.mult)
                        nxt = bufs[li % 2]
                        q.tensor_tensor(out=nxt[:, lo:hi], in0=cur[:, lo:hi],
                                        in1=MB[:, lo:hi], op=ALU.add)
                        cur = nxt
                    q.tensor_tensor(out=MB[:, lo:hi], in0=UU[:, lo:hi], in1=cur[:, lo:hi],
                                    op=ALU.subtract)
                    inst = q.tensor_tensor(out=XQT[:, lo:hi], in0=MB[:, lo:hi],
                                           in1=SGN[:, lo:hi], op=ALU.mult)
                    return S.done("vector", inst)
                t_lad = S.sec("vector", ladv)
                t_lad2 = t_lad
                t_lad_v = t_lad
                t_sgn_use = t_lad
                def qtr(q, tt=tt, g=g, t_lad=t_lad, t_lad2=t_lad2):
                    S.wait(q, "scalar", t_lad)
                    S.wait(q, "scalar", t_lad2)
                    S.wait(q, "scalar", t_xqf_use[g % 2])
                    return S.dma(q, "scalar", XQF[:, g % 2, tt, :, :], XQT[:, :],
                                 transpose=True)
                t_qtr = S.sec("scalar", qtr)
                t_qtr_tiles.append(t_qtr)
            qtr_snaps[g] = tuple(t_qtr_tiles)
            if g >= 2:
                emit_out_block(g - 2, ffev_snaps[g - 2])
            if g < 1:
                continue
            gm = g - 1
            for oc in range(16):
                goc = gm * 16 + oc
                bank = ps[f"p{oc % 2}"]
                def mm(q, oc=oc, goc=goc, bank=bank, g=gm, toks=qtr_snaps[gm]):
                    S.wait(q, "tensor", t_wq)
                    for tk in toks:
                        S.wait(q, "tensor", tk)
                    S.wait(q, "tensor", t_ffev.get(goc - 2))
                    inst = None
                    for ic in range(16):
                        inst = q.matmul(out=bank[:, :], lhsT=WQ[:, ic, oc * 128:(oc + 1) * 128],
                                        rhs=XQF[:, g % 2, :, ic, :], start=(ic == 0), stop=(ic == 15))
                    return S.done("tensor", inst)
                t_mm = S.sec("tensor", mm)
                def ev(q, oc=oc, bank=bank, t_mm=t_mm):
                    S.wait(q, "scalar", t_mm)
                    S.wait(q, "scalar", st["t_ft_last"])
                    inst = q.copy(out=FF[:, :, oc, :], in_=bank[:, :].rearrange("p (a b) -> p a b", a=4))
                    return S.done("scalar", inst)
                t_ffev[goc] = S.sec("scalar", ev)
            t_xqf_use[gm % 2] = t_ffev[goc]
            ffev_snaps[gm] = {oc: t_ffev[gm * 16 + oc] for oc in range(16)}
        emit_out_block(NGRP - 2, ffev_snaps[NGRP - 2])
        gm = NGRP - 1
        for oc in range(16):
            goc = gm * 16 + oc
            bank = ps[f"p{oc % 2}"]
            def mm(q, oc=oc, goc=goc, bank=bank, g=gm, toks=qtr_snaps[gm]):
                S.wait(q, "tensor", t_wq)
                for tk in toks:
                    S.wait(q, "tensor", tk)
                S.wait(q, "tensor", t_ffev.get(goc - 2))
                inst = None
                for ic in range(16):
                    inst = q.matmul(out=bank[:, :], lhsT=WQ[:, ic, oc * 128:(oc + 1) * 128],
                                    rhs=XQF[:, g % 2, :, ic, :], start=(ic == 0), stop=(ic == 15))
                return S.done("tensor", inst)
            t_mm = S.sec("tensor", mm)
            def ev(q, oc=oc, bank=bank, t_mm=t_mm):
                S.wait(q, "scalar", t_mm)
                S.wait(q, "scalar", st["t_ft_last"])
                inst = q.copy(out=FF[:, :, oc, :], in_=bank[:, :].rearrange("p (a b) -> p a b", a=4))
                return S.done("scalar", inst)
            t_ffev[goc] = S.sec("scalar", ev)
        ffev_snaps[gm] = {oc: t_ffev[gm * 16 + oc] for oc in range(16)}
        emit_out_block(NGRP - 1, ffev_snaps[NGRP - 1])
        S.flush()


_CACHE = {}

def kernel(L0_attn_hidden_states, L0_attn_W_in, L0_attn_conv_weight,
           L0_attn_conv_bias, L0_attn_A_log, L0_attn_D, L0_attn_dt_bias,
           L0_attn_norm_weight, L0_attn_W_out, L0_ffn_weight):
    x = np.ascontiguousarray(L0_attn_hidden_states, dtype=np.float32)
    key = (tuple(DEBUG_OUTS), tuple(BUILD_STAGES))
    nc = _CACHE.get(key)
    if nc is None:
        nc = build_program(tuple(DEBUG_OUTS), tuple(BUILD_STAGES))
        _CACHE[key] = nc
    base = {
        "w_in": np.ascontiguousarray(L0_attn_W_in, dtype=np.float32),
        "conv_w": np.ascontiguousarray(L0_attn_conv_weight, dtype=np.float32),
        "conv_b": np.ascontiguousarray(L0_attn_conv_bias, dtype=np.float32),
        "a_log": np.ascontiguousarray(L0_attn_A_log, dtype=np.float32),
        "d_vec": np.ascontiguousarray(L0_attn_D, dtype=np.float32),
        "dt_bias": np.ascontiguousarray(L0_attn_dt_bias, dtype=np.float32),
        "normw": np.ascontiguousarray(L0_attn_norm_weight, dtype=np.float32),
        "w_out": np.ascontiguousarray(L0_attn_W_out, dtype=np.float32),
        "w_ffn": np.ascontiguousarray(L0_ffn_weight, dtype=np.float32),
    }
    in_maps = [dict(base, x=np.ascontiguousarray(x[i])) for i in range(8)]
    res = run_bass_kernel_spmd(nc, in_maps, core_ids=list(range(8)))
    _CACHE["last_results"] = res
    return np.stack([np.asarray(r["out"]) for r in res.results], axis=0)
